# version 28
# baseline (speedup 1.0000x reference)
import math
import numpy as np
import ml_dtypes
from contextlib import ExitStack
import concourse.bass as bass
import concourse.mybir as mybir
import bass_rust
from concourse.bass_utils import run_bass_kernel_spmd

F32 = mybir.dt.float32
BF16 = mybir.dt.bfloat16
F8 = mybir.dt.float8e4
U8 = mybir.dt.uint8
AX = mybir.AxisListType
ALU = mybir.AluOpType
ACT = mybir.ActivationFunctionType

FIXC = 12288.0
SUBN2 = float(2.0 ** -12)
SCALE_QK = 0.08838834764831845


class Eng:
    def __init__(self, eng, sem, hwm, cnt, step=1, selfw=False):
        self.e, self.sem, self.step, self.hwm, self.cnt = eng, sem, step, hwm, cnt
        self.selfw = selfw

    def wait(self, *tokens):
        for tok in tokens:
            if tok is None:
                continue
            sem, val = tok
            key = id(sem)
            if self.hwm.get(key, 0) >= val:
                continue
            self.e.wait_ge(sem, val)
            self.hwm[key] = val

    def do(self, method, *args, deps=(), **kw):
        if self.selfw and self.cnt[0] > 0:
            self.wait((self.sem, self.cnt[0]))
        self.wait(*deps)
        instr = getattr(self.e, method)(*args, **kw)
        self.cnt[0] += self.step
        instr.then_inc(self.sem, self.step)
        return (self.sem, self.cnt[0])


def _bucket_f32(d, num_buckets=32, max_distance=128):
    d = np.asarray(d, dtype=np.int64)
    me = num_buckets // 2
    nf = np.maximum(d, 1).astype(np.float32)
    large = (np.log(nf / np.float32(me)) / np.float32(math.log(max_distance / me))
             * np.float32(num_buckets - me)).astype(np.int32) + me
    large = np.minimum(large, num_buckets - 1)
    return np.where(d < me, d.astype(np.int32), large)


def build_nc(S, H, NH, I):
    NB = 32
    RT = S // 128
    OT = RT // 2
    HT = H // 128
    IT = I // 128
    RO = S // 2
    BL = S + 128
    HOG = min(512, H)
    NG = H // HOG
    CHK = min(256, I)
    NCH = (I + CHK - 1) // CHK
    NIC = I // 512
    assert I % 512 == 0 and H % 128 == 0 and S % 512 == 0 and I % CHK == 0

    nc = bass.Bass("TRN2", target_bir_lowering=False)

    x_d = nc.dram_tensor("xb", [S, H], F32, kind="ExternalInput")
    qkv_d = nc.dram_tensor("qkv_w", [H, 3 * H], F32, kind="ExternalInput")
    dense_d = nc.dram_tensor("dense_w", [H, H], F32, kind="ExternalInput")
    rel_d = nc.dram_tensor("rel_table", [NB, NH], F32, kind="ExternalInput")
    ISL = I // 8
    WDS = H // 8
    wg_d = nc.dram_tensor("wg_s", [ISL, H], F32, kind="ExternalInput")
    wu_d = nc.dram_tensor("wu_s", [ISL, H], F32, kind="ExternalInput")
    wd_d = nc.dram_tensor("wd_s", [WDS, I], F32, kind="ExternalInput")
    CSZ_G = HT * 128 * ISL
    CSZ_D = (I // 128) * 128 * WDS
    CSZ = 2 * CSZ_G + CSZ_D
    cb_hbm = nc.dram_tensor("cb_scr", [CSZ], BF16, kind="Internal")
    G_hbm = nc.dram_tensor("G_scr", [8 * CSZ], BF16, kind="Internal")
    cosP_d = nc.dram_tensor("cosP", [64, S], BF16, kind="ExternalInput")
    sinP_d = nc.dram_tensor("sinP", [64, S], BF16, kind="ExternalInput")
    oht1_d = nc.dram_tensor("oht1", [NB, BL], F32, kind="ExternalInput")
    oht2_d = nc.dram_tensor("oht2", [NB, BL], F32, kind="ExternalInput")
    id_d = nc.dram_tensor("identb", [128, 128], BF16, kind="ExternalInput")
    cm_d = nc.dram_tensor("cmask", [128, 128], F32, kind="ExternalInput")
    fm_d = nc.dram_tensor("fmask", [128, 128], F32, kind="ExternalInput")
    out_d = nc.dram_tensor("out", [RO, H], F32, kind="ExternalOutput")
    F1_hbm = nc.dram_tensor("F1_scr", [NH, BL], F32, kind="Internal")
    F2_hbm = nc.dram_tensor("F2_scr", [NH, BL], F32, kind="Internal")
    qxT_hbm = nc.dram_tensor("qxT_scr", [HT * 128, RO], BF16, kind="Internal")
    ctxT_hbm = nc.dram_tensor("ctxT_scr", [NH * 128, RO], BF16, kind="Internal")
    attn_hbm = nc.dram_tensor("attn_scr", [RO, H], F32, kind="Internal")
    qkvb_hbm = nc.dram_tensor("qkvb_scr", [H, 3 * H], BF16, kind="Internal")
    dwb_hbm = nc.dram_tensor("dwb_scr", [H, H], BF16, kind="Internal")
    t_hbm = nc.dram_tensor("t_scr", [RO, I], F32, kind="Internal")
    qt2_hbm = nc.dram_tensor("qt2_scr", [OT * 128, I], BF16, kind="Internal")

    es = ExitStack()
    SB = lambda n, shp, dt=F32: es.enter_context(nc.sbuf_tensor(n, shp, dt))

    sems, cnts = {}, {}
    for nm in ("qc", "qx0", "qx1", "qw", "qctx", "qqxl", "qtld", "qt2l",
               "qw10", "qw11", "qw12", "qw13", "qwd", "qas0", "qas1", "gf", "gt2",
               "gqxt", "gct", "gat", "gt", "go", "gb0", "gb1", "gw", "v",
               "s", "t"):
        sems[nm] = es.enter_context(nc.semaphore(nm + "_sem"))
        cnts[nm] = [0]
    hwm_by_eng = {}

    def mk(engine_obj, eng_key, sem_key, step=1):
        h = hwm_by_eng.setdefault(eng_key, {})
        return Eng(engine_obj, sems[sem_key], h, cnts[sem_key], step)

    block = es.enter_context(nc.Block())
    T = {}

    cur = {}

    SLOT_SEMS = {"qc", "qx0", "qx1", "qw", "qw10", "qw11", "qw12", "qw13",
                 "qwd", "qas0", "qas1", "qt2l"}

    def sec(engname, sem=None):
        dec = getattr(block, engname)
        key = {"sync": ("q", None, 16), "gpsimd": ("g", None, 16),
               "vector": ("v", "v", 1), "scalar": ("s", "s", 1),
               "tensor": ("t", "t", 1)}[engname]
        semname = sem if sem is not None else key[1]
        assert semname is not None, f"DMA section needs explicit sem: {engname}"
        sw = semname in SLOT_SEMS

        def run(emitfn):
            @dec
            def _(engobj):
                h = hwm_by_eng.setdefault(key[0], {})
                emitfn(Eng(engobj, sems[semname], h, cnts[semname], key[2], sw))
        return run

    def barrier():
        snap = [(sems[nm], cnts[nm][0]) for nm in sems if cnts[nm][0] > 0]
        for engname in ("sync", "gpsimd", "vector", "scalar", "tensor"):
            def bwait(E, snap=snap):
                E.wait(*snap)
            sec(engname, "qc" if engname in ("sync", "gpsimd") else None)(bwait)

    cosP = SB("cosP_s", [64, S], BF16)
    sinP = SB("sinP_s", [64, S], BF16)
    identb = SB("identb_s", [128, 128], BF16)
    cmask = SB("cmask_s", [128, 128])
    fmask = SB("fmask_s", [128, 128])
    eps_t = SB("eps_t", [128, 1])
    relsb = SB("relsb", [NB, NH])
    qs_rmax = [SB(f"qs_rmax{a}", [128, 1]) for a in range(2)]
    qs_rinv = [SB(f"qs_rinv{a}", [128, 1]) for a in range(2)]
    qs_shalf = [SB(f"qs_shalf{a}", [128, 1]) for a in range(2)]
    qs_sdq = [SB(f"qs_sdq{a}", [128, 1]) for a in range(2)]
    qset = [0]

    def q_scales(tag, src_ap, deps_src):
        a = qset[0] = 1 - qset[0]
        P = src_ap.shape[0]
        out = {}

        def v1(E):
            rm = E.do("reduce_max", qs_rmax[a][0:P, :], src_ap, axis=AX.X,
                      apply_absolute_value=True,
                      deps=list(deps_src) + [T.get(f"q_rm_war{a}")])
            out["rm"] = rm
            out["ri"] = E.do("reciprocal", qs_rinv[a][0:P, :],
                             qs_rmax[a][0:P, :],
                             deps=[rm, T.get(f"q_ri_war{a}")])
        sec("vector")(v1)

        def s1(E):
            out["sh"] = E.do("activation", qs_shalf[a][0:P, :],
                             qs_rinv[a][0:P, :], ACT.Copy, scale=224.0,
                             deps=[out["ri"], T.get(f"q_sh_war{a}")])
            out["sq"] = E.do("activation", qs_sdq[a][0:P, :],
                             qs_rmax[a][0:P, :], ACT.Copy, scale=1.0 / 224.0,
                             deps=[out["rm"], T.get(f"q_sq_war{a}")])
        sec("scalar")(s1)
        T[f"q_rm_war{a}"] = out["sq"]
        T[f"q_ri_war{a}"] = out["sh"]
        return a, out["sh"], out["sq"]

    def q_apply(tmp, src_ap, dst_ap, sset, dst_war=None):
        a, sh, sq = sset
        b = tmp["set"]
        W = src_ap.shape[-1]
        P = src_ap.shape[0]
        out = {}

        def s1(E):
            w = E.do("activation", tmp["wt"][0:P, 0:W], src_ap, ACT.Copy,
                     scale=qs_shalf[a][0:P, 0:1], deps=[sh, T.get(f"q_w_war{b}")])
            out["c8"] = E.do("copy", tmp["c8t"][0:P, 0:W], tmp["wt"][0:P, 0:W],
                             deps=[w, T.get(f"q_c8_war{b}")])
            out["w"] = w
        sec("scalar")(s1)

        def v2(E):
            fa = E.do("tensor_scalar", tmp["fixa"][0:P, 0:W], tmp["wt"][0:P, 0:W],
                      FIXC, None, op0=ALU.add,
                      deps=[out["w"], T.get(f"q_fa_war{b}")])
            fq = E.do("tensor_scalar", tmp["fixq"][0:P, 0:W],
                      tmp["fixa"][0:P, 0:W],
                      FIXC, qs_sdq[a][0:P, 0:1], op0=ALU.subtract, op1=ALU.mult,
                      deps=[fa, sq, T.get(f"q_fq_war{b}")])
            w2 = E.do("tensor_tensor", tmp["w2t"][0:P, 0:W], tmp["wt"][0:P, 0:W],
                      tmp["wt"][0:P, 0:W], op=ALU.mult,
                      deps=[T.get(f"q_w2_war{b}")])
            mq = E.do("tensor_scalar", tmp["maskt"][0:P, 0:W],
                      tmp["w2t"][0:P, 0:W],
                      SUBN2, None, op0=ALU.is_lt, deps=[w2, T.get(f"q_mq_war{b}")])
            out["qop"] = E.do("tensor_scalar", dst_ap, tmp["c8t"][0:P, 0:W],
                              qs_sdq[a][0:P, 0:1], None, op0=ALU.mult,
                              deps=[out["c8"], sq, dst_war])
            done = E.do("copy_predicated", dst_ap, tmp["maskt"][0:P, 0:W],
                        tmp["fixq"][0:P, 0:W], deps=[mq, fq, out["qop"], dst_war])
            out["done"] = done
            T[f"q_w_war{b}"] = w2
            T[f"q_fa_war{b}"] = fq
            T[f"q_fq_war{b}"] = done
            T[f"q_w2_war{b}"] = mq
            T[f"q_mq_war{b}"] = done
        sec("vector")(v2)
        T[f"q_c8_war{b}"] = out["qop"]
        T[f"q_sh_war{a}"] = out["w"]
        T[f"q_sq_war{a}"] = out["done"]
        return out["done"], out["w"], out["qop"]

    p0 = ExitStack()
    P0 = lambda n, shp, dt=F32: p0.enter_context(nc.sbuf_tensor(n, shp, dt))
    ohtsb = P0("ohtsb", [NB, BL])
    Fsb = P0("Fsb", [NH, BL])
    xs = [P0(f"xs{i}", [128, H]) for i in range(2)]
    xnf = P0("xnf", [128, H])
    CW0 = min(512, H)
    NST = H // CW0
    stats = P0("stats", [128, 6 * NST])
    mvt = P0("mvt", [128, 2])
    sdv = P0("sdv", [128, 1])
    inv = P0("inv", [128, 1])
    CH0 = H // 2
    tmp0 = [{"set": a, "wt": P0(f"t0_wt{a}", [128, CH0]),
             "fixa": P0(f"t0_fa{a}", [128, CH0]),
             "w2t": P0(f"t0_w2{a}", [128, CH0]),
             "maskt": P0(f"t0_mk{a}", [128, CH0], U8),
             "c8t": P0(f"t0_c8{a}", [128, CH0], F8),
             "fixq": P0(f"t0_fq{a}", [128, CH0], BF16)} for a in range(2)]
    xn_bf = [P0(f"xnb{i}", [128, H], BF16) for i in range(2)]
    qx_op = [P0(f"qx{i}", [128, H], BF16) for i in range(2)]
    qxT_f = [P0(f"qxTf{i}", [128, RO], BF16) for i in range(HT)]
    ps_f = p0.enter_context(nc.psum_tensor("ps_f", [NH, 512], F32))
    ps_tp0 = [p0.enter_context(nc.psum_tensor(f"ps_tp0{i}", [128, 128], BF16))
              for i in range(2)]

    xnT_es = ExitStack()
    xnT = [xnT_es.enter_context(nc.sbuf_tensor(f"xnT{i}", [128, S], BF16,
                                               side="right"))
           for i in range(HT)]

    def p0_loads(E):
        E.do("dma_start", cosP[:], cosP_d.ap())
        E.do("dma_start", sinP[:], sinP_d.ap())
        E.do("dma_start", identb[:], id_d.ap())
        E.do("dma_start", cmask[:], cm_d.ap())
        E.do("dma_start", relsb[:], rel_d.ap())
        E.do("dma_start", fmask[:], fm_d.ap())
        tok = E.do("dma_start", ohtsb[:], oht1_d.ap())
        for k in ("cos", "sin", "ident", "cmask", "fmask", "rel", "oht1"):
            T[k] = tok
    sec("sync", "qc")(p0_loads)

    def xload_emit(i):
        def xload(E, i=i):
            T[f"x{i}"] = E.do("dma_start", xs[i % 2][:],
                              x_d.ap()[i * 128:(i + 1) * 128, :],
                              deps=[T.get(f"xfree{i - 2}")])
        sec("sync", f"qx{i % 2}")(xload)

    xload_emit(0)
    xload_emit(1)

    NCHB = (BL + 511) // 512

    def emit_table(tag, oht_tok, dst, war=None):
        for ch in range(NCHB):
            c0 = ch * 512
            cw = min(512, BL - c0)

            def tmm(E, ch=ch, c0=c0, cw=cw):
                T[f"F{tag}mm{ch}"] = E.do(
                    "matmul", ps_f[0:NH, 0:cw], relsb[0:NB, :],
                    ohtsb[:, c0:c0 + cw], start=True, stop=True,
                    deps=[T["rel"], oht_tok, T.get(f"F{tag}ev{ch - 1}"),
                          war if ch == 0 else None])
            sec("tensor")(tmm)

            def tev(E, ch=ch, c0=c0, cw=cw):
                T[f"F{tag}ev{ch}"] = E.do(
                    "tensor_copy", Fsb[:, c0:c0 + cw], ps_f[0:NH, 0:cw],
                    deps=[T[f"F{tag}mm{ch}"],
                          T.get(f"F{'1' if tag == '2' else 'x'}st")])
            sec("vector")(tev)

        def tst(E):
            T[f"F{tag}st"] = E.do("dma_start", dst.ap(), Fsb[:],
                                  deps=[T[f"F{tag}ev{NCHB - 1}"]])
        sec("gpsimd", "gf")(tst)

    emit_table("1", T["oht1"], F1_hbm)

    def oht2_load(E):
        T["oht2"] = E.do("dma_start", ohtsb[:], oht2_d.ap(),
                         deps=[T[f"F1mm{NCHB - 1}"]])
    sec("sync", "qc")(oht2_load)

    emit_table("2", T["oht2"], F2_hbm, war=T[f"F1ev{NCHB - 1}"])

    def eps_set(E):
        T["eps"] = E.do("memset", eps_t[:], 1e-5)
    sec("vector")(eps_set)

    for i in range(RT):
        own = i < OT
        if i >= 1 and i + 1 < RT:
            xload_emit(i + 1)

        def ln_v1(E, i=i):
            t1 = None
            for c in range(NST):
                t1 = E.do("bn_stats", stats[:, 6 * c:6 * c + 6],
                          xs[i % 2][:, CW0 * c:CW0 * (c + 1)],
                          deps=[T[f"x{i}"], T.get(f"mv{i - 1}")] if c == 0 else [t1])
            T[f"mv{i}"] = E.do("bn_aggr", mvt[:], stats[:],
                               deps=[t1, T.get(f"xn{i - 1}")])
        sec("vector")(ln_v1)

        def ln_s1(E, i=i):
            T[f"sd{i}"] = E.do("activation", sdv[:], mvt[:, 1:2], ACT.Sqrt,
                               bias=eps_t[:, 0:1],
                               deps=[T[f"mv{i}"], T["eps"], T.get(f"inv{i - 1}")])
        sec("scalar")(ln_s1)

        def ln_v2(E, i=i):
            T[f"inv{i}"] = E.do("reciprocal", inv[:], sdv[:], deps=[T[f"sd{i}"]])
            T[f"xn{i}"] = E.do("tensor_scalar", xnf[:], xs[i % 2][:], mvt[:, 0:1],
                               inv[:], op0=ALU.subtract, op1=ALU.mult,
                               deps=[T[f"inv{i}"], T.get(f"xnused{i - 1}")])
            T[f"xfree{i}"] = T[f"xn{i}"]
        sec("vector")(ln_v2)

        def ln_s2(E, i=i):
            T[f"xb{i}"] = E.do("activation", xn_bf[i % 2][:], xnf[:], ACT.Copy,
                               deps=[T[f"xn{i}"],
                                     T.get(f"xtp_done{i - 2}")])
        sec("scalar")(ln_s2)

        if own:
            sset = q_scales("p0", xnf[:], deps_src=[T[f"xn{i}"]])
            lastd = lastw = lastq = None
            for ci, c0 in enumerate((0, CH0)):
                d_, w_, q_ = q_apply(tmp0[ci % 2], xnf[:, c0:c0 + CH0],
                                     qx_op[i % 2][:, c0:c0 + CH0], sset,
                                     dst_war=T.get(f"qxtp_done{i - 2}") if c0 == 0 else None)
                lastd, lastw, lastq = d_, w_, q_
            T[f"qfix{i}"] = lastd
            T[f"qfix{i}_s"] = lastq
            T[f"xnused{i}"] = lastw
        else:
            T[f"xnused{i}"] = T[f"xb{i}"]

        for ht in range(HT):
            def xnt_t(E, i=i, ht=ht):
                T[f"xtp{ht}_{i}"] = E.do(
                    "transpose", ps_tp0[ht % 2][:],
                    xn_bf[i % 2][:, ht * 128:(ht + 1) * 128], identb[:],
                    deps=[T[f"xb{i}"], T["ident"],
                          T.get(f"tp0war{ht % 2}")])
            sec("tensor")(xnt_t)

            def xnt_v(E, i=i, ht=ht):
                T[f"xte{ht}_{i}"] = E.do(
                    "tensor_copy", xnT[ht][:, i * 128:(i + 1) * 128],
                    ps_tp0[ht % 2][:], deps=[T[f"xtp{ht}_{i}"]])
                T[f"tp0war{ht % 2}"] = T[f"xte{ht}_{i}"]
            sec("vector")(xnt_v)
        T[f"xtp_done{i}"] = T[f"xtp{HT - 1}_{i}"]

        if own:
            for ht in range(HT):
                def qxt_t(E, i=i, ht=ht):
                    T[f"qxtp{ht}_{i}"] = E.do(
                        "transpose", ps_tp0[ht % 2][:],
                        qx_op[i % 2][:, ht * 128:(ht + 1) * 128], identb[:],
                        deps=[T[f"qfix{i}"], T[f"qfix{i}_s"],
                              T.get(f"tp0war{ht % 2}")])
                sec("tensor")(qxt_t)

                def qxt_v(E, i=i, ht=ht):
                    T[f"qxte{ht}_{i}"] = E.do(
                        "tensor_copy", qxT_f[ht][:, i * 128:(i + 1) * 128],
                        ps_tp0[ht % 2][:], deps=[T[f"qxtp{ht}_{i}"]])
                    T[f"tp0war{ht % 2}"] = T[f"qxte{ht}_{i}"]
                sec("vector")(qxt_v)
            T[f"qxtp_done{i}"] = T[f"qxtp{HT - 1}_{i}"]

    for ht in range(HT):
        def qxt_g(E, ht=ht):
            T[f"qxst{ht}"] = E.do("dma_start",
                                  qxT_hbm.ap()[ht * 128:(ht + 1) * 128, :],
                                  qxT_f[ht][:],
                                  deps=[T[f"qxte{ht}_{OT - 1}"]])
        sec("gpsimd", "gqxt")(qxt_g)
    T["qxst_done"] = T[f"qxst{HT - 1}"]

    T["xnT_done"] = T[f"xte{HT - 1}_{RT - 1}"]
    p0.close()
    barrier()

    pg = ExitStack()
    PG = lambda n, shp, dt=F32: pg.enter_context(nc.sbuf_tensor(n, shp, dt))
    dpst = [pg.enter_context(nc.sbuf_tensor(f"dpst{a}", [128, H], F32))
            for a in range(2)]
    dpsb = [pg.enter_context(nc.sbuf_tensor(f"dpsb{a}", [128, H], BF16))
            for a in range(2)]
    pgst = [PG(f"pgst{a}", [128, max(H, I)]) for a in range(2)]
    pgq = PG("pgq", [128, max(H, I)], BF16)
    tmpg = [{"set": a, "wt": PG(f"tg_wt{a}", [128, CHK]),
             "fixa": PG(f"tg_fa{a}", [128, CHK]),
             "w2t": PG(f"tg_w2{a}", [128, CHK]),
             "maskt": PG(f"tg_mk{a}", [128, CHK], U8),
             "c8t": PG(f"tg_c8{a}", [128, CHK], F8),
             "fixq": PG(f"tg_fq{a}", [128, CHK], BF16)} for a in range(2)]
    cbF = PG("cbF", [128, max(HT * ISL, (I // 128) * WDS)], BF16)
    ps_tpg = [pg.enter_context(nc.psum_tensor(f"ps_tpg{a}", [128, 128], BF16))
              for a in range(2)]
    cc_sem = es.enter_context(nc.semaphore("cc_sem"))

    def emit_slice_prep(key, w_in, rows, rowlen, nblk, off_cb, prev_done,
                        prev_cb=None):
        ntile = (rows + 127) // 128
        for st in range(ntile):
            P = min(128, rows - st * 128)

            def sload(E, st=st, P=P, key=key, w_in=w_in, rowlen=rowlen):
                T[f"pgl{key}{st}"] = E.do(
                    "dma_start", pgst[st % 2][0:P, 0:rowlen],
                    w_in.ap()[st * 128:st * 128 + P, :],
                    deps=[T.get(f"pgw{key}{st - 2}")] +
                         (list(prev_done) if st < 2 else []))
            sec("sync", f"qw1{st % 2}")(sload)

            sset = q_scales(f"pg{key}", pgst[st % 2][0:P, 0:rowlen],
                            deps_src=[T[f"pgl{key}{st}"]])
            lastd = lastw = lastq = None
            for ch in range(rowlen // CHK):
                c0 = ch * CHK
                lastd, lastw, lastq = q_apply(
                    tmpg[ch % 2], pgst[st % 2][0:P, c0:c0 + CHK],
                    pgq[0:P, c0:c0 + CHK], sset,
                    dst_war=T.get(f"pgtp{key}{st - 1}") if ch == 0 else None)
            T[f"pgw{key}{st}"] = lastw
            T[f"pgq{key}{st}"] = lastd
            T[f"pgq{key}{st}_s"] = lastq

            for bt in range(nblk):
                def ptp(E, st=st, P=P, bt=bt, key=key, nblk=nblk):
                    T[f"pgt{key}{st}_{bt}"] = E.do(
                        "transpose", ps_tpg[bt % 2][:, 0:P],
                        pgq[0:P, bt * 128:(bt + 1) * 128], identb[0:P, 0:P],
                        deps=[T[f"pgq{key}{st}"], T[f"pgq{key}{st}_s"],
                              T.get(f"pge{key}{st}_{bt - 2}"),
                              T.get(f"pge{key}{st - 1}_{nblk - 2 + (bt % 2)}")])
                sec("tensor")(ptp)

                def pev(E, st=st, P=P, bt=bt, key=key, rows=rows):
                    T[f"pge{key}{st}_{bt}"] = E.do(
                        "tensor_copy",
                        cbF[:, bt * rows + st * 128: bt * rows + st * 128 + P],
                        ps_tpg[bt % 2][:, 0:P],
                        deps=[T[f"pgt{key}{st}_{bt}"],
                              prev_cb if (st == 0 and bt == 0) else None])
                sec("vector")(pev)
            T[f"pgtp{key}{st}"] = T[f"pgt{key}{st}_{nblk - 1}"]

        def cbst(E, key=key, rows=rows, nblk=nblk, off_cb=off_cb, ntile=ntile):
            dst = bass_rust.AP(cb_hbm, off_cb,
                               [[rows, 128], [128 * rows, nblk], [1, rows]])
            W = max(HT * ISL, (I // 128) * WDS)
            srcap = bass_rust.AP(cbF, 0, [[W, 128], [rows, nblk], [1, rows]])
            T[f"cbst{key}"] = E.do("dma_start", dst, srcap,
                                   deps=[T[f"pge{key}{ntile - 1}_{nblk - 1}"]])
        sec("gpsimd", "gw")(cbst)
        return T[f"cbst{key}"], T[f"pgw{key}{ntile - 1}"], T[f"pgq{key}{ntile - 1}"]

    dg = emit_slice_prep("g", wg_d, ISL, H, HT, 0, [])
    du = emit_slice_prep("u", wu_d, ISL, H, HT, CSZ_G, dg[1:], prev_cb=dg[0])
    dd_ = emit_slice_prep("d", wd_d, WDS, I, I // 128, 2 * CSZ_G, du[1:],
                          prev_cb=du[0])

    def ccgo(E):
        E.wait(dg[0], du[0], dd_[0])
        instr = E.e.collective_compute(
            "AllGather", ALU.bypass, replica_groups=[list(range(8))],
            ins=[cb_hbm.ap().opt()], outs=[G_hbm.ap().opt()])
        instr.then_inc(cc_sem, 1)
    sec("gpsimd", "gw")(ccgo)
    T["ccdone"] = (cc_sem, 1)

    for i in range(HT):
        def dcl(E, i=i):
            T[f"dcl{i}"] = E.do("dma_start", dpst[i % 2][:],
                                dense_d.ap()[i * 128:(i + 1) * 128, :],
                                deps=[T.get(f"dcc{i - 2}")])
        sec("sync", f"qx{i % 2}")(dcl)

        def dcc(E, i=i):
            T[f"dcc{i}"] = E.do("activation", dpsb[i % 2][:], dpst[i % 2][:],
                                ACT.Copy, deps=[T[f"dcl{i}"],
                                                T.get(f"dcs{i - 2}")])
        sec("scalar")(dcc)

        def dcs(E, i=i):
            T[f"dcs{i}"] = E.do("dma_start",
                                dwb_hbm.ap()[i * 128:(i + 1) * 128, :],
                                dpsb[i % 2][:], deps=[T[f"dcc{i}"]])
        sec("gpsimd", "gw" if i % 2 == 0 else "gf")(dcs)
    T["dprec"] = T[f"dcs{HT - 1}"]
    T["dprec2"] = T[f"dcs{HT - 2}"]

    pg.close()
    barrier()

    p1 = ExitStack()
    P1 = lambda n, shp, dt=F32: p1.enter_context(nc.sbuf_tensor(n, shp, dt))
    wq_b = [P1(f"wqb{i}", [128, H], BF16) for i in range(2)]
    wk_b = [P1(f"wkb{i}", [128, H], BF16) for i in range(2)]
    wv_b = [P1(f"wvb{i}", [128, H], BF16) for i in range(2)]
    wstg = [P1("wstg0", [128, H])]
    qb_t = [P1(f"qb_t{a}", [64, RO], BF16) for a in range(2)]
    qb_b = [P1(f"qb_b{a}", [64, RO], BF16) for a in range(2)]
    kb_t = [P1(f"kb_t{a}", [64, S], BF16) for a in range(2)]
    kb_b = [P1(f"kb_b{a}", [64, S], BF16) for a in range(2)]
    vS = P1("vS", [128, S], BF16)
    pT = [P1(f"pT{i}", [128, RO], BF16) for i in range(RT)]
    bias_sb = [P1(f"bias_sb{a}", [128, S], BF16) for a in range(2)]
    hank_sb = P1("hank_sb", [128, S], F32)
    s1 = P1("s1", [128, S], F32)
    esb = P1("esb", [128, S], BF16)
    pbuf = P1("pbuf", [128, S], BF16)
    vTs = pbuf
    rowsum = P1("rowsum", [128, 1], F32)
    rs_acc = P1("rs_acc", [128, 1], F32)
    rtmp1f = P1("rtmp1f", [64, S])
    rtmp = [s1[0:64, :], rtmp1f[:]]
    ctxT_b = P1("ctxT_b", [128, RO], BF16)
    ps_a = p1.enter_context(nc.psum_tensor("ps_a", [128, S], F32))
    ps_sc = p1.enter_context(nc.psum_tensor("ps_sc", [128, S], F32))
    if S <= 1920:
        ps_tp1h = p1.enter_context(nc.psum_tensor("ps_tp1", [128, 128], BF16))
        ps_tp1i = p1.enter_context(nc.psum_tensor("ps_tp1b", [128, 128], BF16))
        tp1s = [ps_tp1h[:], ps_tp1i[:]]
    else:
        tp1s = [ps_sc[:, 3 * S // 4 - 64:3 * S // 4].bitcast(BF16),
                ps_sc[:, S - 64:S].bitcast(BF16)]

    for h in range(NH):
        hb = h % 2

        for ki, (key, base, wb) in enumerate((("wq", 0, wq_b), ("wk", H, wk_b),
                                              ("wv", 2 * H, wv_b))):
            j = 3 * h + ki

            def wload(E, h=h, hb=hb, key=key, base=base, j=j):
                srcap = bass_rust.AP(qkv_d, base + h * 128,
                                     [[3 * H, 128], [128 * 3 * H, HT], [1, 128]])
                dst = bass_rust.AP(wstg[0], 0, [[H, 128], [128, HT], [1, 128]])
                T[f"{key}l{h}"] = E.do("dma_start", dst, srcap,
                                       deps=[T.get(f"wstgu{j - 1}")])
            sec("sync", "qw")(wload)

            def wcast(E, h=h, hb=hb, key=key, wb=wb, j=j):
                T[f"{key}c{h}"] = E.do("tensor_copy", wb[hb][:],
                                       wstg[0][:],
                                       deps=[T[f"{key}l{h}"],
                                             T.get(f"{key}used{h - 2}")])
                T[f"wstgu{j}"] = T[f"{key}c{h}"]
            sec("vector")(wcast)

        def hankload(E, h=h, hb=hb):
            for j in range(OT):
                off = h * BL + (OT - 1 - j) * 256 + 1
                E.do("dma_start", hank_sb[:, j * 128:(j + 1) * 128],
                     bass_rust.AP(F1_hbm, off, [[1, 128], [1, 128]]),
                     deps=[T["F2st"], T.get(f"hkrev{h - 1}")] if j == 0 else [])
                T[f"hkall{h}"] = E.do(
                    "dma_start",
                    hank_sb[:, (OT + j) * 128:(OT + j + 1) * 128],
                    bass_rust.AP(F2_hbm, off, [[1, 128], [1, 128]]))
        sec("gpsimd", f"gb{h % 2}")(hankload)

        def hankrev(E, h=h, hb=hb):
            last = None
            for j in range(2 * OT):
                last = E.do(
                    "tensor_copy", bias_sb[hb][:, j * 128:(j + 1) * 128],
                    hank_sb[:, j * 128:(j + 1) * 128][:, ::-1],
                    deps=[T[f"hkall{h}"],
                          T.get(f"bused{hb}") if j == 0 else None])
            T[f"biasb{h}"] = last
            T[f"hkrev{h}"] = last
        sec("vector")(hankrev)

        def qmm(E, h=h, hb=hb):
            for half in (0, 64):
                for ht in range(HT):
                    for c0 in range(0, RO, 512):
                        cw = min(512, RO - c0)
                        base = (half // 64) * RO
                        T["mmq"] = E.do(
                            "matmul", ps_a[0:64, base + c0:base + c0 + cw],
                            wq_b[hb][:, ht * 128 + half:ht * 128 + half + 64],
                            xnT[ht][:, c0:c0 + cw],
                            start=(ht == 0), stop=(ht == HT - 1),
                            deps=[T[f"wqc{h}"], T["xnT_done"],
                                  T.get(f"vTev{h - 1}"),
                                  T.get(f"ropeq{h - 1}")])
            T[f"mmq{h}"] = T["mmq"]
        sec("tensor")(qmm)

        def ropeq(E, h=h, hb=hb):
            A, B = ps_a[0:64, 0:RO], ps_a[0:64, RO:2 * RO]
            co, si = cosP[:, 0:RO], sinP[:, 0:RO]
            dd = [T[f"mmq{h}"], T["cos"], T["sin"], T.get(f"qbused{h - 2}"),
                  T.get(f"rtmpused{h - 1}")]
            t1 = E.do("tensor_tensor", rtmp[0][:, 0:RO], A, co, op=ALU.mult,
                      deps=dd)
            t2 = E.do("tensor_tensor", rtmp[1][:, 0:RO], B, si, op=ALU.mult)
            t3 = E.do("tensor_tensor", qb_t[hb][:], rtmp[0][:, 0:RO],
                      rtmp[1][:, 0:RO], op=ALU.subtract, deps=[t1, t2])
            t4 = E.do("tensor_tensor", rtmp[0][:, 0:RO], B, co, op=ALU.mult,
                      deps=[t3])
            t5 = E.do("tensor_tensor", rtmp[1][:, 0:RO], A, si, op=ALU.mult,
                      deps=[t3])
            T[f"ropeq{h}"] = E.do("tensor_tensor", qb_b[hb][:],
                                  rtmp[0][:, 0:RO], rtmp[1][:, 0:RO],
                                  op=ALU.add, deps=[t4, t5])
        sec("vector")(ropeq)

        HS = S // 2
        for kh in range(2):
            cbase = kh * HS

            def kmm(E, h=h, hb=hb, kh=kh, cbase=cbase, HS=HS):
                for half in (0, 64):
                    for ht in range(HT):
                        for c0 in range(0, HS, 512):
                            cw = min(512, HS - c0)
                            base = (half // 64) * HS
                            T["mmk"] = E.do(
                                "matmul",
                                ps_a[0:64, base + c0:base + c0 + cw],
                                wk_b[hb][:, ht * 128 + half:ht * 128 + half + 64],
                                xnT[ht][:, cbase + c0:cbase + c0 + cw],
                                start=(ht == 0), stop=(ht == HT - 1),
                                deps=[T[f"wkc{h}"], T[f"ropeq{h}"],
                                      T.get(f"ropek{h}_{kh - 1}")])
                T[f"mmk{h}_{kh}"] = T["mmk"]
            sec("tensor")(kmm)

            def ropek(E, h=h, hb=hb, kh=kh, cbase=cbase, HS=HS):
                A, B = ps_a[0:64, 0:HS], ps_a[0:64, HS:2 * HS]
                co = cosP[:, cbase:cbase + HS]
                si = sinP[:, cbase:cbase + HS]
                dd = [T[f"mmk{h}_{kh}"], T.get(f"kbused{h - 2}")]
                t1 = E.do("tensor_tensor", rtmp[0][:, 0:HS], A, co,
                          op=ALU.mult, deps=dd)
                t2 = E.do("tensor_tensor", rtmp[1][:, 0:HS], B, si,
                          op=ALU.mult)
                t3 = E.do("tensor_tensor", kb_t[hb][:, cbase:cbase + HS],
                          rtmp[0][:, 0:HS], rtmp[1][:, 0:HS],
                          op=ALU.subtract, deps=[t1, t2])
                t4 = E.do("tensor_tensor", rtmp[0][:, 0:HS], B, co,
                          op=ALU.mult, deps=[t3])
                t5 = E.do("tensor_tensor", rtmp[1][:, 0:HS], A, si,
                          op=ALU.mult, deps=[t3])
                T[f"ropek{h}_{kh}"] = E.do(
                    "tensor_tensor", kb_b[hb][:, cbase:cbase + HS],
                    rtmp[0][:, 0:HS], rtmp[1][:, 0:HS], op=ALU.add,
                    deps=[t4, t5])
            sec("vector")(ropek)
        T[f"ropek{h}"] = T[f"ropek{h}_1"]
        T[f"rtmpused{h}"] = T[f"ropek{h}"]

        def vmm(E, h=h, hb=hb):
            for ht in range(HT):
                for c0 in range(0, S, 512):
                    T["mmv"] = E.do("matmul", ps_a[:, c0:c0 + 512],
                                    wv_b[hb][:, ht * 128:(ht + 1) * 128],
                                    xnT[ht][:, c0:c0 + 512],
                                    start=(ht == 0), stop=(ht == HT - 1),
                                    deps=[T[f"wvc{h}"], T[f"ropek{h}"]])
            T[f"mmv{h}"] = T["mmv"]
            T[f"wqused{h}"] = T[f"mmq{h}"]
            T[f"wkused{h}"] = T[f"mmk{h}_1"]
            T[f"wvused{h}"] = T[f"mmv{h}"]
        sec("tensor")(vmm)

        def vev0(E, h=h):
            T[f"vTev{h}"] = E.do("tensor_copy", vTs[:], ps_a[:, 0:S],
                                 deps=[T[f"mmv{h}"], T.get(f"vTused{h - 1}"),
                                       T.get(f"pevlast{h - 1}")])
        sec("vector")(vev0)

        for kt in range(RT):
            def vtp_t(E, h=h, kt=kt):
                T[f"vtp{h}_{kt}"] = E.do("transpose", tp1s[kt % 2],
                                         vTs[:, kt * 128:(kt + 1) * 128],
                                         identb[:],
                                         deps=[T[f"vTev{h}"],
                                               T.get(f"vev{h}_{kt - 2}"),
                                               T.get(f"pevlast{h - 1}") if kt < 2 else None,
                                               T.get(f"ctev{h - 1}") if kt < 2 else None])
            sec("tensor")(vtp_t)

            def vtp_v(E, h=h, kt=kt):
                T[f"vev{h}_{kt}"] = E.do("tensor_copy",
                                         vS[:, kt * 128:(kt + 1) * 128],
                                         tp1s[kt % 2],
                                         deps=[T[f"vtp{h}_{kt}"],
                                               T.get(f"vSused{h - 1}")])
            sec("vector")(vtp_v)
        T[f"vSdone{h}"] = T[f"vev{h}_{RT - 1}"]
        T[f"vTused{h}"] = T[f"vtp{h}_{RT - 1}"]

        for n in range(OT):
            On = (n + 1) * 128
            Kn = 2 * On

            def smm(E, h=h, n=n, On=On, Kn=Kn):
                dd = [T[f"ropek{h}"], T[f"ropeq{h}"], T.get(f"s1ev{h}_{n - 1}"),
                      T.get(f"pevlast{h}_{n - 1}"), T[f"vSdone{h}"],
                      T.get(f"ctev{h - 1}") if n == 0 else None]
                qs_ = n * 128
                for c0 in range(0, On, 512):
                    cw = min(512, On - c0)
                    T["mmsc"] = E.do("matmul", ps_sc[:, c0:c0 + cw],
                                     qb_t[h % 2][:, qs_:qs_ + 128],
                                     kb_t[h % 2][:, c0:c0 + cw], start=True,
                                     stop=False, deps=dd)
                    T["mmsc"] = E.do("matmul", ps_sc[:, c0:c0 + cw],
                                     qb_b[h % 2][:, qs_:qs_ + 128],
                                     kb_b[h % 2][:, c0:c0 + cw], start=False,
                                     stop=True)
                c0 = On
                while c0 < Kn:
                    cw = min((c0 // 512 + 1) * 512, Kn) - c0
                    kc = RO + (c0 - On)
                    T["mmsc"] = E.do("matmul", ps_sc[:, c0:c0 + cw],
                                     qb_t[h % 2][:, qs_:qs_ + 128],
                                     kb_t[h % 2][:, kc:kc + cw], start=True,
                                     stop=False, deps=dd)
                    T["mmsc"] = E.do("matmul", ps_sc[:, c0:c0 + cw],
                                     qb_b[h % 2][:, qs_:qs_ + 128],
                                     kb_b[h % 2][:, kc:kc + cw], start=False,
                                     stop=True)
                    c0 += cw
                T[f"mmsc{h}_{n}"] = T["mmsc"]
            sec("tensor")(smm)

            def sfx(E, h=h, n=n, On=On, Kn=Kn):
                bo = (OT - 1 - n) * 128
                t1 = E.do("scalar_tensor_tensor", s1[:, 0:On], ps_sc[:, 0:On],
                          SCALE_QK, bias_sb[h % 2][:, bo:bo + On], op0=ALU.mult,
                          op1=ALU.add,
                          deps=[T[f"mmsc{h}_{n}"], T[f"biasb{h}"],
                                T.get(f"s1used{h}_{n - 1}"),
                                T.get(f"s1used{h - 1}_{OT - 1}") if n == 0 else None])
                t1 = E.do("scalar_tensor_tensor", s1[:, On:Kn],
                          ps_sc[:, On:Kn], SCALE_QK,
                          bias_sb[h % 2][:, OT * 128 + bo:OT * 128 + bo + On],
                          op0=ALU.mult, op1=ALU.add)
                T[f"bused{h % 2}"] = t1
                T[f"s1ev{h}_{n}"] = t1
                t2 = E.do("tensor_tensor", s1[:, On - 128:On], s1[:, On - 128:On],
                          cmask[:], op=ALU.add, deps=[t1, T["cmask"]])
                t3 = E.do("tensor_tensor", s1[:, Kn - 128:Kn], s1[:, Kn - 128:Kn],
                          fmask[:], op=ALU.add, deps=[t1, T["fmask"]])
                T[f"nm{h}_{n}"] = E.do("reduce_max", rowsum[:], s1[:, 0:Kn],
                                       axis=AX.X, negate=True,
                                       deps=[t2, t3, T.get(f"rsused{h}_{n - 1}"),
                                             T.get(f"rsused{h - 1}_{OT - 1}") if n == 0 else None])
            sec("vector")(sfx)

            def sexp(E, h=h, n=n, Kn=Kn):
                T[f"exp{h}_{n}"] = E.do("activation", esb[:, 0:Kn], s1[:, 0:Kn],
                                        ACT.Exp, bias=rowsum[:, 0:1],
                                        accum_out=rs_acc[:, 0:1],
                                        deps=[T[f"nm{h}_{n}"],
                                              T.get(f"pbb{h}_{n - 1}"),
                                              T.get(f"pbb{h - 1}_{OT - 1}") if n == 0 else None])
                T[f"s1used{h}_{n}"] = T[f"exp{h}_{n}"]
            sec("scalar")(sexp)

            def srec(E, h=h, n=n):
                T[f"rsi{h}_{n}"] = E.do("reciprocal", rowsum[:], rs_acc[:],
                                        deps=[T[f"exp{h}_{n}"]])
            sec("vector")(srec)

            def spb(E, h=h, n=n, Kn=Kn):
                T[f"pbb{h}_{n}"] = E.do("tensor_scalar", pbuf[:, 0:Kn],
                                        esb[:, 0:Kn], rowsum[:, 0:1], None,
                                        op0=ALU.mult,
                                        deps=[T[f"rsi{h}_{n}"],
                                              T.get(f"pbused{h}_{n - 1}"),
                                              T[f"vTused{h}"] if n == 0 else None,
                                              T.get(f"pbused{h - 1}_{OT - 1}") if n == 0 else None])
                T[f"rsused{h}_{n}"] = T[f"pbb{h}_{n}"]
            sec("vector")(spb)

            for ct in range(Kn // 128):
                def ptp_t(E, h=h, n=n, ct=ct):
                    T[f"ptp{h}_{n}_{ct}"] = E.do(
                        "transpose", tp1s[ct % 2],
                        pbuf[:, ct * 128:(ct + 1) * 128], identb[:],
                        deps=[T[f"pbb{h}_{n}"], T.get(f"pev{h}_{n}_{ct - 2}"),
                              T.get(f"pevlast{h}_{n - 1}") if ct < 2 else None,
                              T[f"vSdone{h}"] if ct < 2 else None])
                sec("tensor")(ptp_t)

                def ptp_v(E, h=h, n=n, ct=ct):
                    kt = ct if ct <= n else OT + (ct - (n + 1))
                    T[f"pev{h}_{n}_{ct}"] = E.do(
                        "tensor_copy", pT[kt][:, n * 128:(n + 1) * 128],
                        tp1s[ct % 2],
                        deps=[T[f"ptp{h}_{n}_{ct}"],
                              T.get(f"pTused{h - 1}") if (ct == 0 and n == 0) else None])
                sec("vector")(ptp_v)
            T[f"pevlast{h}_{n}"] = T[f"pev{h}_{n}_{Kn // 128 - 1}"]
            T[f"pbused{h}_{n}"] = T[f"pevlast{h}_{n}"]

        T[f"pevlast{h}"] = T[f"pevlast{h}_{OT - 1}"]

        def avmm(E, h=h):
            nkt = 2 * OT
            last = None
            for kt in range(nkt):
                qs = kt if kt < OT else kt - OT
                deps = [T[f"vSdone{h}"], T[f"ropek{h}"], T.get(f"ctev{h - 1}")]
                for n in range(qs, OT):
                    ct = kt if kt < OT else (n + 1) + (kt - OT)
                    deps.append(T[f"pev{h}_{n}_{ct}"])
                if kt == 0:
                    deps.append(T[f"s1ev{h}_{OT - 1}"])
                for c0 in range(0, RO, 512):
                    a = max(c0, qs * 128)
                    b = min(c0 + 512, RO)
                    if a >= b:
                        continue
                    last = E.do("matmul", ps_sc[:, a:b],
                                vS[:, kt * 128:(kt + 1) * 128], pT[kt][:, a:b],
                                start=(kt == 0),
                                stop=(kt == nkt - 1 and b == RO), deps=deps)
            T[f"mmav{h}"] = last
            T[f"vSused{h}"] = last
            T[f"pTused{h}"] = last
        sec("tensor")(avmm)

        def ctev(E, h=h):
            T[f"ctev{h}"] = E.do("tensor_copy", ctxT_b[:], ps_sc[:, 0:RO],
                                 deps=[T[f"mmav{h}"], T.get(f"ctst{h - 1}")])
            T[f"qbused{h}"] = T[f"mmsc{h}_{OT - 1}"]
            T[f"kbused{h}"] = T[f"mmsc{h}_{OT - 1}"]
        sec("vector")(ctev)

        def ctst(E, h=h):
            T[f"ctst{h}"] = E.do("dma_start",
                                 ctxT_hbm.ap()[h * 128:(h + 1) * 128, :],
                                 ctxT_b[:], deps=[T[f"ctev{h}"]])
        sec("gpsimd", "gct")(ctst)

    p1.close()
    xnT_es.close()
    barrier()

    pd = ExitStack()
    PD = lambda n, shp, dt=F32: pd.enter_context(nc.sbuf_tensor(n, shp, dt))
    wd_b = [PD(f"wdb{i}", [128, H], BF16) for i in range(HT)]
    ctx_sb = [PD(f"ctxs{i}", [128, RO], BF16) for i in range(HT)]
    att_f = PD("att_f", [128, H])
    ps_d = pd.enter_context(nc.psum_tensor("ps_d", [128, H], F32))

    def wdload_d(E):
        for ct in range(HT):
            T[f"wdc{ct}"] = E.do("dma_start", wd_b[ct][:],
                                 dwb_hbm.ap()[ct * 128:(ct + 1) * 128, :],
                                 deps=[T["dprec"], T["dprec2"]] if ct == 0 else [])
    sec("sync", "qctx")(wdload_d)

    def dctx(E):
        for hh in range(HT):
            T[f"ctl{hh}"] = E.do("dma_start", ctx_sb[hh][:],
                                 ctxT_hbm.ap()[hh * 128:(hh + 1) * 128, :],
                                 deps=[T[f"ctst{NH - 1}"]])
        T["ctl_all"] = T[f"ctl{HT - 1}"]
    sec("sync", "qctx")(dctx)

    for rt in range(OT):
        def dmm(E, rt=rt):
            for ct in range(HT):
                for c0 in range(0, H, 512):
                    cw = min(512, H - c0)
                    T["mmd"] = E.do("matmul", ps_d[:, c0:c0 + cw],
                                    ctx_sb[ct][:, rt * 128:(rt + 1) * 128],
                                    wd_b[ct][:, c0:c0 + cw],
                                    start=(ct == 0), stop=(ct == HT - 1),
                                    deps=[T["ctl_all"],
                                          T.get(f"attev{rt - 1}")])
            T[f"mmd{rt}"] = T["mmd"]
        sec("tensor")(dmm)

        def dev(E, rt=rt):
            T[f"attev{rt}"] = E.do("tensor_copy", att_f[:], ps_d[:, 0:H],
                                   deps=[T[f"mmd{rt}"], T.get(f"attst{rt - 1}")])
        sec("vector")(dev)

        def dst(E, rt=rt):
            T[f"attst{rt}"] = E.do("dma_start",
                                   attn_hbm.ap()[rt * 128:(rt + 1) * 128, :],
                                   att_f[:], deps=[T[f"attev{rt}"]])
        sec("gpsimd", "gat")(dst)

    pd.close()
    barrier()

    GR = OT // 2
    p3a = ExitStack()
    PA = lambda n, shp, dt=F32: p3a.enter_context(
        nc.sbuf_tensor(n, shp, dt, side="right"))
    tst3 = [PA(f"tst3{i}", [128, I]) for i in range(2)]
    qtr = PA("qtr", [128, I], BF16)
    qtt = PA("qtt", [128, I], BF16)
    tmpa = [{"set": a, "wt": PA(f"ta_wt{a}", [128, CHK]),
             "fixa": PA(f"ta_fa{a}", [128, CHK]),
             "w2t": PA(f"ta_w2{a}", [128, CHK]),
             "maskt": PA(f"ta_mk{a}", [128, CHK], U8),
             "c8t": PA(f"ta_c8{a}", [128, CHK], F8),
             "fixq": PA(f"ta_fq{a}", [128, CHK], BF16)} for a in range(2)]
    ps_tpa = [p3a.enter_context(nc.psum_tensor(f"ps_tpa{i}", [128, 128], BF16))
              for i in range(2)]

    p2 = ExitStack()
    P2 = lambda n, shp, dt=F32: p2.enter_context(nc.sbuf_tensor(n, shp, dt))
    qxT_sb = [P2(f"qxTs{i}", [128, RO], BF16) for i in range(HT)]
    qwTgb = [P2(f"qwTgb{a}", [128, HT * 512], BF16) for a in range(2)]
    qwTub = [P2(f"qwTub{a}", [128, HT * 512], BF16) for a in range(2)]
    S1WL = {}
    sg_sb = [P2(f"sg_sb{i}", [128, 512]) for i in range(2)]
    tch = [P2(f"tch{i}", [128, 512], F32) for i in range(2)]
    ps_g = [p2.enter_context(nc.psum_tensor(f"ps_g{i}", [128, 512], F32))
            for i in range(2)]
    ps_u = [p2.enter_context(nc.psum_tensor(f"ps_u{i}", [128, 512], F32))
            for i in range(2)]

    def qxl(E):
        for ht in range(HT):
            T[f"qxl{ht}"] = E.do("dma_start", qxT_sb[ht][:],
                                 qxT_hbm.ap()[ht * 128:(ht + 1) * 128, :],
                                 deps=[T["qxst_done"]])
        T["qxl_done"] = T[f"qxl{HT - 1}"]
    sec("sync", "qqxl")(qxl)

    def _gpieces(w0):
        out = []
        o = 0
        w = w0
        while w < w0 + 512:
            c = w // ISL
            j0 = w - c * ISL
            ln = min(ISL - j0, w0 + 512 - w)
            out.append((c, j0, ln, o))
            o += ln
            w += ln
        return out

    def _s1prev(gi, r, k):
        s = gi * GR + r - k
        if s < 0:
            return None
        return divmod(s, GR)

    def emit_s1_ic(ps, ic):
        gi = ps * NIC + ic
        buf = gi % 2
        pieces = _gpieces(ic * 512)
        for gu, off_cb, qwB, sp in (("g", 0, qwTgb, ("qw10", "qw12")),
                                    ("u", CSZ_G, qwTub, ("qw11", "qw13"))):
            S1WL[(gi, gu)] = {}
            for pi, (c, j0, ln, o) in enumerate(pieces):
                def wTl(E, gi=gi, buf=buf, off_cb=off_cb, qwB=qwB,
                        c=c, j0=j0, ln=ln, o=o):
                    srcap = bass_rust.AP(G_hbm, c * CSZ + off_cb + j0,
                                         [[ISL, 128], [128 * ISL, HT], [1, ln]])
                    dst = bass_rust.AP(qwB[buf], o,
                                       [[HT * 512, 128], [512, HT], [1, ln]])
                    tok = E.do("dma_start", dst, srcap,
                               deps=[T["ccdone"], T.get(f"mm1done{gi - 2}")])
                    S1WL[(gi, gu)][id(tok[0])] = tok
                sec("sync", sp[pi % 2])(wTl)

        for r in range(GR):
            rt = ps * GR + r
            pp = (gi * GR + r) % 2
            pv2 = _s1prev(gi, r, 2)
            pv1 = _s1prev(gi, r, 1)

            def s1mm(E, gi=gi, rt=rt, r=r, pp=pp, buf=buf, pv2=pv2):
                for ht in range(HT):
                    dd = (list(S1WL[(gi, "g")].values()) +
                          list(S1WL[(gi, "u")].values()) + [T["qxl_done"]])
                    if pv2 is not None:
                        dd += [T[f"silu{pv2[0]}_{pv2[1]}"],
                               T[f"tmul{pv2[0]}_{pv2[1]}"]]
                    T["mm_g"] = E.do("matmul", ps_g[pp][:],
                                     qxT_sb[ht][:, rt * 128:(rt + 1) * 128],
                                     qwTgb[buf][:, ht * 512:(ht + 1) * 512],
                                     start=(ht == 0),
                                     stop=(ht == HT - 1), deps=dd)
                    T["mm_u"] = E.do("matmul", ps_u[pp][:],
                                     qxT_sb[ht][:, rt * 128:(rt + 1) * 128],
                                     qwTub[buf][:, ht * 512:(ht + 1) * 512],
                                     start=(ht == 0),
                                     stop=(ht == HT - 1))
                T[f"mm1g{gi}_{r}"] = T["mm_g"]
                T[f"mm1u{gi}_{r}"] = T["mm_u"]
            sec("tensor")(s1mm)

            def s1silu(E, gi=gi, r=r, pp=pp, pv2=pv2):
                T[f"silu{gi}_{r}"] = E.do(
                    "activation", sg_sb[pp][:], ps_g[pp][:], ACT.Silu,
                    deps=[T[f"mm1g{gi}_{r}"],
                          T[f"tmul{pv2[0]}_{pv2[1]}"] if pv2 else None])
            sec("scalar")(s1silu)

            def s1mul(E, gi=gi, r=r, pp=pp, pv2=pv2):
                T[f"tmul{gi}_{r}"] = E.do(
                    "tensor_tensor", tch[pp][:], sg_sb[pp][:], ps_u[pp][:],
                    op=ALU.mult,
                    deps=[T[f"silu{gi}_{r}"], T[f"mm1u{gi}_{r}"],
                          T[f"tst{pv2[0]}_{pv2[1]}"] if pv2 else None])
            sec("vector")(s1mul)

            def s1st(E, gi=gi, ic=ic, rt=rt, r=r, pp=pp):
                T[f"tst{gi}_{r}"] = E.do(
                    "dma_start",
                    t_hbm.ap()[rt * 128:(rt + 1) * 128, ic * 512:(ic + 1) * 512],
                    tch[pp][:], deps=[T[f"tmul{gi}_{r}"]])
            sec("gpsimd", "gt" if (gi * GR + r) % 2 == 0 else "gt2")(s1st)

        T[f"mm1done{gi}"] = T[f"mm1u{gi}_{GR - 1}"]

    PSET = {}

    def emit_p3a(rt, part):
        NCH2 = max(1, NCH // 2)
        if part == 0:
            def tload(E, rt=rt):
                ps = rt // GR
                gi_last = ps * NIC + NIC - 1
                T[f"tld{rt}"] = E.do("dma_start", tst3[rt % 2][:],
                                     t_hbm.ap()[rt * 128:(rt + 1) * 128, :],
                                     deps=[T[f"tst{gi_last}_{rt % GR}"],
                                           T.get(f"tsh{rt - 1}"),
                                           T.get(f"tqw{rt - 2}")])
            sec("sync", "qtld")(tload)

            sset = q_scales("p3a", tst3[rt % 2][:], deps_src=[T[f"tld{rt}"]])
            PSET[rt] = sset
            T[f"tsh{rt}"] = sset[1]
            chunks = range(0, NCH2)
        else:
            sset = PSET[rt]
            chunks = range(max(1, NCH // 2), NCH)
        for ch in chunks:
            c0 = ch * CHK
            done, wtok, qoptok = q_apply(
                tmpa[ch % 2], tst3[rt % 2][:, c0:c0 + CHK], qtr[:, c0:c0 + CHK],
                sset, dst_war=T.get(f"ttp_done{rt - 1}") if ch == 0 else None)
            T[f"tq{rt}"] = done
            T[f"tq{rt}_s"] = qoptok
            T[f"tqw{rt}"] = wtok
        if part == 0:
            return

        for it in range(IT):
            def ttp_t(E, rt=rt, it=it):
                T[f"ttp{rt}_{it}"] = E.do(
                    "transpose", ps_tpa[it % 2][:],
                    qtr[:, it * 128:(it + 1) * 128], identb[:],
                    deps=[T[f"tq{rt}"], T[f"tq{rt}_s"],
                          T.get(f"tte{rt}_{it - 2}")])
            sec("tensor")(ttp_t)

            def ttp_v(E, rt=rt, it=it):
                T[f"tte{rt}_{it}"] = E.do(
                    "tensor_copy", qtt[:, it * 128:(it + 1) * 128],
                    ps_tpa[it % 2][:],
                    deps=[T[f"ttp{rt}_{it}"],
                          T.get(f"qtsp{rt - 1}") if it < 2 else None])
            sec("vector")(ttp_v)
        T[f"ttp_done{rt}"] = T[f"ttp{rt}_{IT - 1}"]

        def qtsp(E, rt=rt):
            T[f"qtsp{rt}"] = E.do("dma_start",
                                  qt2_hbm.ap()[rt * 128:(rt + 1) * 128, :],
                                  qtt[:], deps=[T[f"tte{rt}_{IT - 1}"]])
        sec("gpsimd", "gqxt")(qtsp)

    for ic in range(NIC):
        emit_s1_ic(0, ic)
    npieces = 2 * GR
    done_p = 0
    for ic in range(NIC):
        emit_s1_ic(1, ic)
        if 1 <= ic and done_p < npieces:
            emit_p3a(done_p // 2, done_p % 2)
            done_p += 1
    while done_p < npieces:
        emit_p3a(done_p // 2, done_p % 2)
        done_p += 1

    p2.close()
    barrier()

    p3 = ExitStack()
    P3 = lambda n, shp, dt=F32: p3.enter_context(nc.sbuf_tensor(n, shp, dt))
    qwdTb = P3("qwdTb", [128, IT * HOG], BF16)
    qtrow = [P3(f"qtrow{i}", [128, I], BF16) for i in range(2)]
    xsl = [P3(f"xsl{i}", [128, HOG]) for i in range(2)]
    asl = [P3(f"asl{i}", [128, HOG]) for i in range(2)]
    o1 = P3("o1", [128, HOG])
    o2 = P3("o2", [128, HOG])
    ps_y = [p3.enter_context(nc.psum_tensor(f"ps_y{i}", [128, HOG], F32))
            for i in range(2)]
    CPG = HOG // WDS

    WDLT = {}

    def emit_2b_wload(g):
        WDLT[g] = {}
        for cg in range(CPG):
            def wdl(E, g=g, cg=cg):
                c = g * CPG + cg
                srcap = bass_rust.AP(G_hbm, c * CSZ + 2 * CSZ_G,
                                     [[WDS, 128], [128 * WDS, IT], [1, WDS]])
                dst = bass_rust.AP(qwdTb, cg * WDS,
                                   [[IT * HOG, 128], [HOG, IT], [1, WDS]])
                tok = E.do("dma_start", dst, srcap,
                           deps=[T["ccdone"], T.get(f"mm2done{g - 1}")])
                WDLT[g][id(tok[0])] = tok
            s2 = (("qwd", "qctx") if g % 2 == 0 else ("qw", "qqxl"))
            sec("sync", s2[cg % 2])(wdl)

    def emit_2b_rt(g, rt):
        pp = rt % 2

        def qtload(E, g=g, rt=rt, pp=pp):
            if rt >= 2:
                war = T[f"mmy{g}_{rt - 2}"]
            elif g >= 1:
                war = T[f"mmy{g - 1}_{OT - 2 + rt}"]
            else:
                war = None
            T[f"qtl{g}_{rt}"] = E.do(
                "dma_start", qtrow[pp][:],
                qt2_hbm.ap()[rt * 128:(rt + 1) * 128, :],
                deps=[T[f"qtsp{rt}"], war])
        sec("sync", "qt2l")(qtload)

        def ldres_x(E, g=g, rt=rt, pp=pp):
            T[f"xld{g}_{rt}"] = E.do(
                "dma_start", xsl[pp][:],
                x_d.ap()[rt * 128:(rt + 1) * 128, g * HOG:(g + 1) * HOG],
                deps=[T.get(f"oadd{g}_{rt - 2}"),
                      T.get(f"oadd{g - 1}_{OT - 2 + rt}") if rt < 2 else None])
        sec("sync", f"qx{pp}")(ldres_x)

        def ldres_a(E, g=g, rt=rt, pp=pp):
            T[f"ald{g}_{rt}"] = E.do(
                "dma_start", asl[pp][:],
                attn_hbm.ap()[rt * 128:(rt + 1) * 128, g * HOG:(g + 1) * HOG],
                deps=[T[f"attst{OT - 1}"], T.get(f"oadd{g}_{rt - 2}"),
                      T.get(f"oadd{g - 1}_{OT - 2 + rt}") if rt < 2 else None])
        sec("sync", f"qas{pp}")(ldres_a)

        def ymm(E, g=g, rt=rt, pp=pp):
            for it in range(IT):
                T["mmy"] = E.do("matmul", ps_y[pp][:],
                                qtrow[pp][:, it * 128:(it + 1) * 128],
                                qwdTb[:, it * HOG:(it + 1) * HOG],
                                start=(it == 0), stop=(it == IT - 1),
                                deps=list(WDLT[g].values()) + [T[f"qtl{g}_{rt}"],
                                      T.get(f"yev{g}_{rt - 2}"),
                                      T.get(f"yev{g - 1}_{OT - 2 + rt}") if rt < 2 else None])
            T[f"mmy{g}_{rt}"] = T["mmy"]
        sec("tensor")(ymm)

        def yadd(E, g=g, rt=rt, pp=pp):
            a1 = E.do("tensor_tensor", o1[:], ps_y[pp][:], xsl[pp][:],
                      op=ALU.add,
                      deps=[T[f"mmy{g}_{rt}"], T[f"xld{g}_{rt}"],
                            T.get(f"ost{g}_{rt - 1}"),
                            T.get(f"ost{g - 1}_{OT - 1}") if rt == 0 else None])
            T[f"yev{g}_{rt}"] = a1
            T[f"oadd{g}_{rt}"] = E.do("tensor_tensor", o2[:], o1[:],
                                      asl[pp][:], op=ALU.add,
                                      deps=[a1, T[f"ald{g}_{rt}"]])
        sec("vector")(yadd)

        def ost(E, g=g, rt=rt):
            T[f"ost{g}_{rt}"] = E.do(
                "dma_start",
                out_d.ap()[rt * 128:(rt + 1) * 128, g * HOG:(g + 1) * HOG],
                o2[:], deps=[T[f"oadd{g}_{rt}"]])
        sec("gpsimd", "go")(ost)
        if rt == OT - 1:
            T[f"mm2done{g}"] = T[f"mmy{g}_{OT - 1}"]

    emit_2b_wload(0)
    for i in range(GR):
        emit_p3a(GR + i, 0)
        emit_p3a(GR + i, 1)
        emit_2b_rt(0, 2 * i)
        emit_2b_rt(0, 2 * i + 1)

    for g in range(1, NG):
        emit_2b_wload(g)
        for rt in range(OT):
            emit_2b_rt(g, rt)

    def fin(E):
        for g in range(NG):
            for rt in range(OT):
                E.wait(T[f"ost{g}_{rt}"])
    sec("gpsimd", "go")(fin)

    p3.close()
    p3a.close()
    es.close()
    return nc



_CACHED = {}


def _constants(S, NH, par):
    HD = 128
    inv = (np.float32(1.0) /
           np.power(np.float32(10000.0),
                    np.arange(0, HD, 2, dtype=np.float32) / np.float32(HD)))
    pos_perm = np.concatenate(
        [np.arange(S).reshape(-1, 128)[par::2].ravel(),
         np.arange(S).reshape(-1, 128)[1 - par::2].ravel()]).astype(np.int64)
    ang = inv[:, None].astype(np.float32) * pos_perm[None, :].astype(np.float32)
    cosP = np.cos(ang.astype(np.float64)).astype(np.float32).astype(ml_dtypes.bfloat16)
    sinP = np.sin(ang.astype(np.float64)).astype(np.float32).astype(ml_dtypes.bfloat16)

    BL = S + 128
    NB = 32
    y = np.arange(BL)
    d1 = y - 128
    oht1 = np.zeros((NB, BL), dtype=np.float32)
    ok = (d1 >= 0) & (d1 < S)
    b1 = _bucket_f32(np.clip(d1, 0, S - 1))
    oht1[b1[ok], y[ok]] = 1.0
    d2 = y - 128 + (2 * par - 1) * 128
    oht2 = np.zeros((NB, BL), dtype=np.float32)
    ok2 = (d2 >= 0) & (d2 < S)
    b2 = _bucket_f32(np.clip(d2, 0, S - 1))
    oht2[b2[ok2], y[ok2]] = 1.0

    ident = np.eye(128, dtype=np.float32).astype(ml_dtypes.bfloat16)
    cmaskc = np.where(np.arange(128)[None, :] <= np.arange(128)[:, None],
                      np.float32(0.0), np.float32(-1e30)).astype(np.float32)
    fmaskc = (np.zeros((128, 128), np.float32) if par == 1
              else np.full((128, 128), np.float32(-1e30), np.float32))
    return cosP, sinP, oht1, oht2, ident, cmaskc, fmaskc, pos_perm


def kernel(L0_attn_x, L0_attn_qkv_w, L0_attn_dense_w, L0_attn_rel_attn_table,
           L0_ffn_w_gate, L0_ffn_w_up, L0_ffn_w_down, L0_ffn_up_reorder_index,
           L0_ffn_down_reorder_index):
    x = np.ascontiguousarray(np.asarray(L0_attn_x, dtype=np.float32))
    B, S, H = x.shape
    NH = H // 128
    I = L0_ffn_w_gate.shape[0]
    ncore = 8

    down_idx = np.asarray(L0_ffn_down_reorder_index).astype(np.int64)
    wg_p = np.ascontiguousarray(np.asarray(L0_ffn_w_gate, np.float32)[down_idx])
    wu_p = np.ascontiguousarray(np.asarray(L0_ffn_w_up, np.float32)[down_idx])
    wd = np.ascontiguousarray(np.asarray(L0_ffn_w_down, np.float32))
    qkv = np.ascontiguousarray(np.asarray(L0_attn_qkv_w, np.float32))
    dw = np.ascontiguousarray(np.asarray(L0_attn_dense_w, np.float32))
    rel = np.ascontiguousarray(np.asarray(L0_attn_rel_attn_table, np.float32))

    key = (S, H, NH, I)
    if key not in _CACHED:
        _CACHED[key] = build_nc(S, H, NH, I)
    nc = _CACHED[key]

    in_maps = []
    perms = []
    for c in range(ncore):
        b, par = c // 2, c % 2
        cosP, sinP, oht1, oht2, ident, cmaskc, fmaskc, pos_perm = \
            _constants(S, NH, par)
        xb = np.ascontiguousarray(x[b][pos_perm])
        perms.append(pos_perm)
        isl = wg_p.shape[0] // ncore
        wds = wd.shape[0] // ncore
        in_maps.append({
            "xb": xb, "qkv_w": qkv, "dense_w": dw, "rel_table": rel,
            "wg_s": np.ascontiguousarray(wg_p[c * isl:(c + 1) * isl]),
            "wu_s": np.ascontiguousarray(wu_p[c * isl:(c + 1) * isl]),
            "wd_s": np.ascontiguousarray(wd[c * wds:(c + 1) * wds]),
            "cosP": cosP, "sinP": sinP, "oht1": oht1, "oht2": oht2,
            "identb": ident, "cmask": cmaskc, "fmask": fmaskc,
        })

    res = run_bass_kernel_spmd(nc, in_maps, core_ids=list(range(ncore)))

    out = np.empty((B, S, H), dtype=np.float32)
    for c in range(ncore):
        b, par = c // 2, c % 2
        own = perms[c][: S // 2]
        out[b][own] = res.results[c]["out"]
    return out


# version 29
# speedup vs baseline: 1.0444x; 1.0444x over previous
import math
import numpy as np
import ml_dtypes
from contextlib import ExitStack
import concourse.bass as bass
import concourse.mybir as mybir
import bass_rust
from concourse.bass_utils import run_bass_kernel_spmd

F32 = mybir.dt.float32
BF16 = mybir.dt.bfloat16
F8 = mybir.dt.float8e4
U8 = mybir.dt.uint8
AX = mybir.AxisListType
ALU = mybir.AluOpType
ACT = mybir.ActivationFunctionType

FIXC = 12288.0
SUBN2 = float(2.0 ** -12)
SCALE_QK = 0.08838834764831845


class Eng:
    def __init__(self, eng, sem, hwm, cnt, step=1, selfw=False):
        self.e, self.sem, self.step, self.hwm, self.cnt = eng, sem, step, hwm, cnt
        self.selfw = selfw

    def wait(self, *tokens):
        for tok in tokens:
            if tok is None:
                continue
            sem, val = tok
            key = id(sem)
            if self.hwm.get(key, 0) >= val:
                continue
            self.e.wait_ge(sem, val)
            self.hwm[key] = val

    def do(self, method, *args, deps=(), **kw):
        if self.selfw and self.cnt[0] > 0:
            self.wait((self.sem, self.cnt[0]))
        self.wait(*deps)
        instr = getattr(self.e, method)(*args, **kw)
        self.cnt[0] += self.step
        instr.then_inc(self.sem, self.step)
        return (self.sem, self.cnt[0])


def _bucket_f32(d, num_buckets=32, max_distance=128):
    d = np.asarray(d, dtype=np.int64)
    me = num_buckets // 2
    nf = np.maximum(d, 1).astype(np.float32)
    large = (np.log(nf / np.float32(me)) / np.float32(math.log(max_distance / me))
             * np.float32(num_buckets - me)).astype(np.int32) + me
    large = np.minimum(large, num_buckets - 1)
    return np.where(d < me, d.astype(np.int32), large)


def build_nc(S, H, NH, I):
    NB = 32
    RT = S // 128
    OT = RT // 2
    HT = H // 128
    IT = I // 128
    RO = S // 2
    BL = S + 128
    HOG = min(512, H)
    NG = H // HOG
    CHK = min(256, I)
    NCH = (I + CHK - 1) // CHK
    NIC = I // 512
    assert I % 512 == 0 and H % 128 == 0 and S % 512 == 0 and I % CHK == 0

    nc = bass.Bass("TRN2", target_bir_lowering=False)

    x_d = nc.dram_tensor("xb", [S, H], F32, kind="ExternalInput")
    qkv_d = nc.dram_tensor("qkv_w", [H, 3 * H], F32, kind="ExternalInput")
    dense_d = nc.dram_tensor("dense_w", [H, H], F32, kind="ExternalInput")
    rel_d = nc.dram_tensor("rel_table", [NB, NH], F32, kind="ExternalInput")
    ISL = I // 8
    WDS = H // 8
    wg_d = nc.dram_tensor("wg_s", [ISL, H], F32, kind="ExternalInput")
    wu_d = nc.dram_tensor("wu_s", [ISL, H], F32, kind="ExternalInput")
    wd_d = nc.dram_tensor("wd_s", [WDS, I], F32, kind="ExternalInput")
    CSZ_G = HT * 128 * ISL
    CSZ_D = (I // 128) * 128 * WDS
    CSZ = 2 * CSZ_G + CSZ_D
    cb_hbm = nc.dram_tensor("cb_scr", [CSZ], BF16, kind="Internal")
    G_hbm = nc.dram_tensor("G_scr", [8 * CSZ], BF16, kind="Internal")
    cosP_d = nc.dram_tensor("cosP", [64, S], BF16, kind="ExternalInput")
    sinP_d = nc.dram_tensor("sinP", [64, S], BF16, kind="ExternalInput")
    oht1_d = nc.dram_tensor("oht1", [NB, BL], F32, kind="ExternalInput")
    oht2_d = nc.dram_tensor("oht2", [NB, BL], F32, kind="ExternalInput")
    id_d = nc.dram_tensor("identb", [128, 128], BF16, kind="ExternalInput")
    cm_d = nc.dram_tensor("cmask", [128, 128], F32, kind="ExternalInput")
    fm_d = nc.dram_tensor("fmask", [128, 128], F32, kind="ExternalInput")
    out_d = nc.dram_tensor("out", [RO, H], F32, kind="ExternalOutput")
    F1_hbm = nc.dram_tensor("F1_scr", [NH, BL], F32, kind="Internal")
    F2_hbm = nc.dram_tensor("F2_scr", [NH, BL], F32, kind="Internal")
    qxT_hbm = nc.dram_tensor("qxT_scr", [HT * 128, RO], BF16, kind="Internal")
    ctxT_hbm = nc.dram_tensor("ctxT_scr", [NH * 128, RO], BF16, kind="Internal")
    attn_hbm = nc.dram_tensor("attn_scr", [RO, H], F32, kind="Internal")
    qkvb_hbm = nc.dram_tensor("qkvb_scr", [H, 3 * H], BF16, kind="Internal")
    dwb_hbm = nc.dram_tensor("dwb_scr", [H, H], BF16, kind="Internal")
    t_hbm = nc.dram_tensor("t_scr", [RO, I], F32, kind="Internal")
    qt2_hbm = nc.dram_tensor("qt2_scr", [OT * 128, I], BF16, kind="Internal")

    es = ExitStack()
    SB = lambda n, shp, dt=F32: es.enter_context(nc.sbuf_tensor(n, shp, dt))

    sems, cnts = {}, {}
    for nm in ("qc", "qx0", "qx1", "qw", "qctx", "qqxl", "qtld", "qt2l",
               "qw10", "qw11", "qw12", "qw13", "qwd", "qas0", "qas1", "gf", "gt2",
               "gqxt", "gct", "gat", "gt", "go", "gb0", "gb1", "gw", "v",
               "s", "t"):
        sems[nm] = es.enter_context(nc.semaphore(nm + "_sem"))
        cnts[nm] = [0]
    hwm_by_eng = {}

    def mk(engine_obj, eng_key, sem_key, step=1):
        h = hwm_by_eng.setdefault(eng_key, {})
        return Eng(engine_obj, sems[sem_key], h, cnts[sem_key], step)

    block = es.enter_context(nc.Block())
    T = {}

    cur = {}

    SLOT_SEMS = {"qc", "qx0", "qx1", "qw", "qw10", "qw11", "qw12", "qw13",
                 "qwd", "qas0", "qas1", "qt2l"}

    def sec(engname, sem=None):
        dec = getattr(block, engname)
        key = {"sync": ("q", None, 16), "gpsimd": ("g", None, 16),
               "vector": ("v", "v", 1), "scalar": ("s", "s", 1),
               "tensor": ("t", "t", 1)}[engname]
        semname = sem if sem is not None else key[1]
        assert semname is not None, f"DMA section needs explicit sem: {engname}"
        sw = semname in SLOT_SEMS

        def run(emitfn):
            @dec
            def _(engobj):
                h = hwm_by_eng.setdefault(key[0], {})
                emitfn(Eng(engobj, sems[semname], h, cnts[semname], key[2], sw))
        return run

    def barrier():
        snap = [(sems[nm], cnts[nm][0]) for nm in sems if cnts[nm][0] > 0]
        for engname in ("sync", "gpsimd", "vector", "scalar", "tensor"):
            def bwait(E, snap=snap):
                E.wait(*snap)
            sec(engname, "qc" if engname in ("sync", "gpsimd") else None)(bwait)

    cosP = SB("cosP_s", [64, S], BF16)
    sinP = SB("sinP_s", [64, S], BF16)
    identb = SB("identb_s", [128, 128], BF16)
    cmask = SB("cmask_s", [128, 128])
    fmask = SB("fmask_s", [128, 128])
    eps_t = SB("eps_t", [128, 1])
    relsb = SB("relsb", [NB, NH])
    qs_rmax = [SB(f"qs_rmax{a}", [128, 1]) for a in range(2)]
    qs_rinv = [SB(f"qs_rinv{a}", [128, 1]) for a in range(2)]
    qs_shalf = [SB(f"qs_shalf{a}", [128, 1]) for a in range(2)]
    qs_sdq = [SB(f"qs_sdq{a}", [128, 1]) for a in range(2)]
    qset = [0]

    def q_scales(tag, src_ap, deps_src):
        a = qset[0] = 1 - qset[0]
        P = src_ap.shape[0]
        out = {}

        def v1(E):
            rm = E.do("reduce_max", qs_rmax[a][0:P, :], src_ap, axis=AX.X,
                      apply_absolute_value=True,
                      deps=list(deps_src) + [T.get(f"q_rm_war{a}")])
            out["rm"] = rm
            out["ri"] = E.do("reciprocal", qs_rinv[a][0:P, :],
                             qs_rmax[a][0:P, :],
                             deps=[rm, T.get(f"q_ri_war{a}")])
        sec("vector")(v1)

        def s1(E):
            out["sh"] = E.do("activation", qs_shalf[a][0:P, :],
                             qs_rinv[a][0:P, :], ACT.Copy, scale=224.0,
                             deps=[out["ri"], T.get(f"q_sh_war{a}")])
            out["sq"] = E.do("activation", qs_sdq[a][0:P, :],
                             qs_rmax[a][0:P, :], ACT.Copy, scale=1.0 / 224.0,
                             deps=[out["rm"], T.get(f"q_sq_war{a}")])
        sec("scalar")(s1)
        T[f"q_rm_war{a}"] = out["sq"]
        T[f"q_ri_war{a}"] = out["sh"]
        return a, out["sh"], out["sq"]

    def q_apply(tmp, src_ap, dst_ap, sset, dst_war=None):
        a, sh, sq = sset
        b = tmp["set"]
        W = src_ap.shape[-1]
        P = src_ap.shape[0]
        out = {}

        def s1(E):
            w = E.do("activation", tmp["wt"][0:P, 0:W], src_ap, ACT.Copy,
                     scale=qs_shalf[a][0:P, 0:1], deps=[sh, T.get(f"q_w_war{b}")])
            out["c8"] = E.do("copy", tmp["c8t"][0:P, 0:W], tmp["wt"][0:P, 0:W],
                             deps=[w, T.get(f"q_c8_war{b}")])
            out["w"] = w
        sec("scalar")(s1)

        def v2(E):
            fa = E.do("tensor_scalar", tmp["fixa"][0:P, 0:W], tmp["wt"][0:P, 0:W],
                      FIXC, None, op0=ALU.add,
                      deps=[out["w"], T.get(f"q_fa_war{b}")])
            fq = E.do("tensor_scalar", tmp["fixq"][0:P, 0:W],
                      tmp["fixa"][0:P, 0:W],
                      FIXC, qs_sdq[a][0:P, 0:1], op0=ALU.subtract, op1=ALU.mult,
                      deps=[fa, sq, T.get(f"q_fq_war{b}")])
            w2 = E.do("tensor_tensor", tmp["w2t"][0:P, 0:W], tmp["wt"][0:P, 0:W],
                      tmp["wt"][0:P, 0:W], op=ALU.mult,
                      deps=[T.get(f"q_w2_war{b}")])
            mq = E.do("tensor_scalar", tmp["maskt"][0:P, 0:W],
                      tmp["w2t"][0:P, 0:W],
                      SUBN2, None, op0=ALU.is_lt, deps=[w2, T.get(f"q_mq_war{b}")])
            out["qop"] = E.do("tensor_scalar", dst_ap, tmp["c8t"][0:P, 0:W],
                              qs_sdq[a][0:P, 0:1], None, op0=ALU.mult,
                              deps=[out["c8"], sq, dst_war])
            done = E.do("copy_predicated", dst_ap, tmp["maskt"][0:P, 0:W],
                        tmp["fixq"][0:P, 0:W], deps=[mq, fq, out["qop"], dst_war])
            out["done"] = done
            T[f"q_w_war{b}"] = w2
            T[f"q_fa_war{b}"] = fq
            T[f"q_fq_war{b}"] = done
            T[f"q_w2_war{b}"] = mq
            T[f"q_mq_war{b}"] = done
        sec("vector")(v2)
        T[f"q_c8_war{b}"] = out["qop"]
        T[f"q_sh_war{a}"] = out["w"]
        T[f"q_sq_war{a}"] = out["done"]
        return out["done"], out["w"], out["qop"]

    p0 = ExitStack()
    P0 = lambda n, shp, dt=F32: p0.enter_context(nc.sbuf_tensor(n, shp, dt))
    ohtsb = P0("ohtsb", [NB, BL])
    Fsb = P0("Fsb", [NH, BL])
    xs = [P0(f"xs{i}", [128, H]) for i in range(2)]
    xnf = P0("xnf", [128, H])
    CW0 = min(512, H)
    NST = H // CW0
    stats = P0("stats", [128, 6 * NST])
    mvt = P0("mvt", [128, 2])
    sdv = P0("sdv", [128, 1])
    inv = P0("inv", [128, 1])
    CH0 = H // 2
    tmp0 = [{"set": a, "wt": P0(f"t0_wt{a}", [128, CH0]),
             "fixa": P0(f"t0_fa{a}", [128, CH0]),
             "w2t": P0(f"t0_w2{a}", [128, CH0]),
             "maskt": P0(f"t0_mk{a}", [128, CH0], U8),
             "c8t": P0(f"t0_c8{a}", [128, CH0], F8),
             "fixq": P0(f"t0_fq{a}", [128, CH0], BF16)} for a in range(2)]
    xn_bf = [P0(f"xnb{i}", [128, H], BF16) for i in range(2)]
    qx_op = [P0(f"qx{i}", [128, H], BF16) for i in range(2)]
    qxT_f = [P0(f"qxTf{i}", [128, RO], BF16) for i in range(HT)]
    ps_f = p0.enter_context(nc.psum_tensor("ps_f", [NH, 512], F32))
    ps_tp0 = [p0.enter_context(nc.psum_tensor(f"ps_tp0{i}", [128, 128], BF16))
              for i in range(2)]

    xnT_es = ExitStack()
    xnT = [xnT_es.enter_context(nc.sbuf_tensor(f"xnT{i}", [128, S], BF16,
                                               side="right"))
           for i in range(HT)]

    def p0_loads(E):
        E.do("dma_start", cosP[:], cosP_d.ap())
        E.do("dma_start", sinP[:], sinP_d.ap())
        E.do("dma_start", identb[:], id_d.ap())
        E.do("dma_start", cmask[:], cm_d.ap())
        E.do("dma_start", relsb[:], rel_d.ap())
        E.do("dma_start", fmask[:], fm_d.ap())
        tok = E.do("dma_start", ohtsb[:], oht1_d.ap())
        for k in ("cos", "sin", "ident", "cmask", "fmask", "rel", "oht1"):
            T[k] = tok
    sec("sync", "qc")(p0_loads)

    def xload_emit(i):
        def xload(E, i=i):
            T[f"x{i}"] = E.do("dma_start", xs[i % 2][:],
                              x_d.ap()[i * 128:(i + 1) * 128, :],
                              deps=[T.get(f"xfree{i - 2}")])
        sec("sync", f"qx{i % 2}")(xload)

    xload_emit(0)
    xload_emit(1)

    NCHB = (BL + 511) // 512

    def emit_table(tag, oht_tok, dst, war=None):
        for ch in range(NCHB):
            c0 = ch * 512
            cw = min(512, BL - c0)

            def tmm(E, ch=ch, c0=c0, cw=cw):
                T[f"F{tag}mm{ch}"] = E.do(
                    "matmul", ps_f[0:NH, 0:cw], relsb[0:NB, :],
                    ohtsb[:, c0:c0 + cw], start=True, stop=True,
                    deps=[T["rel"], oht_tok, T.get(f"F{tag}ev{ch - 1}"),
                          war if ch == 0 else None])
            sec("tensor")(tmm)

            def tev(E, ch=ch, c0=c0, cw=cw):
                T[f"F{tag}ev{ch}"] = E.do(
                    "tensor_copy", Fsb[:, c0:c0 + cw], ps_f[0:NH, 0:cw],
                    deps=[T[f"F{tag}mm{ch}"],
                          T.get(f"F{'1' if tag == '2' else 'x'}st")])
            sec("vector")(tev)

        def tst(E):
            T[f"F{tag}st"] = E.do("dma_start", dst.ap(), Fsb[:],
                                  deps=[T[f"F{tag}ev{NCHB - 1}"]])
        sec("gpsimd", "gf")(tst)

    emit_table("1", T["oht1"], F1_hbm)

    def oht2_load(E):
        T["oht2"] = E.do("dma_start", ohtsb[:], oht2_d.ap(),
                         deps=[T[f"F1mm{NCHB - 1}"]])
    sec("sync", "qc")(oht2_load)

    emit_table("2", T["oht2"], F2_hbm, war=T[f"F1ev{NCHB - 1}"])

    def eps_set(E):
        T["eps"] = E.do("memset", eps_t[:], 1e-5)
    sec("vector")(eps_set)

    for i in range(RT):
        own = i < OT
        if i >= 1 and i + 1 < RT:
            xload_emit(i + 1)

        def ln_v1(E, i=i):
            t1 = None
            for c in range(NST):
                t1 = E.do("bn_stats", stats[:, 6 * c:6 * c + 6],
                          xs[i % 2][:, CW0 * c:CW0 * (c + 1)],
                          deps=[T[f"x{i}"], T.get(f"mv{i - 1}")] if c == 0 else [t1])
            T[f"mv{i}"] = E.do("bn_aggr", mvt[:], stats[:],
                               deps=[t1, T.get(f"xn{i - 1}")])
        sec("vector")(ln_v1)

        def ln_s1(E, i=i):
            T[f"sd{i}"] = E.do("activation", sdv[:], mvt[:, 1:2], ACT.Sqrt,
                               bias=eps_t[:, 0:1],
                               deps=[T[f"mv{i}"], T["eps"], T.get(f"inv{i - 1}")])
        sec("scalar")(ln_s1)

        def ln_v2(E, i=i):
            T[f"inv{i}"] = E.do("reciprocal", inv[:], sdv[:], deps=[T[f"sd{i}"]])
            T[f"xn{i}"] = E.do("tensor_scalar", xnf[:], xs[i % 2][:], mvt[:, 0:1],
                               inv[:], op0=ALU.subtract, op1=ALU.mult,
                               deps=[T[f"inv{i}"], T.get(f"xnused{i - 1}")])
            T[f"xfree{i}"] = T[f"xn{i}"]
        sec("vector")(ln_v2)

        def ln_s2(E, i=i):
            T[f"xb{i}"] = E.do("activation", xn_bf[i % 2][:], xnf[:], ACT.Copy,
                               deps=[T[f"xn{i}"],
                                     T.get(f"xtp_done{i - 2}")])
        sec("scalar")(ln_s2)

        if own:
            sset = q_scales("p0", xnf[:], deps_src=[T[f"xn{i}"]])
            lastd = lastw = lastq = None
            for ci, c0 in enumerate((0, CH0)):
                d_, w_, q_ = q_apply(tmp0[ci % 2], xnf[:, c0:c0 + CH0],
                                     qx_op[i % 2][:, c0:c0 + CH0], sset,
                                     dst_war=T.get(f"qxtp_done{i - 2}") if c0 == 0 else None)
                lastd, lastw, lastq = d_, w_, q_
            T[f"qfix{i}"] = lastd
            T[f"qfix{i}_s"] = lastq
            T[f"xnused{i}"] = lastw
        else:
            T[f"xnused{i}"] = T[f"xb{i}"]

        for ht in range(HT):
            def xnt_t(E, i=i, ht=ht):
                T[f"xtp{ht}_{i}"] = E.do(
                    "transpose", ps_tp0[ht % 2][:],
                    xn_bf[i % 2][:, ht * 128:(ht + 1) * 128], identb[:],
                    deps=[T[f"xb{i}"], T["ident"],
                          T.get(f"tp0war{ht % 2}")])
            sec("tensor")(xnt_t)

            def xnt_v(E, i=i, ht=ht):
                T[f"xte{ht}_{i}"] = E.do(
                    "tensor_copy", xnT[ht][:, i * 128:(i + 1) * 128],
                    ps_tp0[ht % 2][:], deps=[T[f"xtp{ht}_{i}"]])
                T[f"tp0war{ht % 2}"] = T[f"xte{ht}_{i}"]
            sec("vector")(xnt_v)
        T[f"xtp_done{i}"] = T[f"xtp{HT - 1}_{i}"]

        if own:
            for ht in range(HT):
                def qxt_t(E, i=i, ht=ht):
                    T[f"qxtp{ht}_{i}"] = E.do(
                        "transpose", ps_tp0[ht % 2][:],
                        qx_op[i % 2][:, ht * 128:(ht + 1) * 128], identb[:],
                        deps=[T[f"qfix{i}"], T[f"qfix{i}_s"],
                              T.get(f"tp0war{ht % 2}")])
                sec("tensor")(qxt_t)

                def qxt_v(E, i=i, ht=ht):
                    T[f"qxte{ht}_{i}"] = E.do(
                        "tensor_copy", qxT_f[ht][:, i * 128:(i + 1) * 128],
                        ps_tp0[ht % 2][:], deps=[T[f"qxtp{ht}_{i}"]])
                    T[f"tp0war{ht % 2}"] = T[f"qxte{ht}_{i}"]
                sec("vector")(qxt_v)
            T[f"qxtp_done{i}"] = T[f"qxtp{HT - 1}_{i}"]

    for ht in range(HT):
        def qxt_g(E, ht=ht):
            T[f"qxst{ht}"] = E.do("dma_start",
                                  qxT_hbm.ap()[ht * 128:(ht + 1) * 128, :],
                                  qxT_f[ht][:],
                                  deps=[T[f"qxte{ht}_{OT - 1}"]])
        sec("gpsimd", "gqxt")(qxt_g)
    T["qxst_done"] = T[f"qxst{HT - 1}"]

    T["xnT_done"] = T[f"xte{HT - 1}_{RT - 1}"]
    p0.close()
    barrier()

    pg = ExitStack()
    PG = lambda n, shp, dt=F32: pg.enter_context(nc.sbuf_tensor(n, shp, dt))
    dpst = [pg.enter_context(nc.sbuf_tensor(f"dpst{a}", [128, H], F32))
            for a in range(2)]
    dpsb = [pg.enter_context(nc.sbuf_tensor(f"dpsb{a}", [128, H], BF16))
            for a in range(2)]
    pgst = [PG(f"pgst{a}", [128, max(H, I)]) for a in range(2)]
    pgq = PG("pgq", [128, max(H, I)], BF16)
    tmpg = [{"set": a, "wt": PG(f"tg_wt{a}", [128, CHK]),
             "fixa": PG(f"tg_fa{a}", [128, CHK]),
             "w2t": PG(f"tg_w2{a}", [128, CHK]),
             "maskt": PG(f"tg_mk{a}", [128, CHK], U8),
             "c8t": PG(f"tg_c8{a}", [128, CHK], F8),
             "fixq": PG(f"tg_fq{a}", [128, CHK], BF16)} for a in range(2)]
    cbF = PG("cbF", [128, max(HT * ISL, (I // 128) * WDS)], BF16)
    ps_tpg = [pg.enter_context(nc.psum_tensor(f"ps_tpg{a}", [128, 128], BF16))
              for a in range(2)]
    cc_sem = es.enter_context(nc.semaphore("cc_sem"))

    def emit_slice_prep(key, w_in, rows, rowlen, nblk, off_cb, prev_done,
                        prev_cb=None):
        ntile = (rows + 127) // 128
        for st in range(ntile):
            P = min(128, rows - st * 128)

            def sload(E, st=st, P=P, key=key, w_in=w_in, rowlen=rowlen):
                T[f"pgl{key}{st}"] = E.do(
                    "dma_start", pgst[st % 2][0:P, 0:rowlen],
                    w_in.ap()[st * 128:st * 128 + P, :],
                    deps=[T.get(f"pgw{key}{st - 2}")] +
                         (list(prev_done) if st < 2 else []))
            sec("sync", f"qw1{st % 2}")(sload)

            sset = q_scales(f"pg{key}", pgst[st % 2][0:P, 0:rowlen],
                            deps_src=[T[f"pgl{key}{st}"]])
            lastd = lastw = lastq = None
            for ch in range(rowlen // CHK):
                c0 = ch * CHK
                lastd, lastw, lastq = q_apply(
                    tmpg[ch % 2], pgst[st % 2][0:P, c0:c0 + CHK],
                    pgq[0:P, c0:c0 + CHK], sset,
                    dst_war=T.get(f"pgtp{key}{st - 1}") if ch == 0 else None)
            T[f"pgw{key}{st}"] = lastw
            T[f"pgq{key}{st}"] = lastd
            T[f"pgq{key}{st}_s"] = lastq

            for bt in range(nblk):
                def ptp(E, st=st, P=P, bt=bt, key=key, nblk=nblk):
                    T[f"pgt{key}{st}_{bt}"] = E.do(
                        "transpose", ps_tpg[bt % 2][:, 0:P],
                        pgq[0:P, bt * 128:(bt + 1) * 128], identb[0:P, 0:P],
                        deps=[T[f"pgq{key}{st}"], T[f"pgq{key}{st}_s"],
                              T.get(f"pge{key}{st}_{bt - 2}"),
                              T.get(f"pge{key}{st - 1}_{nblk - 2 + (bt % 2)}")])
                sec("tensor")(ptp)

                def pev(E, st=st, P=P, bt=bt, key=key, rows=rows):
                    T[f"pge{key}{st}_{bt}"] = E.do(
                        "tensor_copy",
                        cbF[:, bt * rows + st * 128: bt * rows + st * 128 + P],
                        ps_tpg[bt % 2][:, 0:P],
                        deps=[T[f"pgt{key}{st}_{bt}"],
                              prev_cb if (st == 0 and bt == 0) else None])
                sec("vector")(pev)
            T[f"pgtp{key}{st}"] = T[f"pgt{key}{st}_{nblk - 1}"]

        def cbst(E, key=key, rows=rows, nblk=nblk, off_cb=off_cb, ntile=ntile):
            dst = bass_rust.AP(cb_hbm, off_cb,
                               [[rows, 128], [128 * rows, nblk], [1, rows]])
            W = max(HT * ISL, (I // 128) * WDS)
            srcap = bass_rust.AP(cbF, 0, [[W, 128], [rows, nblk], [1, rows]])
            T[f"cbst{key}"] = E.do("dma_start", dst, srcap,
                                   deps=[T[f"pge{key}{ntile - 1}_{nblk - 1}"]])
        sec("gpsimd", "gw")(cbst)
        return T[f"cbst{key}"], T[f"pgw{key}{ntile - 1}"], T[f"pgq{key}{ntile - 1}"]

    dg = emit_slice_prep("g", wg_d, ISL, H, HT, 0, [])
    du = emit_slice_prep("u", wu_d, ISL, H, HT, CSZ_G, dg[1:], prev_cb=dg[0])
    dd_ = emit_slice_prep("d", wd_d, WDS, I, I // 128, 2 * CSZ_G, du[1:],
                          prev_cb=du[0])

    def ccgo(E):
        E.wait(dg[0], du[0], dd_[0])
        instr = E.e.collective_compute(
            "AllGather", ALU.bypass, replica_groups=[list(range(8))],
            ins=[cb_hbm.ap().opt()], outs=[G_hbm.ap().opt()])
        instr.then_inc(cc_sem, 1)
    sec("gpsimd", "gw")(ccgo)
    T["ccdone"] = (cc_sem, 1)

    for i in range(HT):
        def dcl(E, i=i):
            T[f"dcl{i}"] = E.do("dma_start", dpst[i % 2][:],
                                dense_d.ap()[i * 128:(i + 1) * 128, :],
                                deps=[T.get(f"dcc{i - 2}")])
        sec("sync", f"qx{i % 2}")(dcl)

        def dcc(E, i=i):
            T[f"dcc{i}"] = E.do("activation", dpsb[i % 2][:], dpst[i % 2][:],
                                ACT.Copy, deps=[T[f"dcl{i}"],
                                                T.get(f"dcs{i - 2}")])
        sec("scalar")(dcc)

        def dcs(E, i=i):
            T[f"dcs{i}"] = E.do("dma_start",
                                dwb_hbm.ap()[i * 128:(i + 1) * 128, :],
                                dpsb[i % 2][:], deps=[T[f"dcc{i}"]])
        sec("gpsimd", "gw" if i % 2 == 0 else "gf")(dcs)
    T["dprec"] = T[f"dcs{HT - 1}"]
    T["dprec2"] = T[f"dcs{HT - 2}"]

    pg.close()
    barrier()

    p1 = ExitStack()
    P1 = lambda n, shp, dt=F32: p1.enter_context(nc.sbuf_tensor(n, shp, dt))
    wq_b = [P1(f"wqb{i}", [128, H], BF16) for i in range(2)]
    wk_b = [P1(f"wkb{i}", [128, H], BF16) for i in range(2)]
    wv_b = [P1(f"wvb{i}", [128, H], BF16) for i in range(2)]
    wstg = [P1("wstg0", [128, H])]
    qb_t = [P1(f"qb_t{a}", [64, RO], BF16) for a in range(2)]
    qb_b = [P1(f"qb_b{a}", [64, RO], BF16) for a in range(2)]
    kb_t = [P1(f"kb_t{a}", [64, S], BF16) for a in range(2)]
    kb_b = [P1(f"kb_b{a}", [64, S], BF16) for a in range(2)]
    vS = P1("vS", [128, S], BF16)
    pT = [P1(f"pT{i}", [128, RO], BF16) for i in range(RT)]
    bias_sb = [P1(f"bias_sb{a}", [128, S], BF16) for a in range(2)]
    hank_sb = P1("hank_sb", [128, S], F32)
    s1 = P1("s1", [128, S], F32)
    esb = P1("esb", [128, S], BF16)
    pbuf = P1("pbuf", [128, S], BF16)
    vTs = pbuf
    rowsum = P1("rowsum", [128, 1], F32)
    rs_acc = P1("rs_acc", [128, 1], F32)
    rtmp1f = P1("rtmp1f", [64, S])
    rtmp = [s1[0:64, :], rtmp1f[:]]
    ctxT_b = P1("ctxT_b", [128, RO], BF16)
    ps_a = p1.enter_context(nc.psum_tensor("ps_a", [128, S], F32))
    ps_sc = p1.enter_context(nc.psum_tensor("ps_sc", [128, S], F32))
    if S <= 1920:
        ps_tp1h = p1.enter_context(nc.psum_tensor("ps_tp1", [128, 128], BF16))
        ps_tp1i = p1.enter_context(nc.psum_tensor("ps_tp1b", [128, 128], BF16))
        tp1s = [ps_tp1h[:], ps_tp1i[:]]
    else:
        tp1s = [ps_sc[:, 3 * S // 4 - 64:3 * S // 4].bitcast(BF16),
                ps_sc[:, S - 64:S].bitcast(BF16)]

    for h in range(NH):
        hb = h % 2

        for ki, (key, base, wb) in enumerate((("wq", 0, wq_b), ("wk", H, wk_b),
                                              ("wv", 2 * H, wv_b))):
            j = 3 * h + ki

            def wload(E, h=h, hb=hb, key=key, base=base, j=j):
                srcap = bass_rust.AP(qkv_d, base + h * 128,
                                     [[3 * H, 128], [128 * 3 * H, HT], [1, 128]])
                dst = bass_rust.AP(wstg[0], 0, [[H, 128], [128, HT], [1, 128]])
                T[f"{key}l{h}"] = E.do("dma_start", dst, srcap,
                                       deps=[T.get(f"wstgu{j - 1}")])
            sec("sync", "qw")(wload)

            def wcast(E, h=h, hb=hb, key=key, wb=wb, j=j):
                T[f"{key}c{h}"] = E.do("tensor_copy", wb[hb][:],
                                       wstg[0][:],
                                       deps=[T[f"{key}l{h}"],
                                             T.get(f"{key}used{h - 2}")])
                T[f"wstgu{j}"] = T[f"{key}c{h}"]
            sec("vector")(wcast)

        def hankload(E, h=h, hb=hb):
            for j in range(OT):
                off = h * BL + (OT - 1 - j) * 256 + 1
                E.do("dma_start", hank_sb[:, j * 128:(j + 1) * 128],
                     bass_rust.AP(F1_hbm, off, [[1, 128], [1, 128]]),
                     deps=[T["F2st"], T.get(f"hkrev{h - 1}")] if j == 0 else [])
                T[f"hkall{h}"] = E.do(
                    "dma_start",
                    hank_sb[:, (OT + j) * 128:(OT + j + 1) * 128],
                    bass_rust.AP(F2_hbm, off, [[1, 128], [1, 128]]))
        sec("gpsimd", f"gb{h % 2}")(hankload)

        def hankrev(E, h=h, hb=hb):
            last = None
            for j in range(2 * OT):
                last = E.do(
                    "tensor_copy", bias_sb[hb][:, j * 128:(j + 1) * 128],
                    hank_sb[:, j * 128:(j + 1) * 128][:, ::-1],
                    deps=[T[f"hkall{h}"],
                          T.get(f"bused{hb}") if j == 0 else None])
            T[f"biasb{h}"] = last
            T[f"hkrev{h}"] = last
        sec("vector")(hankrev)

        def qmm(E, h=h, hb=hb):
            for half in (0, 64):
                for ht in range(HT):
                    for c0 in range(0, RO, 512):
                        cw = min(512, RO - c0)
                        base = (half // 64) * RO
                        T["mmq"] = E.do(
                            "matmul", ps_a[0:64, base + c0:base + c0 + cw],
                            wq_b[hb][:, ht * 128 + half:ht * 128 + half + 64],
                            xnT[ht][:, c0:c0 + cw],
                            start=(ht == 0), stop=(ht == HT - 1),
                            deps=[T[f"wqc{h}"], T["xnT_done"],
                                  T.get(f"vTev{h - 1}"),
                                  T.get(f"ropeq{h - 1}")])
            T[f"mmq{h}"] = T["mmq"]
        sec("tensor")(qmm)

        def ropeq(E, h=h, hb=hb):
            A, B = ps_a[0:64, 0:RO], ps_a[0:64, RO:2 * RO]
            co, si = cosP[:, 0:RO], sinP[:, 0:RO]
            dd = [T[f"mmq{h}"], T["cos"], T["sin"], T.get(f"qbused{h - 2}"),
                  T.get(f"rtmpused{h - 1}")]
            t1 = E.do("tensor_tensor", rtmp[0][:, 0:RO], A, co, op=ALU.mult,
                      deps=dd)
            t2 = E.do("tensor_tensor", rtmp[1][:, 0:RO], B, si, op=ALU.mult)
            t3 = E.do("tensor_tensor", qb_t[hb][:], rtmp[0][:, 0:RO],
                      rtmp[1][:, 0:RO], op=ALU.subtract, deps=[t1, t2])
            t4 = E.do("tensor_tensor", rtmp[0][:, 0:RO], B, co, op=ALU.mult,
                      deps=[t3])
            t5 = E.do("tensor_tensor", rtmp[1][:, 0:RO], A, si, op=ALU.mult,
                      deps=[t3])
            T[f"ropeq{h}"] = E.do("tensor_tensor", qb_b[hb][:],
                                  rtmp[0][:, 0:RO], rtmp[1][:, 0:RO],
                                  op=ALU.add, deps=[t4, t5])
        sec("vector")(ropeq)

        HS = S // 2
        for kh in range(2):
            cbase = kh * HS

            def kmm(E, h=h, hb=hb, kh=kh, cbase=cbase, HS=HS):
                for half in (0, 64):
                    for ht in range(HT):
                        for c0 in range(0, HS, 512):
                            cw = min(512, HS - c0)
                            base = (half // 64) * HS
                            T["mmk"] = E.do(
                                "matmul",
                                ps_a[0:64, base + c0:base + c0 + cw],
                                wk_b[hb][:, ht * 128 + half:ht * 128 + half + 64],
                                xnT[ht][:, cbase + c0:cbase + c0 + cw],
                                start=(ht == 0), stop=(ht == HT - 1),
                                deps=[T[f"wkc{h}"], T[f"ropeq{h}"],
                                      T.get(f"ropek{h}_{kh - 1}")])
                T[f"mmk{h}_{kh}"] = T["mmk"]
            sec("tensor")(kmm)

            def ropek(E, h=h, hb=hb, kh=kh, cbase=cbase, HS=HS):
                A, B = ps_a[0:64, 0:HS], ps_a[0:64, HS:2 * HS]
                co = cosP[:, cbase:cbase + HS]
                si = sinP[:, cbase:cbase + HS]
                dd = [T[f"mmk{h}_{kh}"], T.get(f"kbused{h - 2}")]
                t1 = E.do("tensor_tensor", rtmp[0][:, 0:HS], A, co,
                          op=ALU.mult, deps=dd)
                t2 = E.do("tensor_tensor", rtmp[1][:, 0:HS], B, si,
                          op=ALU.mult)
                t3 = E.do("tensor_tensor", kb_t[hb][:, cbase:cbase + HS],
                          rtmp[0][:, 0:HS], rtmp[1][:, 0:HS],
                          op=ALU.subtract, deps=[t1, t2])
                t4 = E.do("tensor_tensor", rtmp[0][:, 0:HS], B, co,
                          op=ALU.mult, deps=[t3])
                t5 = E.do("tensor_tensor", rtmp[1][:, 0:HS], A, si,
                          op=ALU.mult, deps=[t3])
                T[f"ropek{h}_{kh}"] = E.do(
                    "tensor_tensor", kb_b[hb][:, cbase:cbase + HS],
                    rtmp[0][:, 0:HS], rtmp[1][:, 0:HS], op=ALU.add,
                    deps=[t4, t5])
            sec("vector")(ropek)
        T[f"ropek{h}"] = T[f"ropek{h}_1"]
        T[f"rtmpused{h}"] = T[f"ropek{h}"]

        def vmm(E, h=h, hb=hb):
            for ht in range(HT):
                for c0 in range(0, S, 512):
                    T["mmv"] = E.do("matmul", ps_a[:, c0:c0 + 512],
                                    wv_b[hb][:, ht * 128:(ht + 1) * 128],
                                    xnT[ht][:, c0:c0 + 512],
                                    start=(ht == 0), stop=(ht == HT - 1),
                                    deps=[T[f"wvc{h}"], T[f"ropek{h}"]])
            T[f"mmv{h}"] = T["mmv"]
            T[f"wqused{h}"] = T[f"mmq{h}"]
            T[f"wkused{h}"] = T[f"mmk{h}_1"]
            T[f"wvused{h}"] = T[f"mmv{h}"]
        sec("tensor")(vmm)

        def vev0(E, h=h):
            T[f"vTev{h}"] = E.do("tensor_copy", vTs[:], ps_a[:, 0:S],
                                 deps=[T[f"mmv{h}"], T.get(f"vTused{h - 1}"),
                                       T.get(f"pevlast{h - 1}")])
        sec("vector")(vev0)

        for kt in range(RT):
            def vtp_t(E, h=h, kt=kt):
                T[f"vtp{h}_{kt}"] = E.do("transpose", tp1s[kt % 2],
                                         vTs[:, kt * 128:(kt + 1) * 128],
                                         identb[:],
                                         deps=[T[f"vTev{h}"],
                                               T.get(f"vev{h}_{kt - 2}"),
                                               T.get(f"pevlast{h - 1}") if kt < 2 else None,
                                               T.get(f"ctev{h - 1}") if kt < 2 else None])
            sec("tensor")(vtp_t)

            def vtp_v(E, h=h, kt=kt):
                T[f"vev{h}_{kt}"] = E.do("tensor_copy",
                                         vS[:, kt * 128:(kt + 1) * 128],
                                         tp1s[kt % 2],
                                         deps=[T[f"vtp{h}_{kt}"],
                                               T.get(f"vSused{h - 1}")])
            sec("vector")(vtp_v)
        T[f"vSdone{h}"] = T[f"vev{h}_{RT - 1}"]
        T[f"vTused{h}"] = T[f"vtp{h}_{RT - 1}"]

        for n in range(OT):
            On = (n + 1) * 128
            Kn = 2 * On

            def smm(E, h=h, n=n, On=On, Kn=Kn):
                dd = [T[f"ropek{h}"], T[f"ropeq{h}"], T.get(f"s1ev{h}_{n - 1}"),
                      T.get(f"pevlast{h}_{n - 1}"), T[f"vSdone{h}"],
                      T.get(f"ctev{h - 1}") if n == 0 else None]
                qs_ = n * 128
                for c0 in range(0, On, 512):
                    cw = min(512, On - c0)
                    T["mmsc"] = E.do("matmul", ps_sc[:, c0:c0 + cw],
                                     qb_t[h % 2][:, qs_:qs_ + 128],
                                     kb_t[h % 2][:, c0:c0 + cw], start=True,
                                     stop=False, deps=dd)
                    T["mmsc"] = E.do("matmul", ps_sc[:, c0:c0 + cw],
                                     qb_b[h % 2][:, qs_:qs_ + 128],
                                     kb_b[h % 2][:, c0:c0 + cw], start=False,
                                     stop=True)
                c0 = On
                while c0 < Kn:
                    cw = min((c0 // 512 + 1) * 512, Kn) - c0
                    kc = RO + (c0 - On)
                    T["mmsc"] = E.do("matmul", ps_sc[:, c0:c0 + cw],
                                     qb_t[h % 2][:, qs_:qs_ + 128],
                                     kb_t[h % 2][:, kc:kc + cw], start=True,
                                     stop=False, deps=dd)
                    T["mmsc"] = E.do("matmul", ps_sc[:, c0:c0 + cw],
                                     qb_b[h % 2][:, qs_:qs_ + 128],
                                     kb_b[h % 2][:, kc:kc + cw], start=False,
                                     stop=True)
                    c0 += cw
                T[f"mmsc{h}_{n}"] = T["mmsc"]
            sec("tensor")(smm)

            def sfx(E, h=h, n=n, On=On, Kn=Kn):
                bo = (OT - 1 - n) * 128
                t1 = E.do("scalar_tensor_tensor", s1[:, 0:On], ps_sc[:, 0:On],
                          SCALE_QK, bias_sb[h % 2][:, bo:bo + On], op0=ALU.mult,
                          op1=ALU.add,
                          deps=[T[f"mmsc{h}_{n}"], T[f"biasb{h}"],
                                T.get(f"s1used{h}_{n - 1}"),
                                T.get(f"s1used{h - 1}_{OT - 1}") if n == 0 else None])
                t1 = E.do("scalar_tensor_tensor", s1[:, On:Kn],
                          ps_sc[:, On:Kn], SCALE_QK,
                          bias_sb[h % 2][:, OT * 128 + bo:OT * 128 + bo + On],
                          op0=ALU.mult, op1=ALU.add)
                T[f"bused{h % 2}"] = t1
                T[f"s1ev{h}_{n}"] = t1
                t2 = E.do("tensor_tensor", s1[:, On - 128:On], s1[:, On - 128:On],
                          cmask[:], op=ALU.add, deps=[t1, T["cmask"]])
                t3 = E.do("tensor_tensor", s1[:, Kn - 128:Kn], s1[:, Kn - 128:Kn],
                          fmask[:], op=ALU.add, deps=[t1, T["fmask"]])
                T[f"nm{h}_{n}"] = E.do("reduce_max", rowsum[:], s1[:, 0:Kn],
                                       axis=AX.X, negate=True,
                                       deps=[t2, t3, T.get(f"rsused{h}_{n - 1}"),
                                             T.get(f"rsused{h - 1}_{OT - 1}") if n == 0 else None])
            sec("vector")(sfx)

            def sexp(E, h=h, n=n, Kn=Kn):
                T[f"exp{h}_{n}"] = E.do("activation", esb[:, 0:Kn], s1[:, 0:Kn],
                                        ACT.Exp, bias=rowsum[:, 0:1],
                                        accum_out=rs_acc[:, 0:1],
                                        deps=[T[f"nm{h}_{n}"],
                                              T.get(f"pbb{h}_{n - 1}"),
                                              T.get(f"pbb{h - 1}_{OT - 1}") if n == 0 else None])
                T[f"s1used{h}_{n}"] = T[f"exp{h}_{n}"]
            sec("scalar")(sexp)

            def srec(E, h=h, n=n):
                T[f"rsi{h}_{n}"] = E.do("reciprocal", rowsum[:], rs_acc[:],
                                        deps=[T[f"exp{h}_{n}"]])
            sec("vector")(srec)

            def spb(E, h=h, n=n, Kn=Kn):
                T[f"pbb{h}_{n}"] = E.do("tensor_scalar", pbuf[:, 0:Kn],
                                        esb[:, 0:Kn], rowsum[:, 0:1], None,
                                        op0=ALU.mult,
                                        deps=[T[f"rsi{h}_{n}"],
                                              T.get(f"pbused{h}_{n - 1}"),
                                              T[f"vTused{h}"] if n == 0 else None,
                                              T.get(f"pbused{h - 1}_{OT - 1}") if n == 0 else None])
                T[f"rsused{h}_{n}"] = T[f"pbb{h}_{n}"]
            sec("vector")(spb)

            for ct in range(Kn // 128):
                def ptp_t(E, h=h, n=n, ct=ct):
                    T[f"ptp{h}_{n}_{ct}"] = E.do(
                        "transpose", tp1s[ct % 2],
                        pbuf[:, ct * 128:(ct + 1) * 128], identb[:],
                        deps=[T[f"pbb{h}_{n}"], T.get(f"pev{h}_{n}_{ct - 2}"),
                              T.get(f"pevlast{h}_{n - 1}") if ct < 2 else None,
                              T[f"vSdone{h}"] if ct < 2 else None])
                sec("tensor")(ptp_t)

                def ptp_v(E, h=h, n=n, ct=ct):
                    kt = ct if ct <= n else OT + (ct - (n + 1))
                    T[f"pev{h}_{n}_{ct}"] = E.do(
                        "tensor_copy", pT[kt][:, n * 128:(n + 1) * 128],
                        tp1s[ct % 2],
                        deps=[T[f"ptp{h}_{n}_{ct}"],
                              T.get(f"pTused{h - 1}") if (ct == 0 and n == 0) else None])
                sec("vector")(ptp_v)
            T[f"pevlast{h}_{n}"] = T[f"pev{h}_{n}_{Kn // 128 - 1}"]
            T[f"pbused{h}_{n}"] = T[f"pevlast{h}_{n}"]

        T[f"pevlast{h}"] = T[f"pevlast{h}_{OT - 1}"]

        def avmm(E, h=h):
            nkt = 2 * OT
            last = None
            for kt in range(nkt):
                qs = kt if kt < OT else kt - OT
                deps = [T[f"vSdone{h}"], T[f"ropek{h}"], T.get(f"ctev{h - 1}")]
                for n in range(qs, OT):
                    ct = kt if kt < OT else (n + 1) + (kt - OT)
                    deps.append(T[f"pev{h}_{n}_{ct}"])
                if kt == 0:
                    deps.append(T[f"s1ev{h}_{OT - 1}"])
                for c0 in range(0, RO, 512):
                    a = max(c0, qs * 128)
                    b = min(c0 + 512, RO)
                    if a >= b:
                        continue
                    last = E.do("matmul", ps_sc[:, a:b],
                                vS[:, kt * 128:(kt + 1) * 128], pT[kt][:, a:b],
                                start=(kt == 0),
                                stop=(kt == nkt - 1 and b == RO), deps=deps)
            T[f"mmav{h}"] = last
            T[f"vSused{h}"] = last
            T[f"pTused{h}"] = last
        sec("tensor")(avmm)

        def ctev(E, h=h):
            T[f"ctev{h}"] = E.do("tensor_copy", ctxT_b[:], ps_sc[:, 0:RO],
                                 deps=[T[f"mmav{h}"], T.get(f"ctst{h - 1}")])
            T[f"qbused{h}"] = T[f"mmsc{h}_{OT - 1}"]
            T[f"kbused{h}"] = T[f"mmsc{h}_{OT - 1}"]
        sec("vector")(ctev)

        def ctst(E, h=h):
            T[f"ctst{h}"] = E.do("dma_start",
                                 ctxT_hbm.ap()[h * 128:(h + 1) * 128, :],
                                 ctxT_b[:], deps=[T[f"ctev{h}"]])
        sec("gpsimd", "gct")(ctst)

    p1.close()
    xnT_es.close()
    barrier()

    pd = ExitStack()
    PD = lambda n, shp, dt=F32: pd.enter_context(nc.sbuf_tensor(n, shp, dt))
    wd_b = [PD(f"wdb{i}", [128, H], BF16) for i in range(HT)]
    ctx_sb = [PD(f"ctxs{i}", [128, RO], BF16) for i in range(HT)]
    att_f = PD("att_f", [128, H])
    ps_d = pd.enter_context(nc.psum_tensor("ps_d", [128, H], F32))

    def wdload_d(E):
        for ct in range(HT):
            T[f"wdc{ct}"] = E.do("dma_start", wd_b[ct][:],
                                 dwb_hbm.ap()[ct * 128:(ct + 1) * 128, :],
                                 deps=[T["dprec"], T["dprec2"]] if ct == 0 else [])
    sec("sync", "qctx")(wdload_d)

    def dctx(E):
        for hh in range(HT):
            T[f"ctl{hh}"] = E.do("dma_start", ctx_sb[hh][:],
                                 ctxT_hbm.ap()[hh * 128:(hh + 1) * 128, :],
                                 deps=[T[f"ctst{NH - 1}"]])
        T["ctl_all"] = T[f"ctl{HT - 1}"]
    sec("sync", "qctx")(dctx)

    for rt in range(OT):
        def dmm(E, rt=rt):
            for ct in range(HT):
                for c0 in range(0, H, 512):
                    cw = min(512, H - c0)
                    T["mmd"] = E.do("matmul", ps_d[:, c0:c0 + cw],
                                    ctx_sb[ct][:, rt * 128:(rt + 1) * 128],
                                    wd_b[ct][:, c0:c0 + cw],
                                    start=(ct == 0), stop=(ct == HT - 1),
                                    deps=[T["ctl_all"],
                                          T.get(f"attev{rt - 1}")])
            T[f"mmd{rt}"] = T["mmd"]
        sec("tensor")(dmm)

        def dev(E, rt=rt):
            T[f"attev{rt}"] = E.do("tensor_copy", att_f[:], ps_d[:, 0:H],
                                   deps=[T[f"mmd{rt}"], T.get(f"attst{rt - 1}")])
        sec("vector")(dev)

        def dst(E, rt=rt):
            T[f"attst{rt}"] = E.do("dma_start",
                                   attn_hbm.ap()[rt * 128:(rt + 1) * 128, :],
                                   att_f[:], deps=[T[f"attev{rt}"]])
        sec("gpsimd", "gat")(dst)

    pd.close()
    barrier()

    GR = OT // 2
    p3a = ExitStack()
    PA = lambda n, shp, dt=F32: p3a.enter_context(
        nc.sbuf_tensor(n, shp, dt, side="right"))
    tst3 = [PA(f"tst3{i}", [128, I]) for i in range(2)]
    qtr = PA("qtr", [128, I], BF16)
    qtt = PA("qtt", [128, I], BF16)
    tmpa = [{"set": a, "wt": PA(f"ta_wt{a}", [128, CHK]),
             "fixa": PA(f"ta_fa{a}", [128, CHK]),
             "w2t": PA(f"ta_w2{a}", [128, CHK]),
             "maskt": PA(f"ta_mk{a}", [128, CHK], U8),
             "c8t": PA(f"ta_c8{a}", [128, CHK], F8),
             "fixq": PA(f"ta_fq{a}", [128, CHK], BF16)} for a in range(2)]
    ps_tpa = [p3a.enter_context(nc.psum_tensor(f"ps_tpa{i}", [128, 128], BF16))
              for i in range(2)]

    p2 = ExitStack()
    P2 = lambda n, shp, dt=F32: p2.enter_context(nc.sbuf_tensor(n, shp, dt))
    qxT_sb = [P2(f"qxTs{i}", [128, RO], BF16) for i in range(HT)]
    qwTgb = [P2(f"qwTgb{a}", [128, HT * 512], BF16) for a in range(2)]
    qwTub = [P2(f"qwTub{a}", [128, HT * 512], BF16) for a in range(2)]
    S1WL = {}
    sg_sb = [P2(f"sg_sb{i}", [128, 512]) for i in range(2)]
    tch = [P2(f"tch{i}", [128, 512], F32) for i in range(2)]
    ps_g = [p2.enter_context(nc.psum_tensor(f"ps_g{i}", [128, 512], F32))
            for i in range(2)]
    ps_u = [p2.enter_context(nc.psum_tensor(f"ps_u{i}", [128, 512], F32))
            for i in range(2)]

    def qxl(E):
        for ht in range(HT):
            T[f"qxl{ht}"] = E.do("dma_start", qxT_sb[ht][:],
                                 qxT_hbm.ap()[ht * 128:(ht + 1) * 128, :],
                                 deps=[T["qxst_done"]])
        T["qxl_done"] = T[f"qxl{HT - 1}"]
    sec("sync", "qqxl")(qxl)

    def _gpieces(w0):
        out = []
        o = 0
        w = w0
        while w < w0 + 512:
            c = w // ISL
            j0 = w - c * ISL
            ln = min(ISL - j0, w0 + 512 - w)
            out.append((c, j0, ln, o))
            o += ln
            w += ln
        return out

    def _s1prev(gi, r, k):
        s = gi * GR + r - k
        if s < 0:
            return None
        return divmod(s, GR)

    def emit_s1_ic(ps, ic, mid=None, end=None):
        gi = ps * NIC + ic
        buf = gi % 2
        pieces = _gpieces(ic * 512)
        for gu, off_cb, qwB, sp in (("g", 0, qwTgb, ("qw10", "qw12")),
                                    ("u", CSZ_G, qwTub, ("qw11", "qw13"))):
            S1WL[(gi, gu)] = {}
            for pi, (c, j0, ln, o) in enumerate(pieces):
                def wTl(E, gi=gi, buf=buf, off_cb=off_cb, qwB=qwB,
                        c=c, j0=j0, ln=ln, o=o):
                    srcap = bass_rust.AP(G_hbm, c * CSZ + off_cb + j0,
                                         [[ISL, 128], [128 * ISL, HT], [1, ln]])
                    dst = bass_rust.AP(qwB[buf], o,
                                       [[HT * 512, 128], [512, HT], [1, ln]])
                    tok = E.do("dma_start", dst, srcap,
                               deps=[T["ccdone"], T.get(f"mm1done{gi - 2}")])
                    S1WL[(gi, gu)][id(tok[0])] = tok
                sec("sync", sp[pi % 2])(wTl)

        for r in range(GR):
            rt = ps * GR + r
            pp = (gi * GR + r) % 2
            pv2 = _s1prev(gi, r, 2)
            pv1 = _s1prev(gi, r, 1)

            def s1mm(E, gi=gi, rt=rt, r=r, pp=pp, buf=buf, pv2=pv2):
                for ht in range(HT):
                    dd = (list(S1WL[(gi, "g")].values()) +
                          list(S1WL[(gi, "u")].values()) + [T["qxl_done"]])
                    if pv2 is not None:
                        dd += [T[f"silu{pv2[0]}_{pv2[1]}"],
                               T[f"tmul{pv2[0]}_{pv2[1]}"]]
                    T["mm_g"] = E.do("matmul", ps_g[pp][:],
                                     qxT_sb[ht][:, rt * 128:(rt + 1) * 128],
                                     qwTgb[buf][:, ht * 512:(ht + 1) * 512],
                                     start=(ht == 0),
                                     stop=(ht == HT - 1), deps=dd)
                    T["mm_u"] = E.do("matmul", ps_u[pp][:],
                                     qxT_sb[ht][:, rt * 128:(rt + 1) * 128],
                                     qwTub[buf][:, ht * 512:(ht + 1) * 512],
                                     start=(ht == 0),
                                     stop=(ht == HT - 1))
                T[f"mm1g{gi}_{r}"] = T["mm_g"]
                T[f"mm1u{gi}_{r}"] = T["mm_u"]
            sec("tensor")(s1mm)

            def s1silu(E, gi=gi, r=r, pp=pp, pv2=pv2):
                T[f"silu{gi}_{r}"] = E.do(
                    "activation", sg_sb[pp][:], ps_g[pp][:], ACT.Silu,
                    deps=[T[f"mm1g{gi}_{r}"],
                          T[f"tmul{pv2[0]}_{pv2[1]}"] if pv2 else None])
            sec("scalar")(s1silu)

            def s1mul(E, gi=gi, r=r, pp=pp, pv2=pv2):
                T[f"tmul{gi}_{r}"] = E.do(
                    "tensor_tensor", tch[pp][:], sg_sb[pp][:], ps_u[pp][:],
                    op=ALU.mult,
                    deps=[T[f"silu{gi}_{r}"], T[f"mm1u{gi}_{r}"],
                          T[f"tst{pv2[0]}_{pv2[1]}"] if pv2 else None])
            sec("vector")(s1mul)

            def s1st(E, gi=gi, ic=ic, rt=rt, r=r, pp=pp):
                T[f"tst{gi}_{r}"] = E.do(
                    "dma_start",
                    t_hbm.ap()[rt * 128:(rt + 1) * 128, ic * 512:(ic + 1) * 512],
                    tch[pp][:], deps=[T[f"tmul{gi}_{r}"]])
            sec("gpsimd", "gt" if (gi * GR + r) % 2 == 0 else "gt2")(s1st)
            if mid is not None and r == GR // 2 - 1:
                mid()

        if end is not None:
            end()
        T[f"mm1done{gi}"] = T[f"mm1u{gi}_{GR - 1}"]

    PSET = {}
    P3A_NP = 4 if NCH >= 8 else 2

    def emit_p3a(rt, part):
        bounds = [(NCH * k) // P3A_NP for k in range(P3A_NP + 1)]
        if part == 0:
            def tload(E, rt=rt):
                ps = rt // GR
                gi_last = ps * NIC + NIC - 1
                T[f"tld{rt}"] = E.do("dma_start", tst3[rt % 2][:],
                                     t_hbm.ap()[rt * 128:(rt + 1) * 128, :],
                                     deps=[T[f"tst{gi_last}_{rt % GR}"],
                                           T.get(f"tsh{rt - 1}"),
                                           T.get(f"tqw{rt - 2}")])
            sec("sync", "qtld")(tload)

            sset = q_scales("p3a", tst3[rt % 2][:], deps_src=[T[f"tld{rt}"]])
            PSET[rt] = sset
            T[f"tsh{rt}"] = sset[1]
        else:
            sset = PSET[rt]
        for ch in range(bounds[part], bounds[part + 1]):
            c0 = ch * CHK
            done, wtok, qoptok = q_apply(
                tmpa[ch % 2], tst3[rt % 2][:, c0:c0 + CHK], qtr[:, c0:c0 + CHK],
                sset, dst_war=T.get(f"ttp_done{rt - 1}") if ch == 0 else None)
            T[f"tq{rt}"] = done
            T[f"tq{rt}_s"] = qoptok
            T[f"tqw{rt}"] = wtok
        if part < P3A_NP - 1:
            return

        for it in range(IT):
            def ttp_t(E, rt=rt, it=it):
                T[f"ttp{rt}_{it}"] = E.do(
                    "transpose", ps_tpa[it % 2][:],
                    qtr[:, it * 128:(it + 1) * 128], identb[:],
                    deps=[T[f"tq{rt}"], T[f"tq{rt}_s"],
                          T.get(f"tte{rt}_{it - 2}")])
            sec("tensor")(ttp_t)

            def ttp_v(E, rt=rt, it=it):
                T[f"tte{rt}_{it}"] = E.do(
                    "tensor_copy", qtt[:, it * 128:(it + 1) * 128],
                    ps_tpa[it % 2][:],
                    deps=[T[f"ttp{rt}_{it}"],
                          T.get(f"qtsp{rt - 1}") if it < 2 else None])
            sec("vector")(ttp_v)
        T[f"ttp_done{rt}"] = T[f"ttp{rt}_{IT - 1}"]

        def qtsp(E, rt=rt):
            T[f"qtsp{rt}"] = E.do("dma_start",
                                  qt2_hbm.ap()[rt * 128:(rt + 1) * 128, :],
                                  qtt[:], deps=[T[f"tte{rt}_{IT - 1}"]])
        sec("gpsimd", "gqxt")(qtsp)

    for ic in range(NIC):
        emit_s1_ic(0, ic)
    npieces = [P3A_NP * GR]
    done_p = [0]

    def p3a_next():
        if done_p[0] < npieces[0]:
            emit_p3a(done_p[0] // P3A_NP, done_p[0] % P3A_NP)
            done_p[0] += 1
    for ic in range(NIC):
        emit_s1_ic(1, ic,
                   mid=(p3a_next if ic >= 1 and GR >= 2 else None),
                   end=(p3a_next if ic >= 1 else None))
    while done_p[0] < npieces[0]:
        p3a_next()

    p2.close()
    barrier()

    p3 = ExitStack()
    P3 = lambda n, shp, dt=F32: p3.enter_context(nc.sbuf_tensor(n, shp, dt))
    qwdTb = P3("qwdTb", [128, IT * HOG], BF16)
    qtrow = [P3(f"qtrow{i}", [128, I], BF16) for i in range(2)]
    xsl = [P3(f"xsl{i}", [128, HOG]) for i in range(2)]
    asl = [P3(f"asl{i}", [128, HOG]) for i in range(2)]
    o1 = P3("o1", [128, HOG])
    o2 = P3("o2", [128, HOG])
    ps_y = [p3.enter_context(nc.psum_tensor(f"ps_y{i}", [128, HOG], F32))
            for i in range(2)]
    CPG = HOG // WDS

    WDLT = {}

    def emit_2b_wload(g):
        WDLT[g] = {}
        for cg in range(CPG):
            def wdl(E, g=g, cg=cg):
                c = g * CPG + cg
                srcap = bass_rust.AP(G_hbm, c * CSZ + 2 * CSZ_G,
                                     [[WDS, 128], [128 * WDS, IT], [1, WDS]])
                dst = bass_rust.AP(qwdTb, cg * WDS,
                                   [[IT * HOG, 128], [HOG, IT], [1, WDS]])
                tok = E.do("dma_start", dst, srcap,
                           deps=[T["ccdone"], T.get(f"mm2done{g - 1}")])
                WDLT[g][id(tok[0])] = tok
            s2 = (("qwd", "qctx") if g % 2 == 0 else ("qw", "qqxl"))
            sec("sync", s2[cg % 2])(wdl)

    def emit_2b_rt(g, rt):
        pp = rt % 2

        def qtload(E, g=g, rt=rt, pp=pp):
            if rt >= 2:
                war = T[f"mmy{g}_{rt - 2}"]
            elif g >= 1:
                war = T[f"mmy{g - 1}_{OT - 2 + rt}"]
            else:
                war = None
            T[f"qtl{g}_{rt}"] = E.do(
                "dma_start", qtrow[pp][:],
                qt2_hbm.ap()[rt * 128:(rt + 1) * 128, :],
                deps=[T[f"qtsp{rt}"], war])
        sec("sync", "qt2l")(qtload)

        def ldres_x(E, g=g, rt=rt, pp=pp):
            T[f"xld{g}_{rt}"] = E.do(
                "dma_start", xsl[pp][:],
                x_d.ap()[rt * 128:(rt + 1) * 128, g * HOG:(g + 1) * HOG],
                deps=[T.get(f"oadd{g}_{rt - 2}"),
                      T.get(f"oadd{g - 1}_{OT - 2 + rt}") if rt < 2 else None])
        sec("sync", f"qx{pp}")(ldres_x)

        def ldres_a(E, g=g, rt=rt, pp=pp):
            T[f"ald{g}_{rt}"] = E.do(
                "dma_start", asl[pp][:],
                attn_hbm.ap()[rt * 128:(rt + 1) * 128, g * HOG:(g + 1) * HOG],
                deps=[T[f"attst{OT - 1}"], T.get(f"oadd{g}_{rt - 2}"),
                      T.get(f"oadd{g - 1}_{OT - 2 + rt}") if rt < 2 else None])
        sec("sync", f"qas{pp}")(ldres_a)

        def ymm(E, g=g, rt=rt, pp=pp):
            for it in range(IT):
                T["mmy"] = E.do("matmul", ps_y[pp][:],
                                qtrow[pp][:, it * 128:(it + 1) * 128],
                                qwdTb[:, it * HOG:(it + 1) * HOG],
                                start=(it == 0), stop=(it == IT - 1),
                                deps=list(WDLT[g].values()) + [T[f"qtl{g}_{rt}"],
                                      T.get(f"yev{g}_{rt - 2}"),
                                      T.get(f"yev{g - 1}_{OT - 2 + rt}") if rt < 2 else None])
            T[f"mmy{g}_{rt}"] = T["mmy"]
        sec("tensor")(ymm)

        def yadd(E, g=g, rt=rt, pp=pp):
            a1 = E.do("tensor_tensor", o1[:], ps_y[pp][:], xsl[pp][:],
                      op=ALU.add,
                      deps=[T[f"mmy{g}_{rt}"], T[f"xld{g}_{rt}"],
                            T.get(f"ost{g}_{rt - 1}"),
                            T.get(f"ost{g - 1}_{OT - 1}") if rt == 0 else None])
            T[f"yev{g}_{rt}"] = a1
            T[f"oadd{g}_{rt}"] = E.do("tensor_tensor", o2[:], o1[:],
                                      asl[pp][:], op=ALU.add,
                                      deps=[a1, T[f"ald{g}_{rt}"]])
        sec("vector")(yadd)

        def ost(E, g=g, rt=rt):
            T[f"ost{g}_{rt}"] = E.do(
                "dma_start",
                out_d.ap()[rt * 128:(rt + 1) * 128, g * HOG:(g + 1) * HOG],
                o2[:], deps=[T[f"oadd{g}_{rt}"]])
        sec("gpsimd", "go")(ost)
        if rt == OT - 1:
            T[f"mm2done{g}"] = T[f"mmy{g}_{OT - 1}"]

    emit_2b_wload(0)
    for i in range(GR):
        for prt in range(P3A_NP):
            emit_p3a(GR + i, prt)
        emit_2b_rt(0, 2 * i)
        emit_2b_rt(0, 2 * i + 1)

    for g in range(1, NG):
        emit_2b_wload(g)
        for rt in range(OT):
            emit_2b_rt(g, rt)

    def fin(E):
        for g in range(NG):
            for rt in range(OT):
                E.wait(T[f"ost{g}_{rt}"])
    sec("gpsimd", "go")(fin)

    p3.close()
    p3a.close()
    es.close()
    return nc



_CACHED = {}


def _constants(S, NH, par):
    HD = 128
    inv = (np.float32(1.0) /
           np.power(np.float32(10000.0),
                    np.arange(0, HD, 2, dtype=np.float32) / np.float32(HD)))
    pos_perm = np.concatenate(
        [np.arange(S).reshape(-1, 128)[par::2].ravel(),
         np.arange(S).reshape(-1, 128)[1 - par::2].ravel()]).astype(np.int64)
    ang = inv[:, None].astype(np.float32) * pos_perm[None, :].astype(np.float32)
    cosP = np.cos(ang.astype(np.float64)).astype(np.float32).astype(ml_dtypes.bfloat16)
    sinP = np.sin(ang.astype(np.float64)).astype(np.float32).astype(ml_dtypes.bfloat16)

    BL = S + 128
    NB = 32
    y = np.arange(BL)
    d1 = y - 128
    oht1 = np.zeros((NB, BL), dtype=np.float32)
    ok = (d1 >= 0) & (d1 < S)
    b1 = _bucket_f32(np.clip(d1, 0, S - 1))
    oht1[b1[ok], y[ok]] = 1.0
    d2 = y - 128 + (2 * par - 1) * 128
    oht2 = np.zeros((NB, BL), dtype=np.float32)
    ok2 = (d2 >= 0) & (d2 < S)
    b2 = _bucket_f32(np.clip(d2, 0, S - 1))
    oht2[b2[ok2], y[ok2]] = 1.0

    ident = np.eye(128, dtype=np.float32).astype(ml_dtypes.bfloat16)
    cmaskc = np.where(np.arange(128)[None, :] <= np.arange(128)[:, None],
                      np.float32(0.0), np.float32(-1e30)).astype(np.float32)
    fmaskc = (np.zeros((128, 128), np.float32) if par == 1
              else np.full((128, 128), np.float32(-1e30), np.float32))
    return cosP, sinP, oht1, oht2, ident, cmaskc, fmaskc, pos_perm


def kernel(L0_attn_x, L0_attn_qkv_w, L0_attn_dense_w, L0_attn_rel_attn_table,
           L0_ffn_w_gate, L0_ffn_w_up, L0_ffn_w_down, L0_ffn_up_reorder_index,
           L0_ffn_down_reorder_index):
    x = np.ascontiguousarray(np.asarray(L0_attn_x, dtype=np.float32))
    B, S, H = x.shape
    NH = H // 128
    I = L0_ffn_w_gate.shape[0]
    ncore = 8

    down_idx = np.asarray(L0_ffn_down_reorder_index).astype(np.int64)
    wg_p = np.ascontiguousarray(np.asarray(L0_ffn_w_gate, np.float32)[down_idx])
    wu_p = np.ascontiguousarray(np.asarray(L0_ffn_w_up, np.float32)[down_idx])
    wd = np.ascontiguousarray(np.asarray(L0_ffn_w_down, np.float32))
    qkv = np.ascontiguousarray(np.asarray(L0_attn_qkv_w, np.float32))
    dw = np.ascontiguousarray(np.asarray(L0_attn_dense_w, np.float32))
    rel = np.ascontiguousarray(np.asarray(L0_attn_rel_attn_table, np.float32))

    key = (S, H, NH, I)
    if key not in _CACHED:
        _CACHED[key] = build_nc(S, H, NH, I)
    nc = _CACHED[key]

    in_maps = []
    perms = []
    for c in range(ncore):
        b, par = c // 2, c % 2
        cosP, sinP, oht1, oht2, ident, cmaskc, fmaskc, pos_perm = \
            _constants(S, NH, par)
        xb = np.ascontiguousarray(x[b][pos_perm])
        perms.append(pos_perm)
        isl = wg_p.shape[0] // ncore
        wds = wd.shape[0] // ncore
        in_maps.append({
            "xb": xb, "qkv_w": qkv, "dense_w": dw, "rel_table": rel,
            "wg_s": np.ascontiguousarray(wg_p[c * isl:(c + 1) * isl]),
            "wu_s": np.ascontiguousarray(wu_p[c * isl:(c + 1) * isl]),
            "wd_s": np.ascontiguousarray(wd[c * wds:(c + 1) * wds]),
            "cosP": cosP, "sinP": sinP, "oht1": oht1, "oht2": oht2,
            "identb": ident, "cmask": cmaskc, "fmask": fmaskc,
        })

    res = run_bass_kernel_spmd(nc, in_maps, core_ids=list(range(ncore)))

    out = np.empty((B, S, H), dtype=np.float32)
    for c in range(ncore):
        b, par = c // 2, c % 2
        own = perms[c][: S // 2]
        out[b][own] = res.results[c]["out"]
    return out


# version 30
# speedup vs baseline: 1.0646x; 1.0193x over previous
import math
import numpy as np
import ml_dtypes
from contextlib import ExitStack
import concourse.bass as bass
import concourse.mybir as mybir
import bass_rust
from concourse.bass_utils import run_bass_kernel_spmd

F32 = mybir.dt.float32
BF16 = mybir.dt.bfloat16
F8 = mybir.dt.float8e4
U8 = mybir.dt.uint8
AX = mybir.AxisListType
ALU = mybir.AluOpType
ACT = mybir.ActivationFunctionType

FIXC = 12288.0
SUBN2 = float(2.0 ** -12)
SCALE_QK = 0.08838834764831845


class Eng:
    def __init__(self, eng, sem, hwm, cnt, step=1, selfw=False):
        self.e, self.sem, self.step, self.hwm, self.cnt = eng, sem, step, hwm, cnt
        self.selfw = selfw

    def wait(self, *tokens):
        for tok in tokens:
            if tok is None:
                continue
            sem, val = tok
            key = id(sem)
            if self.hwm.get(key, 0) >= val:
                continue
            self.e.wait_ge(sem, val)
            self.hwm[key] = val

    def do(self, method, *args, deps=(), **kw):
        if self.selfw and self.cnt[0] > 0:
            self.wait((self.sem, self.cnt[0]))
        self.wait(*deps)
        instr = getattr(self.e, method)(*args, **kw)
        self.cnt[0] += self.step
        instr.then_inc(self.sem, self.step)
        return (self.sem, self.cnt[0])


def _bucket_f32(d, num_buckets=32, max_distance=128):
    d = np.asarray(d, dtype=np.int64)
    me = num_buckets // 2
    nf = np.maximum(d, 1).astype(np.float32)
    large = (np.log(nf / np.float32(me)) / np.float32(math.log(max_distance / me))
             * np.float32(num_buckets - me)).astype(np.int32) + me
    large = np.minimum(large, num_buckets - 1)
    return np.where(d < me, d.astype(np.int32), large)


def build_nc(S, H, NH, I):
    NB = 32
    RT = S // 128
    OT = RT // 2
    HT = H // 128
    IT = I // 128
    RO = S // 2
    BL = S + 128
    HOG = min(512, H)
    NG = H // HOG
    CHK = min(256, I)
    NCH = (I + CHK - 1) // CHK
    NIC = I // 512
    assert I % 512 == 0 and H % 128 == 0 and S % 512 == 0 and I % CHK == 0

    nc = bass.Bass("TRN2", target_bir_lowering=False)

    x_d = nc.dram_tensor("xb", [S, H], F32, kind="ExternalInput")
    qkv_d = nc.dram_tensor("qkv_w", [H, 3 * H], F32, kind="ExternalInput")
    dense_d = nc.dram_tensor("dense_w", [H, H], F32, kind="ExternalInput")
    rel_d = nc.dram_tensor("rel_table", [NB, NH], F32, kind="ExternalInput")
    ISL = I // 8
    WDS = H // 8
    wg_d = nc.dram_tensor("wg_s", [ISL, H], F32, kind="ExternalInput")
    wu_d = nc.dram_tensor("wu_s", [ISL, H], F32, kind="ExternalInput")
    wd_d = nc.dram_tensor("wd_s", [WDS, I], F32, kind="ExternalInput")
    CSZ_G = HT * 128 * ISL
    CSZ_D = (I // 128) * 128 * WDS
    CSZ = 2 * CSZ_G + CSZ_D
    cb_hbm = nc.dram_tensor("cb_scr", [CSZ], BF16, kind="Internal")
    G_hbm = nc.dram_tensor("G_scr", [8 * CSZ], BF16, kind="Internal")
    cosP_d = nc.dram_tensor("cosP", [64, S], BF16, kind="ExternalInput")
    sinP_d = nc.dram_tensor("sinP", [64, S], BF16, kind="ExternalInput")
    oht1_d = nc.dram_tensor("oht1", [NB, BL], F32, kind="ExternalInput")
    oht2_d = nc.dram_tensor("oht2", [NB, BL], F32, kind="ExternalInput")
    id_d = nc.dram_tensor("identb", [128, 128], BF16, kind="ExternalInput")
    cm_d = nc.dram_tensor("cmask", [128, 128], F32, kind="ExternalInput")
    fm_d = nc.dram_tensor("fmask", [128, 128], F32, kind="ExternalInput")
    out_d = nc.dram_tensor("out", [RO, H], F32, kind="ExternalOutput")
    F1_hbm = nc.dram_tensor("F1_scr", [NH, BL], F32, kind="Internal")
    F2_hbm = nc.dram_tensor("F2_scr", [NH, BL], F32, kind="Internal")
    qxT_hbm = nc.dram_tensor("qxT_scr", [HT * 128, RO], BF16, kind="Internal")
    ctxT_hbm = nc.dram_tensor("ctxT_scr", [NH * 128, RO], BF16, kind="Internal")
    attn_hbm = nc.dram_tensor("attn_scr", [RO, H], F32, kind="Internal")
    qkvb_hbm = nc.dram_tensor("qkvb_scr", [H, 3 * H], BF16, kind="Internal")
    dwb_hbm = nc.dram_tensor("dwb_scr", [H, H], BF16, kind="Internal")
    t_hbm = nc.dram_tensor("t_scr", [RO, I], F32, kind="Internal")
    qt2_hbm = nc.dram_tensor("qt2_scr", [OT * 128, I], BF16, kind="Internal")

    es = ExitStack()
    SB = lambda n, shp, dt=F32: es.enter_context(nc.sbuf_tensor(n, shp, dt))

    sems, cnts = {}, {}
    for nm in ("qc", "qx0", "qx1", "qw", "qctx", "qqxl", "qtld", "qt2l",
               "qw10", "qw11", "qw12", "qw13", "qwd", "qas0", "qas1", "gf", "gt2",
               "gqxt", "gct", "gat", "gt", "go", "gb0", "gb1", "gw", "v",
               "s", "t"):
        sems[nm] = es.enter_context(nc.semaphore(nm + "_sem"))
        cnts[nm] = [0]
    hwm_by_eng = {}

    def mk(engine_obj, eng_key, sem_key, step=1):
        h = hwm_by_eng.setdefault(eng_key, {})
        return Eng(engine_obj, sems[sem_key], h, cnts[sem_key], step)

    block = es.enter_context(nc.Block())
    T = {}

    cur = {}

    SLOT_SEMS = {"qc", "qx0", "qx1", "qw", "qw10", "qw11", "qw12", "qw13",
                 "qwd", "qas0", "qas1", "qt2l"}

    def sec(engname, sem=None):
        dec = getattr(block, engname)
        key = {"sync": ("q", None, 16), "gpsimd": ("g", None, 16),
               "vector": ("v", "v", 1), "scalar": ("s", "s", 1),
               "tensor": ("t", "t", 1)}[engname]
        semname = sem if sem is not None else key[1]
        assert semname is not None, f"DMA section needs explicit sem: {engname}"
        sw = semname in SLOT_SEMS

        def run(emitfn):
            @dec
            def _(engobj):
                h = hwm_by_eng.setdefault(key[0], {})
                emitfn(Eng(engobj, sems[semname], h, cnts[semname], key[2], sw))
        return run

    def barrier():
        snap = [(sems[nm], cnts[nm][0]) for nm in sems if cnts[nm][0] > 0]
        for engname in ("sync", "gpsimd", "vector", "scalar", "tensor"):
            def bwait(E, snap=snap):
                E.wait(*snap)
            sec(engname, "qc" if engname in ("sync", "gpsimd") else None)(bwait)

    cosP = SB("cosP_s", [64, S], BF16)
    sinP = SB("sinP_s", [64, S], BF16)
    identb = SB("identb_s", [128, 128], BF16)
    cmask = SB("cmask_s", [128, 128])
    fmask = SB("fmask_s", [128, 128])
    eps_t = SB("eps_t", [128, 1])
    relsb = SB("relsb", [NB, NH])
    qs_rmax = [SB(f"qs_rmax{a}", [128, 1]) for a in range(2)]
    qs_rinv = [SB(f"qs_rinv{a}", [128, 1]) for a in range(2)]
    qs_shalf = [SB(f"qs_shalf{a}", [128, 1]) for a in range(2)]
    qs_sdq = [SB(f"qs_sdq{a}", [128, 1]) for a in range(2)]
    qset = [0]

    def q_scales(tag, src_ap, deps_src):
        a = qset[0] = 1 - qset[0]
        P = src_ap.shape[0]
        out = {}

        def v1(E):
            rm = E.do("reduce_max", qs_rmax[a][0:P, :], src_ap, axis=AX.X,
                      apply_absolute_value=True,
                      deps=list(deps_src) + [T.get(f"q_rm_war{a}")])
            out["rm"] = rm
            out["ri"] = E.do("reciprocal", qs_rinv[a][0:P, :],
                             qs_rmax[a][0:P, :],
                             deps=[rm, T.get(f"q_ri_war{a}")])
        sec("vector")(v1)

        def s1(E):
            out["sh"] = E.do("activation", qs_shalf[a][0:P, :],
                             qs_rinv[a][0:P, :], ACT.Copy, scale=224.0,
                             deps=[out["ri"], T.get(f"q_sh_war{a}")])
            out["sq"] = E.do("activation", qs_sdq[a][0:P, :],
                             qs_rmax[a][0:P, :], ACT.Copy, scale=1.0 / 224.0,
                             deps=[out["rm"], T.get(f"q_sq_war{a}")])
        sec("scalar")(s1)
        T[f"q_rm_war{a}"] = out["sq"]
        T[f"q_ri_war{a}"] = out["sh"]
        return a, out["sh"], out["sq"]

    def q_apply(tmp, src_ap, dst_ap, sset, dst_war=None):
        a, sh, sq = sset
        b = tmp["set"]
        W = src_ap.shape[-1]
        P = src_ap.shape[0]
        out = {}

        def s1(E):
            w = E.do("activation", tmp["wt"][0:P, 0:W], src_ap, ACT.Copy,
                     scale=qs_shalf[a][0:P, 0:1], deps=[sh, T.get(f"q_w_war{b}")])
            out["c8"] = E.do("copy", tmp["c8t"][0:P, 0:W], tmp["wt"][0:P, 0:W],
                             deps=[w, T.get(f"q_c8_war{b}")])
            out["w"] = w
        sec("scalar")(s1)

        def v2(E):
            fa = E.do("tensor_scalar", tmp["fixa"][0:P, 0:W], tmp["wt"][0:P, 0:W],
                      FIXC, None, op0=ALU.add,
                      deps=[out["w"], T.get(f"q_fa_war{b}")])
            fq = E.do("tensor_scalar", tmp["fixq"][0:P, 0:W],
                      tmp["fixa"][0:P, 0:W],
                      FIXC, qs_sdq[a][0:P, 0:1], op0=ALU.subtract, op1=ALU.mult,
                      deps=[fa, sq, T.get(f"q_fq_war{b}")])
            w2 = E.do("tensor_tensor", tmp["w2t"][0:P, 0:W], tmp["wt"][0:P, 0:W],
                      tmp["wt"][0:P, 0:W], op=ALU.mult,
                      deps=[T.get(f"q_w2_war{b}")])
            mq = E.do("tensor_scalar", tmp["maskt"][0:P, 0:W],
                      tmp["w2t"][0:P, 0:W],
                      SUBN2, None, op0=ALU.is_lt, deps=[w2, T.get(f"q_mq_war{b}")])
            out["qop"] = E.do("tensor_scalar", dst_ap, tmp["c8t"][0:P, 0:W],
                              qs_sdq[a][0:P, 0:1], None, op0=ALU.mult,
                              deps=[out["c8"], sq, dst_war])
            done = E.do("copy_predicated", dst_ap, tmp["maskt"][0:P, 0:W],
                        tmp["fixq"][0:P, 0:W], deps=[mq, fq, out["qop"], dst_war])
            out["done"] = done
            T[f"q_w_war{b}"] = w2
            T[f"q_fa_war{b}"] = fq
            T[f"q_fq_war{b}"] = done
            T[f"q_w2_war{b}"] = mq
            T[f"q_mq_war{b}"] = done
        sec("vector")(v2)
        T[f"q_c8_war{b}"] = out["qop"]
        T[f"q_sh_war{a}"] = out["w"]
        T[f"q_sq_war{a}"] = out["done"]
        return out["done"], out["w"], out["qop"]

    p0 = ExitStack()
    P0 = lambda n, shp, dt=F32: p0.enter_context(nc.sbuf_tensor(n, shp, dt))
    ohtsb = P0("ohtsb", [NB, BL])
    Fsb = P0("Fsb", [NH, BL])
    xs = [P0(f"xs{i}", [128, H]) for i in range(2)]
    xnf = P0("xnf", [128, H])
    CW0 = min(512, H)
    NST = H // CW0
    stats = P0("stats", [128, 6 * NST])
    mvt = P0("mvt", [128, 2])
    sdv = P0("sdv", [128, 1])
    inv = P0("inv", [128, 1])
    CH0 = H // 2
    tmp0 = [{"set": a, "wt": P0(f"t0_wt{a}", [128, CH0]),
             "fixa": P0(f"t0_fa{a}", [128, CH0]),
             "w2t": P0(f"t0_w2{a}", [128, CH0]),
             "maskt": P0(f"t0_mk{a}", [128, CH0], U8),
             "c8t": P0(f"t0_c8{a}", [128, CH0], F8),
             "fixq": P0(f"t0_fq{a}", [128, CH0], BF16)} for a in range(2)]
    xn_bf = [P0(f"xnb{i}", [128, H], BF16) for i in range(2)]
    qx_op = [P0(f"qx{i}", [128, H], BF16) for i in range(2)]
    qxT_f = [P0(f"qxTf{i}", [128, RO], BF16) for i in range(HT)]
    ps_f = p0.enter_context(nc.psum_tensor("ps_f", [NH, 512], F32))
    ps_tp0 = [p0.enter_context(nc.psum_tensor(f"ps_tp0{i}", [128, 128], BF16))
              for i in range(2)]

    xnT_es = ExitStack()
    xnT = [xnT_es.enter_context(nc.sbuf_tensor(f"xnT{i}", [128, S], BF16,
                                               side="right"))
           for i in range(HT)]

    def p0_loads(E):
        E.do("dma_start", cosP[:], cosP_d.ap())
        E.do("dma_start", sinP[:], sinP_d.ap())
        E.do("dma_start", identb[:], id_d.ap())
        E.do("dma_start", cmask[:], cm_d.ap())
        E.do("dma_start", relsb[:], rel_d.ap())
        E.do("dma_start", fmask[:], fm_d.ap())
        tok = E.do("dma_start", ohtsb[:], oht1_d.ap())
        for k in ("cos", "sin", "ident", "cmask", "fmask", "rel", "oht1"):
            T[k] = tok
    sec("sync", "qc")(p0_loads)

    def xload_emit(i):
        def xload(E, i=i):
            T[f"x{i}"] = E.do("dma_start", xs[i % 2][:],
                              x_d.ap()[i * 128:(i + 1) * 128, :],
                              deps=[T.get(f"xfree{i - 2}")])
        sec("sync", f"qx{i % 2}")(xload)

    xload_emit(0)
    xload_emit(1)

    NCHB = (BL + 511) // 512

    def emit_table(tag, oht_tok, dst, war=None):
        for ch in range(NCHB):
            c0 = ch * 512
            cw = min(512, BL - c0)

            def tmm(E, ch=ch, c0=c0, cw=cw):
                T[f"F{tag}mm{ch}"] = E.do(
                    "matmul", ps_f[0:NH, 0:cw], relsb[0:NB, :],
                    ohtsb[:, c0:c0 + cw], start=True, stop=True,
                    deps=[T["rel"], oht_tok, T.get(f"F{tag}ev{ch - 1}"),
                          war if ch == 0 else None])
            sec("tensor")(tmm)

            def tev(E, ch=ch, c0=c0, cw=cw):
                T[f"F{tag}ev{ch}"] = E.do(
                    "tensor_copy", Fsb[:, c0:c0 + cw], ps_f[0:NH, 0:cw],
                    deps=[T[f"F{tag}mm{ch}"],
                          T.get(f"F{'1' if tag == '2' else 'x'}st")])
            sec("vector")(tev)

        def tst(E):
            T[f"F{tag}st"] = E.do("dma_start", dst.ap(), Fsb[:],
                                  deps=[T[f"F{tag}ev{NCHB - 1}"]])
        sec("gpsimd", "gf")(tst)

    emit_table("1", T["oht1"], F1_hbm)

    def oht2_load(E):
        T["oht2"] = E.do("dma_start", ohtsb[:], oht2_d.ap(),
                         deps=[T[f"F1mm{NCHB - 1}"]])
    sec("sync", "qc")(oht2_load)

    emit_table("2", T["oht2"], F2_hbm, war=T[f"F1ev{NCHB - 1}"])

    def eps_set(E):
        T["eps"] = E.do("memset", eps_t[:], 1e-5)
    sec("vector")(eps_set)

    for i in range(RT):
        own = i < OT
        if i >= 1 and i + 1 < RT:
            xload_emit(i + 1)

        def ln_v1(E, i=i):
            t1 = None
            for c in range(NST):
                t1 = E.do("bn_stats", stats[:, 6 * c:6 * c + 6],
                          xs[i % 2][:, CW0 * c:CW0 * (c + 1)],
                          deps=[T[f"x{i}"], T.get(f"mv{i - 1}")] if c == 0 else [t1])
            T[f"mv{i}"] = E.do("bn_aggr", mvt[:], stats[:],
                               deps=[t1, T.get(f"xn{i - 1}")])
        sec("vector")(ln_v1)

        def ln_s1(E, i=i):
            T[f"sd{i}"] = E.do("activation", sdv[:], mvt[:, 1:2], ACT.Sqrt,
                               bias=eps_t[:, 0:1],
                               deps=[T[f"mv{i}"], T["eps"], T.get(f"inv{i - 1}")])
        sec("scalar")(ln_s1)

        def ln_v2(E, i=i):
            T[f"inv{i}"] = E.do("reciprocal", inv[:], sdv[:], deps=[T[f"sd{i}"]])
            T[f"xn{i}"] = E.do("tensor_scalar", xnf[:], xs[i % 2][:], mvt[:, 0:1],
                               inv[:], op0=ALU.subtract, op1=ALU.mult,
                               deps=[T[f"inv{i}"], T.get(f"xnused{i - 1}")])
            T[f"xfree{i}"] = T[f"xn{i}"]
        sec("vector")(ln_v2)

        def ln_s2(E, i=i):
            T[f"xb{i}"] = E.do("activation", xn_bf[i % 2][:], xnf[:], ACT.Copy,
                               deps=[T[f"xn{i}"],
                                     T.get(f"xtp_done{i - 2}")])
        sec("scalar")(ln_s2)

        if own:
            sset = q_scales("p0", xnf[:], deps_src=[T[f"xn{i}"]])
            lastd = lastw = lastq = None
            for ci, c0 in enumerate((0, CH0)):
                d_, w_, q_ = q_apply(tmp0[ci % 2], xnf[:, c0:c0 + CH0],
                                     qx_op[i % 2][:, c0:c0 + CH0], sset,
                                     dst_war=T.get(f"qxtp_done{i - 2}") if c0 == 0 else None)
                lastd, lastw, lastq = d_, w_, q_
            T[f"qfix{i}"] = lastd
            T[f"qfix{i}_s"] = lastq
            T[f"xnused{i}"] = lastw
        else:
            T[f"xnused{i}"] = T[f"xb{i}"]

        for ht in range(HT):
            def xnt_t(E, i=i, ht=ht):
                T[f"xtp{ht}_{i}"] = E.do(
                    "transpose", ps_tp0[ht % 2][:],
                    xn_bf[i % 2][:, ht * 128:(ht + 1) * 128], identb[:],
                    deps=[T[f"xb{i}"], T["ident"],
                          T.get(f"tp0war{ht % 2}")])
            sec("tensor")(xnt_t)

            def xnt_v(E, i=i, ht=ht):
                T[f"xte{ht}_{i}"] = E.do(
                    "tensor_copy", xnT[ht][:, i * 128:(i + 1) * 128],
                    ps_tp0[ht % 2][:], deps=[T[f"xtp{ht}_{i}"]])
                T[f"tp0war{ht % 2}"] = T[f"xte{ht}_{i}"]
            sec("vector")(xnt_v)
        T[f"xtp_done{i}"] = T[f"xtp{HT - 1}_{i}"]

        if own:
            for ht in range(HT):
                def qxt_t(E, i=i, ht=ht):
                    T[f"qxtp{ht}_{i}"] = E.do(
                        "transpose", ps_tp0[ht % 2][:],
                        qx_op[i % 2][:, ht * 128:(ht + 1) * 128], identb[:],
                        deps=[T[f"qfix{i}"], T[f"qfix{i}_s"],
                              T.get(f"tp0war{ht % 2}")])
                sec("tensor")(qxt_t)

                def qxt_v(E, i=i, ht=ht):
                    T[f"qxte{ht}_{i}"] = E.do(
                        "tensor_copy", qxT_f[ht][:, i * 128:(i + 1) * 128],
                        ps_tp0[ht % 2][:], deps=[T[f"qxtp{ht}_{i}"]])
                    T[f"tp0war{ht % 2}"] = T[f"qxte{ht}_{i}"]
                sec("vector")(qxt_v)
            T[f"qxtp_done{i}"] = T[f"qxtp{HT - 1}_{i}"]

    for ht in range(HT):
        def qxt_g(E, ht=ht):
            T[f"qxst{ht}"] = E.do("dma_start",
                                  qxT_hbm.ap()[ht * 128:(ht + 1) * 128, :],
                                  qxT_f[ht][:],
                                  deps=[T[f"qxte{ht}_{OT - 1}"]])
        sec("gpsimd", "gqxt")(qxt_g)
    T["qxst_done"] = T[f"qxst{HT - 1}"]

    T["xnT_done"] = T[f"xte{HT - 1}_{RT - 1}"]
    p0.close()
    barrier()

    pg = ExitStack()
    PG = lambda n, shp, dt=F32: pg.enter_context(nc.sbuf_tensor(n, shp, dt))
    dpst = [pg.enter_context(nc.sbuf_tensor(f"dpst{a}", [128, H], F32))
            for a in range(2)]
    dpsb = [pg.enter_context(nc.sbuf_tensor(f"dpsb{a}", [128, H], BF16))
            for a in range(2)]
    pgst = [PG(f"pgst{a}", [128, max(H, I)]) for a in range(2)]
    pgq = PG("pgq", [128, max(H, I)], BF16)
    CHKG = min(512, I)
    tmpg = [{"set": a, "wt": PG(f"tg_wt{a}", [128, CHKG]),
             "fixa": PG(f"tg_fa{a}", [128, CHKG]),
             "w2t": PG(f"tg_w2{a}", [128, CHKG]),
             "maskt": PG(f"tg_mk{a}", [128, CHKG], U8),
             "c8t": PG(f"tg_c8{a}", [128, CHKG], F8),
             "fixq": PG(f"tg_fq{a}", [128, CHKG], BF16)} for a in range(2)]
    cbF = PG("cbF", [128, max(HT * ISL, (I // 128) * WDS)], BF16)
    ps_tpg = [pg.enter_context(nc.psum_tensor(f"ps_tpg{a}", [128, 128], BF16))
              for a in range(2)]
    cc_sem = es.enter_context(nc.semaphore("cc_sem"))

    def emit_slice_prep(key, w_in, rows, rowlen, nblk, off_cb, prev_done,
                        prev_cb=None):
        ntile = (rows + 127) // 128
        for st in range(ntile):
            P = min(128, rows - st * 128)

            def sload(E, st=st, P=P, key=key, w_in=w_in, rowlen=rowlen):
                T[f"pgl{key}{st}"] = E.do(
                    "dma_start", pgst[st % 2][0:P, 0:rowlen],
                    w_in.ap()[st * 128:st * 128 + P, :],
                    deps=[T.get(f"pgw{key}{st - 2}")] +
                         (list(prev_done) if st < 2 else []))
            sec("sync", f"qw1{st % 2}")(sload)

            sset = q_scales(f"pg{key}", pgst[st % 2][0:P, 0:rowlen],
                            deps_src=[T[f"pgl{key}{st}"]])
            lastd = lastw = lastq = None
            wch = min(CHKG, rowlen)
            for ch in range(rowlen // wch):
                c0 = ch * wch
                lastd, lastw, lastq = q_apply(
                    tmpg[ch % 2], pgst[st % 2][0:P, c0:c0 + wch],
                    pgq[0:P, c0:c0 + wch], sset,
                    dst_war=T.get(f"pgtp{key}{st - 1}") if ch == 0 else None)
            T[f"pgw{key}{st}"] = lastw
            T[f"pgq{key}{st}"] = lastd
            T[f"pgq{key}{st}_s"] = lastq

            for bt in range(nblk):
                def ptp(E, st=st, P=P, bt=bt, key=key, nblk=nblk):
                    T[f"pgt{key}{st}_{bt}"] = E.do(
                        "transpose", ps_tpg[bt % 2][:, 0:P],
                        pgq[0:P, bt * 128:(bt + 1) * 128], identb[0:P, 0:P],
                        deps=[T[f"pgq{key}{st}"], T[f"pgq{key}{st}_s"],
                              T.get(f"pge{key}{st}_{bt - 2}"),
                              T.get(f"pge{key}{st - 1}_{nblk - 2 + (bt % 2)}")])
                sec("tensor")(ptp)

                def pev(E, st=st, P=P, bt=bt, key=key, rows=rows):
                    T[f"pge{key}{st}_{bt}"] = E.do(
                        "tensor_copy",
                        cbF[:, bt * rows + st * 128: bt * rows + st * 128 + P],
                        ps_tpg[bt % 2][:, 0:P],
                        deps=[T[f"pgt{key}{st}_{bt}"],
                              prev_cb if (st == 0 and bt == 0) else None])
                sec("vector")(pev)
            T[f"pgtp{key}{st}"] = T[f"pgt{key}{st}_{nblk - 1}"]

        def cbst(E, key=key, rows=rows, nblk=nblk, off_cb=off_cb, ntile=ntile):
            dst = bass_rust.AP(cb_hbm, off_cb,
                               [[rows, 128], [128 * rows, nblk], [1, rows]])
            W = max(HT * ISL, (I // 128) * WDS)
            srcap = bass_rust.AP(cbF, 0, [[W, 128], [rows, nblk], [1, rows]])
            T[f"cbst{key}"] = E.do("dma_start", dst, srcap,
                                   deps=[T[f"pge{key}{ntile - 1}_{nblk - 1}"]])
        sec("gpsimd", "gw")(cbst)
        return T[f"cbst{key}"], T[f"pgw{key}{ntile - 1}"], T[f"pgq{key}{ntile - 1}"]

    dg = emit_slice_prep("g", wg_d, ISL, H, HT, 0, [])
    du = emit_slice_prep("u", wu_d, ISL, H, HT, CSZ_G, dg[1:], prev_cb=dg[0])
    dd_ = emit_slice_prep("d", wd_d, WDS, I, I // 128, 2 * CSZ_G, du[1:],
                          prev_cb=du[0])

    def ccgo(E):
        E.wait(dg[0], du[0], dd_[0])
        instr = E.e.collective_compute(
            "AllGather", ALU.bypass, replica_groups=[list(range(8))],
            ins=[cb_hbm.ap().opt()], outs=[G_hbm.ap().opt()])
        instr.then_inc(cc_sem, 1)
    sec("gpsimd", "gw")(ccgo)
    T["ccdone"] = (cc_sem, 1)

    for i in range(HT):
        def dcl(E, i=i):
            T[f"dcl{i}"] = E.do("dma_start", dpst[i % 2][:],
                                dense_d.ap()[i * 128:(i + 1) * 128, :],
                                deps=[T.get(f"dcc{i - 2}")])
        sec("sync", f"qx{i % 2}")(dcl)

        def dcc(E, i=i):
            T[f"dcc{i}"] = E.do("activation", dpsb[i % 2][:], dpst[i % 2][:],
                                ACT.Copy, deps=[T[f"dcl{i}"],
                                                T.get(f"dcs{i - 2}")])
        sec("scalar")(dcc)

        def dcs(E, i=i):
            T[f"dcs{i}"] = E.do("dma_start",
                                dwb_hbm.ap()[i * 128:(i + 1) * 128, :],
                                dpsb[i % 2][:], deps=[T[f"dcc{i}"]])
        sec("gpsimd", "gw" if i % 2 == 0 else "gf")(dcs)
    T["dprec"] = T[f"dcs{HT - 1}"]
    T["dprec2"] = T[f"dcs{HT - 2}"]

    pg.close()
    barrier()

    p1 = ExitStack()
    P1 = lambda n, shp, dt=F32: p1.enter_context(nc.sbuf_tensor(n, shp, dt))
    wq_b = [P1(f"wqb{i}", [128, H], BF16) for i in range(2)]
    wk_b = [P1(f"wkb{i}", [128, H], BF16) for i in range(2)]
    wv_b = [P1(f"wvb{i}", [128, H], BF16) for i in range(2)]
    wstg = [P1("wstg0", [128, H])]
    qb_t = [P1(f"qb_t{a}", [64, RO], BF16) for a in range(2)]
    qb_b = [P1(f"qb_b{a}", [64, RO], BF16) for a in range(2)]
    kb_t = [P1(f"kb_t{a}", [64, S], BF16) for a in range(2)]
    kb_b = [P1(f"kb_b{a}", [64, S], BF16) for a in range(2)]
    vS = P1("vS", [128, S], BF16)
    pT = [P1(f"pT{i}", [128, RO], BF16) for i in range(RT)]
    bias_sb = [P1(f"bias_sb{a}", [128, S], BF16) for a in range(2)]
    hank_sb = P1("hank_sb", [128, S], F32)
    s1 = P1("s1", [128, S], F32)
    esb = P1("esb", [128, S], BF16)
    pbuf = P1("pbuf", [128, S], BF16)
    vTs = pbuf
    rowsum = P1("rowsum", [128, 1], F32)
    rs_acc = P1("rs_acc", [128, 1], F32)
    rtmp1f = P1("rtmp1f", [64, S])
    rtmp = [s1[0:64, :], rtmp1f[:]]
    ctxT_b = P1("ctxT_b", [128, RO], BF16)
    ps_a = p1.enter_context(nc.psum_tensor("ps_a", [128, S], F32))
    ps_sc = p1.enter_context(nc.psum_tensor("ps_sc", [128, S], F32))
    if S <= 1920:
        ps_tp1h = p1.enter_context(nc.psum_tensor("ps_tp1", [128, 128], BF16))
        ps_tp1i = p1.enter_context(nc.psum_tensor("ps_tp1b", [128, 128], BF16))
        tp1s = [ps_tp1h[:], ps_tp1i[:]]
    else:
        tp1s = [ps_sc[:, 3 * S // 4 - 64:3 * S // 4].bitcast(BF16),
                ps_sc[:, S - 64:S].bitcast(BF16)]

    for h in range(NH):
        hb = h % 2

        for ki, (key, base, wb) in enumerate((("wq", 0, wq_b), ("wk", H, wk_b),
                                              ("wv", 2 * H, wv_b))):
            j = 3 * h + ki

            def wload(E, h=h, hb=hb, key=key, base=base, j=j):
                srcap = bass_rust.AP(qkv_d, base + h * 128,
                                     [[3 * H, 128], [128 * 3 * H, HT], [1, 128]])
                dst = bass_rust.AP(wstg[0], 0, [[H, 128], [128, HT], [1, 128]])
                T[f"{key}l{h}"] = E.do("dma_start", dst, srcap,
                                       deps=[T.get(f"wstgu{j - 1}")])
            sec("sync", "qw")(wload)

            def wcast(E, h=h, hb=hb, key=key, wb=wb, j=j):
                T[f"{key}c{h}"] = E.do("tensor_copy", wb[hb][:],
                                       wstg[0][:],
                                       deps=[T[f"{key}l{h}"],
                                             T.get(f"{key}used{h - 2}")])
                T[f"wstgu{j}"] = T[f"{key}c{h}"]
            sec("vector")(wcast)

        def hankload(E, h=h, hb=hb):
            for j in range(OT):
                off = h * BL + (OT - 1 - j) * 256 + 1
                E.do("dma_start", hank_sb[:, j * 128:(j + 1) * 128],
                     bass_rust.AP(F1_hbm, off, [[1, 128], [1, 128]]),
                     deps=[T["F2st"], T.get(f"hkrev{h - 1}")] if j == 0 else [])
                T[f"hkall{h}"] = E.do(
                    "dma_start",
                    hank_sb[:, (OT + j) * 128:(OT + j + 1) * 128],
                    bass_rust.AP(F2_hbm, off, [[1, 128], [1, 128]]))
        sec("gpsimd", f"gb{h % 2}")(hankload)

        def hankrev(E, h=h, hb=hb):
            last = None
            for j in range(2 * OT):
                last = E.do(
                    "tensor_copy", bias_sb[hb][:, j * 128:(j + 1) * 128],
                    hank_sb[:, j * 128:(j + 1) * 128][:, ::-1],
                    deps=[T[f"hkall{h}"],
                          T.get(f"bused{hb}") if j == 0 else None])
            T[f"biasb{h}"] = last
            T[f"hkrev{h}"] = last
        sec("vector")(hankrev)

        def qmm(E, h=h, hb=hb):
            for half in (0, 64):
                for ht in range(HT):
                    for c0 in range(0, RO, 512):
                        cw = min(512, RO - c0)
                        base = (half // 64) * RO
                        T["mmq"] = E.do(
                            "matmul", ps_a[0:64, base + c0:base + c0 + cw],
                            wq_b[hb][:, ht * 128 + half:ht * 128 + half + 64],
                            xnT[ht][:, c0:c0 + cw],
                            start=(ht == 0), stop=(ht == HT - 1),
                            deps=[T[f"wqc{h}"], T["xnT_done"],
                                  T.get(f"vTev{h - 1}"),
                                  T.get(f"ropeq{h - 1}")])
            T[f"mmq{h}"] = T["mmq"]
        sec("tensor")(qmm)

        def ropeq(E, h=h, hb=hb):
            A, B = ps_a[0:64, 0:RO], ps_a[0:64, RO:2 * RO]
            co, si = cosP[:, 0:RO], sinP[:, 0:RO]
            dd = [T[f"mmq{h}"], T["cos"], T["sin"], T.get(f"qbused{h - 2}"),
                  T.get(f"rtmpused{h - 1}")]
            t1 = E.do("tensor_tensor", rtmp[0][:, 0:RO], A, co, op=ALU.mult,
                      deps=dd)
            t2 = E.do("tensor_tensor", rtmp[1][:, 0:RO], B, si, op=ALU.mult)
            t3 = E.do("tensor_tensor", qb_t[hb][:], rtmp[0][:, 0:RO],
                      rtmp[1][:, 0:RO], op=ALU.subtract, deps=[t1, t2])
            t4 = E.do("tensor_tensor", rtmp[0][:, 0:RO], B, co, op=ALU.mult,
                      deps=[t3])
            t5 = E.do("tensor_tensor", rtmp[1][:, 0:RO], A, si, op=ALU.mult,
                      deps=[t3])
            T[f"ropeq{h}"] = E.do("tensor_tensor", qb_b[hb][:],
                                  rtmp[0][:, 0:RO], rtmp[1][:, 0:RO],
                                  op=ALU.add, deps=[t4, t5])
        sec("vector")(ropeq)

        HS = S // 2
        for kh in range(2):
            cbase = kh * HS

            def kmm(E, h=h, hb=hb, kh=kh, cbase=cbase, HS=HS):
                for half in (0, 64):
                    for ht in range(HT):
                        for c0 in range(0, HS, 512):
                            cw = min(512, HS - c0)
                            base = (half // 64) * HS
                            T["mmk"] = E.do(
                                "matmul",
                                ps_a[0:64, base + c0:base + c0 + cw],
                                wk_b[hb][:, ht * 128 + half:ht * 128 + half + 64],
                                xnT[ht][:, cbase + c0:cbase + c0 + cw],
                                start=(ht == 0), stop=(ht == HT - 1),
                                deps=[T[f"wkc{h}"], T[f"ropeq{h}"],
                                      T.get(f"ropek{h}_{kh - 1}")])
                T[f"mmk{h}_{kh}"] = T["mmk"]
            sec("tensor")(kmm)

            def ropek(E, h=h, hb=hb, kh=kh, cbase=cbase, HS=HS):
                A, B = ps_a[0:64, 0:HS], ps_a[0:64, HS:2 * HS]
                co = cosP[:, cbase:cbase + HS]
                si = sinP[:, cbase:cbase + HS]
                dd = [T[f"mmk{h}_{kh}"], T.get(f"kbused{h - 2}")]
                t1 = E.do("tensor_tensor", rtmp[0][:, 0:HS], A, co,
                          op=ALU.mult, deps=dd)
                t2 = E.do("tensor_tensor", rtmp[1][:, 0:HS], B, si,
                          op=ALU.mult)
                t3 = E.do("tensor_tensor", kb_t[hb][:, cbase:cbase + HS],
                          rtmp[0][:, 0:HS], rtmp[1][:, 0:HS],
                          op=ALU.subtract, deps=[t1, t2])
                t4 = E.do("tensor_tensor", rtmp[0][:, 0:HS], B, co,
                          op=ALU.mult, deps=[t3])
                t5 = E.do("tensor_tensor", rtmp[1][:, 0:HS], A, si,
                          op=ALU.mult, deps=[t3])
                T[f"ropek{h}_{kh}"] = E.do(
                    "tensor_tensor", kb_b[hb][:, cbase:cbase + HS],
                    rtmp[0][:, 0:HS], rtmp[1][:, 0:HS], op=ALU.add,
                    deps=[t4, t5])
            sec("vector")(ropek)
        T[f"ropek{h}"] = T[f"ropek{h}_1"]
        T[f"rtmpused{h}"] = T[f"ropek{h}"]

        def vmm(E, h=h, hb=hb):
            for ht in range(HT):
                for c0 in range(0, S, 512):
                    T["mmv"] = E.do("matmul", ps_a[:, c0:c0 + 512],
                                    wv_b[hb][:, ht * 128:(ht + 1) * 128],
                                    xnT[ht][:, c0:c0 + 512],
                                    start=(ht == 0), stop=(ht == HT - 1),
                                    deps=[T[f"wvc{h}"], T[f"ropek{h}"]])
            T[f"mmv{h}"] = T["mmv"]
            T[f"wqused{h}"] = T[f"mmq{h}"]
            T[f"wkused{h}"] = T[f"mmk{h}_1"]
            T[f"wvused{h}"] = T[f"mmv{h}"]
        sec("tensor")(vmm)

        def vev0(E, h=h):
            T[f"vTev{h}"] = E.do("tensor_copy", vTs[:], ps_a[:, 0:S],
                                 deps=[T[f"mmv{h}"], T.get(f"vTused{h - 1}"),
                                       T.get(f"pevlast{h - 1}")])
        sec("vector")(vev0)

        for kt in range(RT):
            def vtp_t(E, h=h, kt=kt):
                T[f"vtp{h}_{kt}"] = E.do("transpose", tp1s[kt % 2],
                                         vTs[:, kt * 128:(kt + 1) * 128],
                                         identb[:],
                                         deps=[T[f"vTev{h}"],
                                               T.get(f"vev{h}_{kt - 2}"),
                                               T.get(f"pevlast{h - 1}") if kt < 2 else None,
                                               T.get(f"ctev{h - 1}") if kt < 2 else None])
            sec("tensor")(vtp_t)

            def vtp_v(E, h=h, kt=kt):
                T[f"vev{h}_{kt}"] = E.do("tensor_copy",
                                         vS[:, kt * 128:(kt + 1) * 128],
                                         tp1s[kt % 2],
                                         deps=[T[f"vtp{h}_{kt}"],
                                               T.get(f"vSused{h - 1}")])
            sec("vector")(vtp_v)
        T[f"vSdone{h}"] = T[f"vev{h}_{RT - 1}"]
        T[f"vTused{h}"] = T[f"vtp{h}_{RT - 1}"]

        for n in range(OT):
            On = (n + 1) * 128
            Kn = 2 * On

            def smm(E, h=h, n=n, On=On, Kn=Kn):
                dd = [T[f"ropek{h}"], T[f"ropeq{h}"], T.get(f"s1ev{h}_{n - 1}"),
                      T.get(f"pevlast{h}_{n - 1}"), T[f"vSdone{h}"],
                      T.get(f"ctev{h - 1}") if n == 0 else None]
                qs_ = n * 128
                for c0 in range(0, On, 512):
                    cw = min(512, On - c0)
                    T["mmsc"] = E.do("matmul", ps_sc[:, c0:c0 + cw],
                                     qb_t[h % 2][:, qs_:qs_ + 128],
                                     kb_t[h % 2][:, c0:c0 + cw], start=True,
                                     stop=False, deps=dd)
                    T["mmsc"] = E.do("matmul", ps_sc[:, c0:c0 + cw],
                                     qb_b[h % 2][:, qs_:qs_ + 128],
                                     kb_b[h % 2][:, c0:c0 + cw], start=False,
                                     stop=True)
                c0 = On
                while c0 < Kn:
                    cw = min((c0 // 512 + 1) * 512, Kn) - c0
                    kc = RO + (c0 - On)
                    T["mmsc"] = E.do("matmul", ps_sc[:, c0:c0 + cw],
                                     qb_t[h % 2][:, qs_:qs_ + 128],
                                     kb_t[h % 2][:, kc:kc + cw], start=True,
                                     stop=False, deps=dd)
                    T["mmsc"] = E.do("matmul", ps_sc[:, c0:c0 + cw],
                                     qb_b[h % 2][:, qs_:qs_ + 128],
                                     kb_b[h % 2][:, kc:kc + cw], start=False,
                                     stop=True)
                    c0 += cw
                T[f"mmsc{h}_{n}"] = T["mmsc"]
            sec("tensor")(smm)

            def sfx(E, h=h, n=n, On=On, Kn=Kn):
                bo = (OT - 1 - n) * 128
                t1 = E.do("scalar_tensor_tensor", s1[:, 0:On], ps_sc[:, 0:On],
                          SCALE_QK, bias_sb[h % 2][:, bo:bo + On], op0=ALU.mult,
                          op1=ALU.add,
                          deps=[T[f"mmsc{h}_{n}"], T[f"biasb{h}"],
                                T.get(f"s1used{h}_{n - 1}"),
                                T.get(f"s1used{h - 1}_{OT - 1}") if n == 0 else None])
                t1 = E.do("scalar_tensor_tensor", s1[:, On:Kn],
                          ps_sc[:, On:Kn], SCALE_QK,
                          bias_sb[h % 2][:, OT * 128 + bo:OT * 128 + bo + On],
                          op0=ALU.mult, op1=ALU.add)
                T[f"bused{h % 2}"] = t1
                T[f"s1ev{h}_{n}"] = t1
                t2 = E.do("tensor_tensor", s1[:, On - 128:On], s1[:, On - 128:On],
                          cmask[:], op=ALU.add, deps=[t1, T["cmask"]])
                t3 = E.do("tensor_tensor", s1[:, Kn - 128:Kn], s1[:, Kn - 128:Kn],
                          fmask[:], op=ALU.add, deps=[t1, T["fmask"]])
                T[f"nm{h}_{n}"] = E.do("reduce_max", rowsum[:], s1[:, 0:Kn],
                                       axis=AX.X, negate=True,
                                       deps=[t2, t3, T.get(f"rsused{h}_{n - 1}"),
                                             T.get(f"rsused{h - 1}_{OT - 1}") if n == 0 else None])
            sec("vector")(sfx)

            def sexp(E, h=h, n=n, Kn=Kn):
                T[f"exp{h}_{n}"] = E.do("activation", esb[:, 0:Kn], s1[:, 0:Kn],
                                        ACT.Exp, bias=rowsum[:, 0:1],
                                        accum_out=rs_acc[:, 0:1],
                                        deps=[T[f"nm{h}_{n}"],
                                              T.get(f"pbb{h}_{n - 1}"),
                                              T.get(f"pbb{h - 1}_{OT - 1}") if n == 0 else None])
                T[f"s1used{h}_{n}"] = T[f"exp{h}_{n}"]
            sec("scalar")(sexp)

            def srec(E, h=h, n=n):
                T[f"rsi{h}_{n}"] = E.do("reciprocal", rowsum[:], rs_acc[:],
                                        deps=[T[f"exp{h}_{n}"]])
            sec("vector")(srec)

            def spb(E, h=h, n=n, Kn=Kn):
                T[f"pbb{h}_{n}"] = E.do("tensor_scalar", pbuf[:, 0:Kn],
                                        esb[:, 0:Kn], rowsum[:, 0:1], None,
                                        op0=ALU.mult,
                                        deps=[T[f"rsi{h}_{n}"],
                                              T.get(f"pbused{h}_{n - 1}"),
                                              T[f"vTused{h}"] if n == 0 else None,
                                              T.get(f"pbused{h - 1}_{OT - 1}") if n == 0 else None])
                T[f"rsused{h}_{n}"] = T[f"pbb{h}_{n}"]
            sec("vector")(spb)

            for ct in range(Kn // 128):
                def ptp_t(E, h=h, n=n, ct=ct):
                    T[f"ptp{h}_{n}_{ct}"] = E.do(
                        "transpose", tp1s[ct % 2],
                        pbuf[:, ct * 128:(ct + 1) * 128], identb[:],
                        deps=[T[f"pbb{h}_{n}"], T.get(f"pev{h}_{n}_{ct - 2}"),
                              T.get(f"pevlast{h}_{n - 1}") if ct < 2 else None,
                              T[f"vSdone{h}"] if ct < 2 else None])
                sec("tensor")(ptp_t)

                def ptp_v(E, h=h, n=n, ct=ct):
                    kt = ct if ct <= n else OT + (ct - (n + 1))
                    T[f"pev{h}_{n}_{ct}"] = E.do(
                        "tensor_copy", pT[kt][:, n * 128:(n + 1) * 128],
                        tp1s[ct % 2],
                        deps=[T[f"ptp{h}_{n}_{ct}"],
                              T.get(f"pTused{h - 1}") if (ct == 0 and n == 0) else None])
                sec("vector")(ptp_v)
            T[f"pevlast{h}_{n}"] = T[f"pev{h}_{n}_{Kn // 128 - 1}"]
            T[f"pbused{h}_{n}"] = T[f"pevlast{h}_{n}"]

        T[f"pevlast{h}"] = T[f"pevlast{h}_{OT - 1}"]

        def avmm(E, h=h):
            nkt = 2 * OT
            last = None
            for kt in range(nkt):
                qs = kt if kt < OT else kt - OT
                deps = [T[f"vSdone{h}"], T[f"ropek{h}"], T.get(f"ctev{h - 1}")]
                for n in range(qs, OT):
                    ct = kt if kt < OT else (n + 1) + (kt - OT)
                    deps.append(T[f"pev{h}_{n}_{ct}"])
                if kt == 0:
                    deps.append(T[f"s1ev{h}_{OT - 1}"])
                for c0 in range(0, RO, 512):
                    a = max(c0, qs * 128)
                    b = min(c0 + 512, RO)
                    if a >= b:
                        continue
                    last = E.do("matmul", ps_sc[:, a:b],
                                vS[:, kt * 128:(kt + 1) * 128], pT[kt][:, a:b],
                                start=(kt == 0),
                                stop=(kt == nkt - 1 and b == RO), deps=deps)
            T[f"mmav{h}"] = last
            T[f"vSused{h}"] = last
            T[f"pTused{h}"] = last
        sec("tensor")(avmm)

        def ctev(E, h=h):
            T[f"ctev{h}"] = E.do("tensor_copy", ctxT_b[:], ps_sc[:, 0:RO],
                                 deps=[T[f"mmav{h}"], T.get(f"ctst{h - 1}")])
            T[f"qbused{h}"] = T[f"mmsc{h}_{OT - 1}"]
            T[f"kbused{h}"] = T[f"mmsc{h}_{OT - 1}"]
        sec("vector")(ctev)

        def ctst(E, h=h):
            T[f"ctst{h}"] = E.do("dma_start",
                                 ctxT_hbm.ap()[h * 128:(h + 1) * 128, :],
                                 ctxT_b[:], deps=[T[f"ctev{h}"]])
        sec("gpsimd", "gct")(ctst)

    p1.close()
    xnT_es.close()
    barrier()

    pd = ExitStack()
    PD = lambda n, shp, dt=F32: pd.enter_context(nc.sbuf_tensor(n, shp, dt))
    wd_b = [PD(f"wdb{i}", [128, H], BF16) for i in range(HT)]
    ctx_sb = [PD(f"ctxs{i}", [128, RO], BF16) for i in range(HT)]
    att_f = PD("att_f", [128, H])
    ps_d = pd.enter_context(nc.psum_tensor("ps_d", [128, H], F32))

    def wdload_d(E):
        for ct in range(HT):
            T[f"wdc{ct}"] = E.do("dma_start", wd_b[ct][:],
                                 dwb_hbm.ap()[ct * 128:(ct + 1) * 128, :],
                                 deps=[T["dprec"], T["dprec2"]] if ct == 0 else [])
    sec("sync", "qctx")(wdload_d)

    def dctx(E):
        for hh in range(HT):
            T[f"ctl{hh}"] = E.do("dma_start", ctx_sb[hh][:],
                                 ctxT_hbm.ap()[hh * 128:(hh + 1) * 128, :],
                                 deps=[T[f"ctst{NH - 1}"]])
        T["ctl_all"] = T[f"ctl{HT - 1}"]
    sec("sync", "qctx")(dctx)

    for rt in range(OT):
        def dmm(E, rt=rt):
            for ct in range(HT):
                for c0 in range(0, H, 512):
                    cw = min(512, H - c0)
                    T["mmd"] = E.do("matmul", ps_d[:, c0:c0 + cw],
                                    ctx_sb[ct][:, rt * 128:(rt + 1) * 128],
                                    wd_b[ct][:, c0:c0 + cw],
                                    start=(ct == 0), stop=(ct == HT - 1),
                                    deps=[T["ctl_all"],
                                          T.get(f"attev{rt - 1}")])
            T[f"mmd{rt}"] = T["mmd"]
        sec("tensor")(dmm)

        def dev(E, rt=rt):
            T[f"attev{rt}"] = E.do("tensor_copy", att_f[:], ps_d[:, 0:H],
                                   deps=[T[f"mmd{rt}"], T.get(f"attst{rt - 1}")])
        sec("vector")(dev)

        def dst(E, rt=rt):
            T[f"attst{rt}"] = E.do("dma_start",
                                   attn_hbm.ap()[rt * 128:(rt + 1) * 128, :],
                                   att_f[:], deps=[T[f"attev{rt}"]])
        sec("gpsimd", "gat")(dst)

    pd.close()
    barrier()

    GR = OT // 2
    p3a = ExitStack()
    PA = lambda n, shp, dt=F32: p3a.enter_context(
        nc.sbuf_tensor(n, shp, dt, side="right"))
    tst3 = [PA(f"tst3{i}", [128, I]) for i in range(2)]
    qtr = PA("qtr", [128, I], BF16)
    qtt = PA("qtt", [128, I], BF16)
    tmpa = [{"set": a, "wt": PA(f"ta_wt{a}", [128, CHK]),
             "fixa": PA(f"ta_fa{a}", [128, CHK]),
             "w2t": PA(f"ta_w2{a}", [128, CHK]),
             "maskt": PA(f"ta_mk{a}", [128, CHK], U8),
             "c8t": PA(f"ta_c8{a}", [128, CHK], F8),
             "fixq": PA(f"ta_fq{a}", [128, CHK], BF16)} for a in range(2)]
    ps_tpa = [p3a.enter_context(nc.psum_tensor(f"ps_tpa{i}", [128, 128], BF16))
              for i in range(2)]

    p2 = ExitStack()
    P2 = lambda n, shp, dt=F32: p2.enter_context(nc.sbuf_tensor(n, shp, dt))
    qxT_sb = [P2(f"qxTs{i}", [128, RO], BF16) for i in range(HT)]
    qwTgb = [P2(f"qwTgb{a}", [128, HT * 512], BF16) for a in range(2)]
    qwTub = [P2(f"qwTub{a}", [128, HT * 512], BF16) for a in range(2)]
    S1WL = {}
    sg_sb = [P2(f"sg_sb{i}", [128, 512]) for i in range(2)]
    tch = [P2(f"tch{i}", [128, 512], F32) for i in range(2)]
    ps_g = [p2.enter_context(nc.psum_tensor(f"ps_g{i}", [128, 512], F32))
            for i in range(2)]
    ps_u = [p2.enter_context(nc.psum_tensor(f"ps_u{i}", [128, 512], F32))
            for i in range(2)]

    def qxl(E):
        for ht in range(HT):
            T[f"qxl{ht}"] = E.do("dma_start", qxT_sb[ht][:],
                                 qxT_hbm.ap()[ht * 128:(ht + 1) * 128, :],
                                 deps=[T["qxst_done"]])
        T["qxl_done"] = T[f"qxl{HT - 1}"]
    sec("sync", "qqxl")(qxl)

    def _gpieces(w0):
        out = []
        o = 0
        w = w0
        while w < w0 + 512:
            c = w // ISL
            j0 = w - c * ISL
            ln = min(ISL - j0, w0 + 512 - w)
            out.append((c, j0, ln, o))
            o += ln
            w += ln
        return out

    def _s1prev(gi, r, k):
        s = gi * GR + r - k
        if s < 0:
            return None
        return divmod(s, GR)

    def emit_s1_ic(ps, ic, mid=None, end=None):
        gi = ps * NIC + ic
        buf = gi % 2
        pieces = _gpieces(ic * 512)
        for gu, off_cb, qwB, sp in (("g", 0, qwTgb, ("qw10", "qw12")),
                                    ("u", CSZ_G, qwTub, ("qw11", "qw13"))):
            S1WL[(gi, gu)] = {}
            for pi, (c, j0, ln, o) in enumerate(pieces):
                def wTl(E, gi=gi, buf=buf, off_cb=off_cb, qwB=qwB,
                        c=c, j0=j0, ln=ln, o=o):
                    srcap = bass_rust.AP(G_hbm, c * CSZ + off_cb + j0,
                                         [[ISL, 128], [128 * ISL, HT], [1, ln]])
                    dst = bass_rust.AP(qwB[buf], o,
                                       [[HT * 512, 128], [512, HT], [1, ln]])
                    tok = E.do("dma_start", dst, srcap,
                               deps=[T["ccdone"], T.get(f"mm1done{gi - 2}")])
                    S1WL[(gi, gu)][id(tok[0])] = tok
                sec("sync", sp[pi % 2])(wTl)

        for r in range(GR):
            rt = ps * GR + r
            pp = (gi * GR + r) % 2
            pv2 = _s1prev(gi, r, 2)
            pv1 = _s1prev(gi, r, 1)

            def s1mm(E, gi=gi, rt=rt, r=r, pp=pp, buf=buf, pv2=pv2):
                for ht in range(HT):
                    dd = (list(S1WL[(gi, "g")].values()) +
                          list(S1WL[(gi, "u")].values()) + [T["qxl_done"]])
                    if pv2 is not None:
                        dd += [T[f"silu{pv2[0]}_{pv2[1]}"],
                               T[f"tmul{pv2[0]}_{pv2[1]}"]]
                    T["mm_g"] = E.do("matmul", ps_g[pp][:],
                                     qxT_sb[ht][:, rt * 128:(rt + 1) * 128],
                                     qwTgb[buf][:, ht * 512:(ht + 1) * 512],
                                     start=(ht == 0),
                                     stop=(ht == HT - 1), deps=dd)
                    T["mm_u"] = E.do("matmul", ps_u[pp][:],
                                     qxT_sb[ht][:, rt * 128:(rt + 1) * 128],
                                     qwTub[buf][:, ht * 512:(ht + 1) * 512],
                                     start=(ht == 0),
                                     stop=(ht == HT - 1))
                T[f"mm1g{gi}_{r}"] = T["mm_g"]
                T[f"mm1u{gi}_{r}"] = T["mm_u"]
            sec("tensor")(s1mm)

            def s1silu(E, gi=gi, r=r, pp=pp, pv2=pv2):
                T[f"silu{gi}_{r}"] = E.do(
                    "activation", sg_sb[pp][:], ps_g[pp][:], ACT.Silu,
                    deps=[T[f"mm1g{gi}_{r}"],
                          T[f"tmul{pv2[0]}_{pv2[1]}"] if pv2 else None])
            sec("scalar")(s1silu)

            def s1mul(E, gi=gi, r=r, pp=pp, pv2=pv2):
                T[f"tmul{gi}_{r}"] = E.do(
                    "tensor_tensor", tch[pp][:], sg_sb[pp][:], ps_u[pp][:],
                    op=ALU.mult,
                    deps=[T[f"silu{gi}_{r}"], T[f"mm1u{gi}_{r}"],
                          T[f"tst{pv2[0]}_{pv2[1]}"] if pv2 else None])
            sec("vector")(s1mul)

            def s1st(E, gi=gi, ic=ic, rt=rt, r=r, pp=pp):
                T[f"tst{gi}_{r}"] = E.do(
                    "dma_start",
                    t_hbm.ap()[rt * 128:(rt + 1) * 128, ic * 512:(ic + 1) * 512],
                    tch[pp][:], deps=[T[f"tmul{gi}_{r}"]])
            sec("gpsimd", "gt" if (gi * GR + r) % 2 == 0 else "gt2")(s1st)
            if mid is not None and r == GR // 2 - 1:
                mid()

        if end is not None:
            end()
        T[f"mm1done{gi}"] = T[f"mm1u{gi}_{GR - 1}"]

    PSET = {}
    P3A_NP = 4 if NCH >= 8 else 2

    def emit_p3a(rt, part):
        bounds = [(NCH * k) // P3A_NP for k in range(P3A_NP + 1)]
        if part == 0:
            def tload(E, rt=rt):
                ps = rt // GR
                gi_last = ps * NIC + NIC - 1
                T[f"tld{rt}"] = E.do("dma_start", tst3[rt % 2][:],
                                     t_hbm.ap()[rt * 128:(rt + 1) * 128, :],
                                     deps=[T[f"tst{gi_last}_{rt % GR}"],
                                           T.get(f"tsh{rt - 1}"),
                                           T.get(f"tqw{rt - 2}")])
            sec("sync", "qtld")(tload)

            sset = q_scales("p3a", tst3[rt % 2][:], deps_src=[T[f"tld{rt}"]])
            PSET[rt] = sset
            T[f"tsh{rt}"] = sset[1]
        else:
            sset = PSET[rt]
        for ch in range(bounds[part], bounds[part + 1]):
            c0 = ch * CHK
            done, wtok, qoptok = q_apply(
                tmpa[ch % 2], tst3[rt % 2][:, c0:c0 + CHK], qtr[:, c0:c0 + CHK],
                sset, dst_war=T.get(f"ttp_done{rt - 1}") if ch == 0 else None)
            T[f"tq{rt}"] = done
            T[f"tq{rt}_s"] = qoptok
            T[f"tqw{rt}"] = wtok
        if part < P3A_NP - 1:
            return

        for it in range(IT):
            def ttp_t(E, rt=rt, it=it):
                T[f"ttp{rt}_{it}"] = E.do(
                    "transpose", ps_tpa[it % 2][:],
                    qtr[:, it * 128:(it + 1) * 128], identb[:],
                    deps=[T[f"tq{rt}"], T[f"tq{rt}_s"],
                          T.get(f"tte{rt}_{it - 2}")])
            sec("tensor")(ttp_t)

            def ttp_v(E, rt=rt, it=it):
                T[f"tte{rt}_{it}"] = E.do(
                    "tensor_copy", qtt[:, it * 128:(it + 1) * 128],
                    ps_tpa[it % 2][:],
                    deps=[T[f"ttp{rt}_{it}"],
                          T.get(f"qtsp{rt - 1}") if it < 2 else None])
            sec("vector")(ttp_v)
        T[f"ttp_done{rt}"] = T[f"ttp{rt}_{IT - 1}"]

        def qtsp(E, rt=rt):
            T[f"qtsp{rt}"] = E.do("dma_start",
                                  qt2_hbm.ap()[rt * 128:(rt + 1) * 128, :],
                                  qtt[:], deps=[T[f"tte{rt}_{IT - 1}"]])
        sec("gpsimd", "gqxt")(qtsp)

    for ic in range(NIC):
        emit_s1_ic(0, ic)
    npieces = [P3A_NP * GR]
    done_p = [0]

    def p3a_next():
        if done_p[0] < npieces[0]:
            emit_p3a(done_p[0] // P3A_NP, done_p[0] % P3A_NP)
            done_p[0] += 1
    for ic in range(NIC):
        emit_s1_ic(1, ic,
                   mid=(p3a_next if ic >= 1 and GR >= 2 else None),
                   end=(p3a_next if ic >= 1 else None))
    while done_p[0] < npieces[0]:
        p3a_next()

    p2.close()
    barrier()

    p3 = ExitStack()
    P3 = lambda n, shp, dt=F32: p3.enter_context(nc.sbuf_tensor(n, shp, dt))
    qwdTb = P3("qwdTb", [128, IT * HOG], BF16)
    qtrow = [P3(f"qtrow{i}", [128, I], BF16) for i in range(2)]
    xsl = [P3(f"xsl{i}", [128, HOG]) for i in range(2)]
    asl = [P3(f"asl{i}", [128, HOG]) for i in range(2)]
    o1 = P3("o1", [128, HOG])
    o2 = P3("o2", [128, HOG])
    ps_y = [p3.enter_context(nc.psum_tensor(f"ps_y{i}", [128, HOG], F32))
            for i in range(2)]
    CPG = HOG // WDS

    WDLT = {}

    def emit_2b_wload(g):
        WDLT[g] = {}
        for cg in range(CPG):
            def wdl(E, g=g, cg=cg):
                c = g * CPG + cg
                srcap = bass_rust.AP(G_hbm, c * CSZ + 2 * CSZ_G,
                                     [[WDS, 128], [128 * WDS, IT], [1, WDS]])
                dst = bass_rust.AP(qwdTb, cg * WDS,
                                   [[IT * HOG, 128], [HOG, IT], [1, WDS]])
                tok = E.do("dma_start", dst, srcap,
                           deps=[T["ccdone"], T.get(f"mm2done{g - 1}")])
                WDLT[g][id(tok[0])] = tok
            s2 = (("qwd", "qctx") if g % 2 == 0 else ("qw", "qqxl"))
            sec("sync", s2[cg % 2])(wdl)

    def emit_2b_rt(g, rt):
        pp = rt % 2

        def qtload(E, g=g, rt=rt, pp=pp):
            if rt >= 2:
                war = T[f"mmy{g}_{rt - 2}"]
            elif g >= 1:
                war = T[f"mmy{g - 1}_{OT - 2 + rt}"]
            else:
                war = None
            T[f"qtl{g}_{rt}"] = E.do(
                "dma_start", qtrow[pp][:],
                qt2_hbm.ap()[rt * 128:(rt + 1) * 128, :],
                deps=[T[f"qtsp{rt}"], war])
        sec("sync", "qt2l")(qtload)

        def ldres_x(E, g=g, rt=rt, pp=pp):
            T[f"xld{g}_{rt}"] = E.do(
                "dma_start", xsl[pp][:],
                x_d.ap()[rt * 128:(rt + 1) * 128, g * HOG:(g + 1) * HOG],
                deps=[T.get(f"oadd{g}_{rt - 2}"),
                      T.get(f"oadd{g - 1}_{OT - 2 + rt}") if rt < 2 else None])
        sec("sync", f"qx{pp}")(ldres_x)

        def ldres_a(E, g=g, rt=rt, pp=pp):
            T[f"ald{g}_{rt}"] = E.do(
                "dma_start", asl[pp][:],
                attn_hbm.ap()[rt * 128:(rt + 1) * 128, g * HOG:(g + 1) * HOG],
                deps=[T[f"attst{OT - 1}"], T.get(f"oadd{g}_{rt - 2}"),
                      T.get(f"oadd{g - 1}_{OT - 2 + rt}") if rt < 2 else None])
        sec("sync", f"qas{pp}")(ldres_a)

        def ymm(E, g=g, rt=rt, pp=pp):
            for it in range(IT):
                T["mmy"] = E.do("matmul", ps_y[pp][:],
                                qtrow[pp][:, it * 128:(it + 1) * 128],
                                qwdTb[:, it * HOG:(it + 1) * HOG],
                                start=(it == 0), stop=(it == IT - 1),
                                deps=list(WDLT[g].values()) + [T[f"qtl{g}_{rt}"],
                                      T.get(f"yev{g}_{rt - 2}"),
                                      T.get(f"yev{g - 1}_{OT - 2 + rt}") if rt < 2 else None])
            T[f"mmy{g}_{rt}"] = T["mmy"]
        sec("tensor")(ymm)

        def yadd(E, g=g, rt=rt, pp=pp):
            a1 = E.do("tensor_tensor", o1[:], ps_y[pp][:], xsl[pp][:],
                      op=ALU.add,
                      deps=[T[f"mmy{g}_{rt}"], T[f"xld{g}_{rt}"],
                            T.get(f"ost{g}_{rt - 1}"),
                            T.get(f"ost{g - 1}_{OT - 1}") if rt == 0 else None])
            T[f"yev{g}_{rt}"] = a1
            T[f"oadd{g}_{rt}"] = E.do("tensor_tensor", o2[:], o1[:],
                                      asl[pp][:], op=ALU.add,
                                      deps=[a1, T[f"ald{g}_{rt}"]])
        sec("vector")(yadd)

        def ost(E, g=g, rt=rt):
            T[f"ost{g}_{rt}"] = E.do(
                "dma_start",
                out_d.ap()[rt * 128:(rt + 1) * 128, g * HOG:(g + 1) * HOG],
                o2[:], deps=[T[f"oadd{g}_{rt}"]])
        sec("gpsimd", "go")(ost)
        if rt == OT - 1:
            T[f"mm2done{g}"] = T[f"mmy{g}_{OT - 1}"]

    emit_2b_wload(0)
    for i in range(GR):
        for prt in range(P3A_NP):
            emit_p3a(GR + i, prt)
        emit_2b_rt(0, 2 * i)
        emit_2b_rt(0, 2 * i + 1)

    for g in range(1, NG):
        emit_2b_wload(g)
        for rt in range(OT):
            emit_2b_rt(g, rt)

    def fin(E):
        for g in range(NG):
            for rt in range(OT):
                E.wait(T[f"ost{g}_{rt}"])
    sec("gpsimd", "go")(fin)

    p3.close()
    p3a.close()
    es.close()
    return nc



_CACHED = {}


def _constants(S, NH, par):
    HD = 128
    inv = (np.float32(1.0) /
           np.power(np.float32(10000.0),
                    np.arange(0, HD, 2, dtype=np.float32) / np.float32(HD)))
    pos_perm = np.concatenate(
        [np.arange(S).reshape(-1, 128)[par::2].ravel(),
         np.arange(S).reshape(-1, 128)[1 - par::2].ravel()]).astype(np.int64)
    ang = inv[:, None].astype(np.float32) * pos_perm[None, :].astype(np.float32)
    cosP = np.cos(ang.astype(np.float64)).astype(np.float32).astype(ml_dtypes.bfloat16)
    sinP = np.sin(ang.astype(np.float64)).astype(np.float32).astype(ml_dtypes.bfloat16)

    BL = S + 128
    NB = 32
    y = np.arange(BL)
    d1 = y - 128
    oht1 = np.zeros((NB, BL), dtype=np.float32)
    ok = (d1 >= 0) & (d1 < S)
    b1 = _bucket_f32(np.clip(d1, 0, S - 1))
    oht1[b1[ok], y[ok]] = 1.0
    d2 = y - 128 + (2 * par - 1) * 128
    oht2 = np.zeros((NB, BL), dtype=np.float32)
    ok2 = (d2 >= 0) & (d2 < S)
    b2 = _bucket_f32(np.clip(d2, 0, S - 1))
    oht2[b2[ok2], y[ok2]] = 1.0

    ident = np.eye(128, dtype=np.float32).astype(ml_dtypes.bfloat16)
    cmaskc = np.where(np.arange(128)[None, :] <= np.arange(128)[:, None],
                      np.float32(0.0), np.float32(-1e30)).astype(np.float32)
    fmaskc = (np.zeros((128, 128), np.float32) if par == 1
              else np.full((128, 128), np.float32(-1e30), np.float32))
    return cosP, sinP, oht1, oht2, ident, cmaskc, fmaskc, pos_perm


def kernel(L0_attn_x, L0_attn_qkv_w, L0_attn_dense_w, L0_attn_rel_attn_table,
           L0_ffn_w_gate, L0_ffn_w_up, L0_ffn_w_down, L0_ffn_up_reorder_index,
           L0_ffn_down_reorder_index):
    x = np.ascontiguousarray(np.asarray(L0_attn_x, dtype=np.float32))
    B, S, H = x.shape
    NH = H // 128
    I = L0_ffn_w_gate.shape[0]
    ncore = 8

    down_idx = np.asarray(L0_ffn_down_reorder_index).astype(np.int64)
    wg_p = np.ascontiguousarray(np.asarray(L0_ffn_w_gate, np.float32)[down_idx])
    wu_p = np.ascontiguousarray(np.asarray(L0_ffn_w_up, np.float32)[down_idx])
    wd = np.ascontiguousarray(np.asarray(L0_ffn_w_down, np.float32))
    qkv = np.ascontiguousarray(np.asarray(L0_attn_qkv_w, np.float32))
    dw = np.ascontiguousarray(np.asarray(L0_attn_dense_w, np.float32))
    rel = np.ascontiguousarray(np.asarray(L0_attn_rel_attn_table, np.float32))

    key = (S, H, NH, I)
    if key not in _CACHED:
        _CACHED[key] = build_nc(S, H, NH, I)
    nc = _CACHED[key]

    in_maps = []
    perms = []
    for c in range(ncore):
        b, par = c // 2, c % 2
        cosP, sinP, oht1, oht2, ident, cmaskc, fmaskc, pos_perm = \
            _constants(S, NH, par)
        xb = np.ascontiguousarray(x[b][pos_perm])
        perms.append(pos_perm)
        isl = wg_p.shape[0] // ncore
        wds = wd.shape[0] // ncore
        in_maps.append({
            "xb": xb, "qkv_w": qkv, "dense_w": dw, "rel_table": rel,
            "wg_s": np.ascontiguousarray(wg_p[c * isl:(c + 1) * isl]),
            "wu_s": np.ascontiguousarray(wu_p[c * isl:(c + 1) * isl]),
            "wd_s": np.ascontiguousarray(wd[c * wds:(c + 1) * wds]),
            "cosP": cosP, "sinP": sinP, "oht1": oht1, "oht2": oht2,
            "identb": ident, "cmask": cmaskc, "fmask": fmaskc,
        })

    res = run_bass_kernel_spmd(nc, in_maps, core_ids=list(range(ncore)))

    out = np.empty((B, S, H), dtype=np.float32)
    for c in range(ncore):
        b, par = c // 2, c % 2
        own = perms[c][: S // 2]
        out[b][own] = res.results[c]["out"]
    return out
